# version 09
# baseline (speedup 1.0000x reference)
import os
os.environ.setdefault("NEURON_SCRATCHPAD_PAGE_SIZE", "1024")

from contextlib import ExitStack

import numpy as np
import ml_dtypes
import concourse.bass as bass
import concourse.mybir as mybir
from concourse.bass_utils import run_bass_kernel_spmd
from concourse.bass_core_rust import IndirectOffsetOnAxis

F32 = mybir.dt.float32
F16 = mybir.dt.float16
U32 = mybir.dt.uint32
AF = mybir.ActivationFunctionType
ALU = mybir.AluOpType
AX = mybir.AxisListType

B, S, D = 2, 8192, 2048
HEADS, GROUPS, HD = 16, 4, 128
NT = 16
PT = S // 128
KC = D // 128
E, I2, I = 64, 1024, 512
SI = 512
TOPK = 8
CAP = 2048
NCT = 192
BF16 = mybir.dt.bfloat16
LN_EPS = 1e-5
RMS_EPS = 1e-6
MASK_NEG = -30000.0
INV128 = 1.0 / 128.0


class Sem:
    __slots__ = ("h", "v", "name")

    def __init__(self, nc, name):
        self.h = nc.alloc_semaphore(name=name)
        self.v = 0
        self.name = name


def inc(sem, ins, n=1):
    ins.then_inc(sem.h, n)
    sem.v += n
    assert sem.v < 2 * 10 ** 9
    return (sem, sem.v)


class Q:
    def __init__(self, eng):
        self.e = eng
        self.seen = {}

    def wait(self, *evts):
        for evt in evts:
            if evt is None:
                continue
            sem, v = evt
            if self.seen.get(sem, 0) < v:
                self.e.wait_ge(sem.h, v)
                self.seen[sem] = v


class Rot:

    def __init__(self, mksem, tiles, name):
        self.t = tiles
        self.n = len(tiles)
        self.sems = [mksem(f"{name}{i}") for i in range(self.n)]
        self.ready = [None] * self.n
        self.free = [None] * self.n
        self.i = -1

    def nxt(self):
        self.i += 1
        return self.i % self.n


def build(phases=("A", "B", "C", "D"), dbg=(), n_pt=PT, n_nt=NT, n_e=E):
    nc = bass.Bass("TRN2", target_bir_lowering=False, debug=False)
    dbg = set(dbg)
    ap = lambda t: t.ap()
    all_sems = []

    def mksem(name):
        s = Sem(nc, name)
        all_sems.append(s)
        return s

    x_pre = ap(nc.dram_tensor("x_pre", [S, D], F32, kind="ExternalInput"))
    x_own = ap(nc.dram_tensor("x_own", [NT * 128, D], F32, kind="ExternalInput"))
    cos_pre = ap(nc.dram_tensor("cos_pre", [S, 64], F32, kind="ExternalInput"))
    sin_pre = ap(nc.dram_tensor("sin_pre", [S, 64], F32, kind="ExternalInput"))
    cos_own = ap(nc.dram_tensor("cos_own", [NT * 128, 64], F32, kind="ExternalInput"))
    sin_own = ap(nc.dram_tensor("sin_own", [NT * 128, 64], F32, kind="ExternalInput"))
    mask_in = ap(nc.dram_tensor("mask_in", [128, 512], F32, kind="ExternalInput"))
    wq_in = ap(nc.dram_tensor("wq_in", [D, D], F32, kind="ExternalInput"))
    wkv_in = ap(nc.dram_tensor("wkv_in", [D, 1024], F32, kind="ExternalInput"))
    wo_in = ap(nc.dram_tensor("wo_in", [D, D], F32, kind="ExternalInput"))
    bq_in = ap(nc.dram_tensor("bq_in", [1, D], F32, kind="ExternalInput"))
    bkv_in = ap(nc.dram_tensor("bkv_in", [1, 1024], F32, kind="ExternalInput"))
    bo_in = ap(nc.dram_tensor("bo_in", [1, D], F32, kind="ExternalInput"))
    qns_in = ap(nc.dram_tensor("qns_in", [1, 128], F32, kind="ExternalInput"))
    kns_in = ap(nc.dram_tensor("kns_in", [1, 128], F32, kind="ExternalInput"))
    gate_in = ap(nc.dram_tensor("gate_in", [D, E], F32, kind="ExternalInput"))
    guw_in = ap(nc.dram_tensor("guw_in", [E, D, I2], F32, kind="ExternalInput"))
    dpw_in = ap(nc.dram_tensor("dpw_in", [E, I, D], F32, kind="ExternalInput"))
    shgu_in = ap(nc.dram_tensor("shgu_in", [D, 1024], F32, kind="ExternalInput"))
    shd_in = ap(nc.dram_tensor("shd_in", [SI, D], F32, kind="ExternalInput"))
    g1_in = ap(nc.dram_tensor("g1_in", [128, KC], F32, kind="ExternalInput"))

    y_out = ap(nc.dram_tensor("y", [NT * 128, D], F32, kind="ExternalOutput"))

    K_sc = ap(nc.dram_tensor("K_sc", [128, GROUPS, S], F16, kind="Internal"))
    V_sc = ap(nc.dram_tensor("V_sc", [S, 512], F16, kind="Internal"))
    Q_sc = ap(nc.dram_tensor("Q_sc", [NT, 128, HEADS, 128], F16, kind="Internal"))
    X1_sc = ap(nc.dram_tensor("X1_sc", [NT * 128, D], F32, kind="Internal"))
    XN2T_sc = ap(nc.dram_tensor("XN2T_sc", [NT, 128, KC, 128], F16, kind="Internal"))
    XN2_sc = ap(nc.dram_tensor("XN2_sc", [NT * 128, D], F16, kind="Internal"))
    Y_sc = ap(nc.dram_tensor("Y_sc", [E * CAP, D], F16, kind="Internal"))
    TR_d = ap(nc.dram_tensor("TR_d", [E * CAP, 2], F32, kind="Internal"))
    WALL_sc = ap(nc.dram_tensor("WALL_sc", [E, 128, 24576], F16, kind="Internal"))

    dbg_outs = {}

    def dbg_out(name, shape, dt=F32):
        dbg_outs[name] = ap(nc.dram_tensor(name, shape, dt, kind="ExternalOutput"))
        return dbg_outs[name]

    ident16_d = nc.inline_tensor(np.eye(128, dtype=np.float16), "ident16")
    ident32_d = nc.inline_tensor(np.eye(128, dtype=np.float32), "ident32")
    ones_row_d = nc.inline_tensor(np.ones((1, 128), dtype=np.float32), "ones_row")
    iota64_d = nc.inline_tensor(
        np.broadcast_to(np.arange(64, dtype=np.float32), (128, 64)).copy(), "iota64")
    pidx_d = nc.inline_tensor(np.arange(128, dtype=np.float32).reshape(128, 1), "pidx")
    identb_d = nc.inline_tensor(np.eye(128, dtype=np.float16), "identb")
    iotaB192_d = nc.inline_tensor(
        np.broadcast_to(np.arange(NCT, dtype=np.float32), (128, NCT)).copy(), "iotaB192")
    tri_d = nc.inline_tensor(np.triu(np.ones((128, 128)), 1).astype(np.float16), "tri")
    onescol_d = nc.inline_tensor(np.ones((128, 1), dtype=np.float16), "onescol")

    s_pe = mksem("s_pe")
    s_ve = mksem("s_ve")
    s_sc = mksem("s_sc")
    d_pre = mksem("d_pre")
    d_brow = mksem("d_brow")
    d_dbg = mksem("d_dbg")
    s_out = mksem("s_out")

    sync, vec, sca, ten, gp = nc.sync, nc.vector, nc.scalar, nc.tensor, nc.gpsimd
    qsy, qve, qsc, qpe, qgp = Q(sync), Q(vec), Q(sca), Q(ten), Q(gp)
    rb_tab = gp.alloc_register("rb_tab")
    gp.reg_mov(rb_tab, E * CAP - 1)
    rb_tok = gp.alloc_register("rb_tok")
    gp.reg_mov(rb_tok, NT * 128 - 1)
    rb_we = gp.alloc_register("rb_we")
    gp.reg_mov(rb_we, E * 128 - 1)

    vstep = lambda ins: inc(s_ve, ins)
    scstep = lambda ins: inc(s_sc, ins)
    pestep = lambda ins: inc(s_pe, ins)
    pe_now = lambda: (s_pe, s_pe.v)

    def phase_barrier(*extras):
        evs = [(s_pe, s_pe.v), (s_ve, s_ve.v), (s_sc, s_sc.v)] + [e for e in extras if e]
        for q in (qsy, qve, qsc, qpe, qgp):
            q.wait(*evs)

    def mm512(ps_t, lhsT_ap, rhs_ap, n, start, stop, step=512):
        for w0 in range(0, n, step):
            ins = ten.matmul(ps_t[:, w0:w0 + step], lhsT_ap, rhs_ap[:, w0:w0 + step],
                             start=start, stop=stop)
        return ins

    ident16 = nc.alloc_sbuf_tensor("ident16_sb", [128, 128], F16)
    ident32 = nc.alloc_sbuf_tensor("ident32_sb", [128, 128], F32)
    ones_row = nc.alloc_sbuf_tensor("ones_row_sb", [1, 128], F32)
    iota64_sb = nc.alloc_sbuf_tensor("iota64_sb", [128, 64], F32)
    pidx_sb = nc.alloc_sbuf_tensor("pidx_sb", [128, 1], F32)
    mask_sb = nc.alloc_sbuf_tensor("mask_sb", [128, 512], F32)
    qns_b = nc.alloc_sbuf_tensor("qns_b", [128, 128], F32)
    kns_b = nc.alloc_sbuf_tensor("kns_b", [128, 128], F32)
    fidx_sb = nc.alloc_sbuf_tensor("fidx_sb", [128, NT, TOPK], U32)
    IDXT_u = nc.alloc_sbuf_tensor("IDXT_u", [128, NCT], U32)
    IDXD_u = nc.alloc_sbuf_tensor("IDXD_u", [128, NCT], U32)
    identb = nc.alloc_sbuf_tensor("identb_sb", [128, 128], F16)
    iotaB192 = nc.alloc_sbuf_tensor("iotaB192_sb", [128, NCT], F32)
    tri_sb = nc.alloc_sbuf_tensor("tri_sb", [128, 128], F16)
    onescol = nc.alloc_sbuf_tensor("onescol_sb", [128, 1], F16)
    nsrow = nc.alloc_sbuf_tensor("nsrow", [1, 256], F32)

    for dst, src in [(ident16, ident16_d), (ident32, ident32_d), (ones_row, ones_row_d),
                     (iota64_sb, iota64_d), (pidx_sb, pidx_d), (identb, identb_d),
                     (tri_sb, tri_d), (onescol, onescol_d), (iotaB192, iotaB192_d)]:
        inc(d_pre, sync.dma_start(out=dst[:, :], in_=src.ap()), 16)
    inc(d_pre, sync.dma_start(out=mask_sb[:, :], in_=mask_in), 16)
    inc(d_pre, sync.dma_start(out=nsrow[:, 0:128], in_=qns_in), 16)
    e_pre = inc(d_pre, sync.dma_start(out=nsrow[:, 128:256], in_=kns_in), 16)

    with nc.psum_tensor("ps_b0", [128, 256], F32) as ps_b0:
        qpe.wait(e_pre)
        e_p = pestep(ten.matmul(ps_b0[:, :], ones_row[:, :], nsrow[:, :], start=True, stop=True))
        qve.wait(e_p)
        vstep(vec.tensor_copy(out=qns_b[:, :], in_=ps_b0[:, 0:128]))
        vstep(vec.tensor_copy(out=kns_b[:, :], in_=ps_b0[:, 128:256]))

    d_toki = mksem("d_toki")
    with nc.sbuf_tensor("tokinit", [128, 2048], F32) as toki:
        e_mi = vstep(vec.memset(toki[:, :], 1.0e9))
        qgp.wait(e_mi)
        e_toki = inc(d_toki, gp.dma_start(
            out=TR_d.rearrange("(p c) n -> p (c n)", p=128), in_=toki[:, :]), 16)
    qve.wait(e_toki)
    qsy.wait(e_toki)

    d_wcs = mksem("d_wcs")
    precast_jobs = []
    if "C" in phases:
        def mk_pc(src_ap, cc, out_ap):
            def f():
                return inc(d_wcs, gp.dma_start(
                    out=out_ap.rearrange("p (c n) -> p c n", c=cc),
                    in_=src_ap.rearrange("(c p) n -> p c n", p=128)), 16)
            return f

        for e in range(E):
            for cb in range(4):
                precast_jobs.append(mk_pc(guw_in[e, cb * 512:(cb + 1) * 512, :], 4,
                                          WALL_sc[e, :, cb * 4096:(cb + 1) * 4096]))
            for cb in range(2):
                precast_jobs.append(mk_pc(dpw_in[e, cb * 256:(cb + 1) * 256, :], 2,
                                          WALL_sc[e, :, 16384 + cb * 4096:16384 + (cb + 1) * 4096]))

    def run_precast_jobs(n):
        for _ in range(n):
            if precast_jobs:
                precast_jobs.pop(0)()

    brow_consumed = [e_pre]
    bb_prev_evac = [None]
    _bb = [0]

    def bias_bcast(row_ap, width, dst_sb, dst_off):
        _bb[0] += 1
        with nc.sbuf_tensor(f"brow{_bb[0]}", [1, width], F32) as brow, \
             nc.psum_tensor(f"ps_bb{_bb[0]}", [128, width], F32) as ps_bb:
            qsy.wait(brow_consumed[0])
            e_b = inc(d_brow, sync.dma_start(out=brow[:, :], in_=row_ap), 16)
            qpe.wait(e_b, bb_prev_evac[0])
            for w0 in range(0, width, 512):
                e_p = pestep(ten.matmul(ps_bb[:, w0:w0 + 512], ones_row[:, :],
                                        brow[:, w0:w0 + 512], start=True, stop=True))
            brow_consumed[0] = e_p
            qve.wait(e_p)
            e_ev = vstep(vec.tensor_copy(out=dst_sb[:, dst_off:dst_off + width],
                                         in_=ps_bb[:, :]))
            bb_prev_evac[0] = e_ev
            return e_ev

    def stream_cast(src_ap, rows, cols, dst_sb, kc0, lrot):
        n = rows // 128
        e_c = None
        for r in range(0, n, 2):
            take = min(2, n - r)
            ls = lrot.nxt()
            qsy.wait(lrot.free[ls])
            e_l = inc(lrot.sems[ls], sync.dma_start(
                out=lrot.t[ls][:, 0:take * cols].rearrange("p (c n) -> p c n", c=take),
                in_=src_ap[r * 128:(r + take) * 128, :].rearrange("(c p) n -> p c n", p=128)), 16)
            qve.wait(e_l)
            e_c = vstep(vec.tensor_copy(
                out=dst_sb[:, kc0 + r:kc0 + r + take, 0:cols],
                in_=lrot.t[ls][:, 0:take * cols].rearrange("p (c n) -> p c n", c=take)))
            lrot.free[ls] = e_c
        return e_c

    def layernorm_tile(xb, xc, xn16, m, ss, sqscr, evt_x, extra32=None):
        qve.wait(evt_x)
        e1 = vstep(vec.reduce_sum(out=m[:, :], in_=xb, axis=AX.X))
        qve.wait(e1)
        e2 = vstep(vec.tensor_scalar(out=m[:, :], in0=m[:, :], scalar1=1.0 / 2048.0,
                                     scalar2=None, op0=ALU.mult))
        qve.wait(e2)
        e3 = vstep(vec.tensor_scalar(out=xc, in0=xb, scalar1=m[:, 0:1],
                                     scalar2=None, op0=ALU.subtract))
        qsc.wait(e3)
        e4 = scstep(sca.activation(out=sqscr, in_=xc, func=AF.Square, accum_out=ss[:, :]))
        qve.wait(e4)
        e5 = vstep(vec.tensor_scalar(out=ss[:, :], in0=ss[:, :], scalar1=1.0 / 2048.0,
                                     scalar2=LN_EPS, op0=ALU.mult, op1=ALU.add))
        qsc.wait(e5)
        e6 = scstep(sca.activation(out=ss[:, :], in_=ss[:, :], func=AF.Sqrt))
        qve.wait(e6)
        e7 = vstep(vec.reciprocal(out=ss[:, :], in_=ss[:, :]))
        qve.wait(e7)
        e8 = vstep(vec.tensor_scalar(out=xn16, in0=xc, scalar1=ss[:, 0:1],
                                     scalar2=None, op0=ALU.mult))
        e9 = None
        if extra32 is not None:
            e9 = vstep(vec.tensor_scalar(out=extra32, in0=xc, scalar1=ss[:, 0:1],
                                         scalar2=None, op0=ALU.mult))
        return e8, e3, e9

    def transpose16(src_col, n_chunks, ps_tr, dst16, src_evt,
                    extra_pe_wait=None, extra_ve_wait=None, ident=None):
        last = None
        for r in range(0, n_chunks, 4):
            take = min(4, n_chunks - r)
            qpe.wait(src_evt, extra_pe_wait)
            if last is not None:
                qpe.wait(last)
            for u in range(take):
                e_t = pestep(ten.transpose(ps_tr[:, u * 128:(u + 1) * 128],
                                           src_col(r + u),
                                           (ident if ident is not None else ident16)[:, :]))
            qve.wait(e_t, extra_ve_wait)
            last = vstep(vec.tensor_copy(out=dst16[:, r * 128:(r + take) * 128],
                                         in_=ps_tr[:, 0:take * 128]))
        return last

    e_A_done = ()
    if "A" in phases:
        esA = ExitStack()
        sb = lambda n, s, d=F32: esA.enter_context(nc.sbuf_tensor(n, s, d))
        psm = lambda n, s, d=F32: esA.enter_context(nc.psum_tensor(n, s, d))
        wq16 = sb("wq16", [128, KC, D], F16)
        wkv16 = sb("wkv16", [128, KC, 1024], F16)
        bqkv_b = sb("bqkv_b", [128, 3072], F32)
        with nc.sbuf_tensor("landA0", [128, 4096], F32) as lA0, \
             nc.sbuf_tensor("landA1", [128, 4096], F32) as lA1:
            lrotA = Rot(mksem, [lA0, lA1], "d_lA")
            stream_cast(wq_in, D, 2048, wq16, 0, lrotA)
            e_lc = stream_cast(wkv_in, D, 1024, wkv16, 0, lrotA)
            bias_bcast(bq_in, 2048, bqkv_b, 0)
            e_bias = bias_bcast(bkv_in, 1024, bqkv_b, 2048)
        qsy.wait(e_lc, brow_consumed[0])

        xrot = Rot(mksem, [sb(f"xa{i}", [128, D]) for i in range(2)], "d_xa")
        csrot = Rot(mksem, [sb(f"cs{i}", [128, 128]) for i in range(2)], "d_cs")
        d_ks = mksem("d_ks")
        d_vs = mksem("d_vs")
        d_qs = mksem("d_qs")
        xc_a = sb("xc_a", [128, D])
        m_a = sb("m_a", [128, 1])
        ss_a = sb("ss_a", [128, 1])
        xn16_a = sb("xn16_a", [128, D], F16)
        xnT_a = sb("xnT_a", [128, D], F16)
        kv16 = sb("kv16", [128, 1024], F16)
        q16_t = sb("q16_t", [128, D], F16)
        msn = sb("msn", [128, 16])
        ropen = sb("ropen", [128, D])
        rope2 = sb("rope2", [128, D])
        sqn = rope2.ap()
        o16 = sb("o16", [128, D], F16)
        kT_sb = sb("kT_sb", [128, 512], F16)
        ps_tr_a = psm("ps_tr_a", [128, 512], F16)
        ps_kv = psm("ps_kv", [128, 1024])
        ps_q = psm("ps_q", [128, 2048])

        def rope_norm(src16, ngrp, ns_b, cs_t, out_f16, src_evt, cs_evt, war_evt):
            w = ngrp * 128
            qve.wait(src_evt, war_evt)
            e1 = vstep(vec.tensor_tensor(out=sqn[:, 0:w], in0=src16[:, 0:w],
                                         in1=src16[:, 0:w], op=ALU.mult))
            qve.wait(e1)
            e2 = vstep(vec.reduce_sum(out=msn[:, 0:ngrp],
                                      in_=sqn[:, 0:w].rearrange("p (g x) -> p g x", g=ngrp),
                                      axis=AX.X))
            qve.wait(e2)
            e3 = vstep(vec.tensor_scalar(out=msn[:, 0:ngrp], in0=msn[:, 0:ngrp],
                                         scalar1=1.0 / 128.0, scalar2=RMS_EPS,
                                         op0=ALU.mult, op1=ALU.add))
            qsc.wait(e3)
            e4 = scstep(sca.activation(out=msn[:, 0:ngrp], in_=msn[:, 0:ngrp], func=AF.Sqrt))
            qve.wait(e4)
            e5 = vstep(vec.reciprocal(out=msn[:, 0:ngrp], in_=msn[:, 0:ngrp]))
            qve.wait(e5)
            r2 = ropen[:, 0:w].rearrange("p (g x) -> p g x", g=ngrp)
            e6 = vstep(vec.tensor_tensor(
                out=r2, in0=src16[:, 0:w].rearrange("p (g x) -> p g x", g=ngrp),
                in1=msn[:, 0:ngrp].unsqueeze(2).broadcast_to([128, ngrp, 128]), op=ALU.mult))
            qve.wait(e6)
            e7 = vstep(vec.tensor_tensor(
                out=r2, in0=r2,
                in1=ns_b[:, :].unsqueeze(1).broadcast_to([128, ngrp, 128]), op=ALU.mult))
            r3 = ropen[:, 0:w].rearrange("p (g h x) -> p g h x", g=ngrp, h=2)
            o3 = rope2[:, 0:w].rearrange("p (g h x) -> p g h x", g=ngrp, h=2)
            f3 = out_f16[:, 0:w].rearrange("p (g h x) -> p g h x", g=ngrp, h=2)
            cosb = cs_t[:, 0:64].unsqueeze(1).broadcast_to([128, ngrp, 64])
            sinb = cs_t[:, 64:128].unsqueeze(1).broadcast_to([128, ngrp, 64])
            qve.wait(e7, cs_evt)
            ea = vstep(vec.tensor_tensor(out=o3[:, :, 0, :], in0=r3[:, :, 0, :], in1=cosb, op=ALU.mult))
            eb = vstep(vec.tensor_tensor(out=o3[:, :, 1, :], in0=r3[:, :, 1, :], in1=sinb, op=ALU.mult))
            qve.wait(ea, eb)
            ec = vstep(vec.tensor_tensor(out=f3[:, :, 0, :], in0=o3[:, :, 0, :],
                                         in1=o3[:, :, 1, :], op=ALU.subtract))
            qve.wait(ec)
            ed = vstep(vec.tensor_tensor(out=o3[:, :, 0, :], in0=r3[:, :, 0, :], in1=sinb, op=ALU.mult))
            ee = vstep(vec.tensor_tensor(out=o3[:, :, 1, :], in0=r3[:, :, 1, :], in1=cosb, op=ALU.mult))
            qve.wait(ed, ee)
            ef = vstep(vec.tensor_tensor(out=f3[:, :, 1, :], in0=o3[:, :, 0, :],
                                         in1=o3[:, :, 1, :], op=ALU.add))
            qve.wait(ef)
            return ef

        e_ks = e_vs = e_qs = None
        prev_kv_evac = None
        prev_kT = None
        prev_xnT_read = None
        for p in range(n_pt):
            run_precast_jobs(10)
            slot = xrot.nxt()
            qsy.wait(xrot.free[slot])
            e_x = inc(xrot.sems[slot], sync.dma_start(
                out=xrot.t[slot][:, :], in_=x_pre[p * 128:(p + 1) * 128, :]), 16)
            cslot = csrot.nxt()
            qsy.wait(csrot.free[cslot])
            inc(csrot.sems[cslot], sync.dma_start(
                out=csrot.t[cslot][:, 0:64], in_=cos_pre[p * 128:(p + 1) * 128, :]), 16)
            e_cs = inc(csrot.sems[cslot], sync.dma_start(
                out=csrot.t[cslot][:, 64:128], in_=sin_pre[p * 128:(p + 1) * 128, :]), 16)
            e_xn, e_relx, _ = layernorm_tile(xrot.t[slot][:, :], xc_a[:, :], xn16_a[:, :],
                                             m_a, ss_a, ropen[:, :], e_x)
            xrot.free[slot] = e_relx
            e_xnT = transpose16(lambda c: xn16_a[:, c * 128:(c + 1) * 128], KC,
                                ps_tr_a, xnT_a, e_xn, extra_pe_wait=prev_xnT_read)
            qpe.wait(e_xnT, e_bias, prev_kv_evac)
            for kc in range(KC):
                e_mm = mm512(ps_kv, xnT_a[:, kc * 128:(kc + 1) * 128],
                             wkv16[:, kc, :], 1024, kc == 0, kc == KC - 1)
            e_mm = pestep(e_mm)
            prev_xnT_read = e_mm
            qve.wait(e_mm, e_vs)
            e_kv = vstep(vec.tensor_tensor(out=kv16[:, :], in0=ps_kv[:, :],
                                           in1=bqkv_b[:, 2048:3072], op=ALU.add))
            prev_kv_evac = e_kv
            e_k = rope_norm(kv16, GROUPS, kns_b, csrot.t[cslot], o16, e_kv, e_cs, prev_kT)
            csrot.free[cslot] = e_k
            e_kT = transpose16(lambda c: o16[:, c * 128:(c + 1) * 128], GROUPS,
                               ps_tr_a, kT_sb, e_k, extra_ve_wait=e_ks)
            prev_kT = pe_now()
            qgp.wait(e_kT)
            e_ks = inc(d_ks, gp.dma_start(
                out=K_sc[:, :, p * 128:(p + 1) * 128],
                in_=kT_sb[:, :].rearrange("p (g x) -> p g x", g=GROUPS)), 16)
            qgp.wait(e_kv)
            e_vs = inc(d_vs, gp.dma_start(out=V_sc[p * 128:(p + 1) * 128, :],
                                          in_=kv16[:, 512:1024]), 16)

        prev_q_evac = None
        for j in range(n_nt):
            run_precast_jobs(10)
            slot = xrot.nxt()
            qsy.wait(xrot.free[slot])
            e_x = inc(xrot.sems[slot], sync.dma_start(
                out=xrot.t[slot][:, :], in_=x_own[j * 128:(j + 1) * 128, :]), 16)
            cslot = csrot.nxt()
            qsy.wait(csrot.free[cslot])
            inc(csrot.sems[cslot], sync.dma_start(
                out=csrot.t[cslot][:, 0:64], in_=cos_own[j * 128:(j + 1) * 128, :]), 16)
            e_cs = inc(csrot.sems[cslot], sync.dma_start(
                out=csrot.t[cslot][:, 64:128], in_=sin_own[j * 128:(j + 1) * 128, :]), 16)
            qve.wait(e_qs)
            e_xn, e_relx, _ = layernorm_tile(xrot.t[slot][:, :], xc_a[:, :], xn16_a[:, :],
                                             m_a, ss_a, ropen[:, :], e_x)
            xrot.free[slot] = e_relx
            e_xnT = transpose16(lambda c: xn16_a[:, c * 128:(c + 1) * 128], KC,
                                ps_tr_a, xnT_a, e_xn, extra_pe_wait=prev_xnT_read)
            qpe.wait(e_xnT, prev_q_evac)
            for kc in range(KC):
                e_mm = mm512(ps_q, xnT_a[:, kc * 128:(kc + 1) * 128],
                             wq16[:, kc, :], 2048, kc == 0, kc == KC - 1)
            e_mm = pestep(e_mm)
            prev_xnT_read = e_mm
            qve.wait(e_mm)
            e_q = vstep(vec.tensor_tensor(out=q16_t[:, :], in0=ps_q[:, :],
                                          in1=bqkv_b[:, 0:2048], op=ALU.add))
            prev_q_evac = e_q
            e_qr = rope_norm(q16_t, HEADS, qns_b, csrot.t[cslot], o16, e_q, e_cs, prev_kT)
            csrot.free[cslot] = e_qr
            e_qT = transpose16(lambda c: o16[:, c * 128:(c + 1) * 128], HEADS,
                               ps_tr_a, xn16_a, e_qr,
                               extra_pe_wait=prev_xnT_read, extra_ve_wait=e_qs)
            prev_kT = pe_now()
            qgp.wait(e_qT)
            e_qs = inc(d_qs, gp.dma_start(
                out=Q_sc[j, :, :, :],
                in_=xn16_a[:, :].rearrange("p (h x) -> p h x", h=HEADS)), 16)

        run_precast_jobs(len(precast_jobs))
        e_A_done = (e_ks, e_vs, e_qs)
        if "KV" in dbg:
            d1 = dbg_out("dbg_K", [128, GROUPS, S], F16)
            d2 = dbg_out("dbg_V", [S, 512], F16)
            d3 = dbg_out("dbg_Q", [NT, 128, HEADS, 128], F16)
            qsy.wait(*e_A_done)
            inc(d_dbg, sync.dma_start(out=d1, in_=K_sc), 16)
            inc(d_dbg, sync.dma_start(out=d2, in_=V_sc), 16)
            ec = inc(d_dbg, sync.dma_start(out=d3, in_=Q_sc), 16)
            qsy.wait(ec)
        esA.close()

    e_B_done = ()
    if "B" in phases:
        esB = ExitStack()
        sb = lambda n, s, d=F32: esB.enter_context(nc.sbuf_tensor(n, s, d))
        psm = lambda n, s, d=F32: esB.enter_context(nc.psum_tensor(n, s, d))
        wo16 = sb("wo16", [128, KC, D], F16)
        bo_b = sb("bo_b", [128, D], F32)
        gate32 = sb("gate32", [128, KC, E], F32)
        d_g32 = mksem("d_g32")
        phase_barrier(*e_A_done)
        with nc.sbuf_tensor("landB0", [128, 4096], F32) as lB0, \
             nc.sbuf_tensor("landB1", [128, 4096], F32) as lB1:
            lrotB = Rot(mksem, [lB0, lB1], "d_lB")
            e_lastB = stream_cast(wo_in, D, 2048, wo16, 0, lrotB)
            e_bo = bias_bcast(bo_in, 2048, bo_b, 0)
        qsy.wait(e_lastB, brow_consumed[0])
        e_g32 = inc(d_g32, sync.dma_start(out=gate32[:, :, :],
                                          in_=gate_in.rearrange("(c p) n -> p c n", p=128)), 16)

        d_qt = mksem("d_qt")
        d_xb = mksem("d_xb")
        d_x1s = mksem("d_x1s")
        d_xns = mksem("d_xns")
        qT_b = sb("qT_b", [128, HEADS, 128], F16)
        x_t = sb("x_t", [128, D])
        logits4 = sb("logits4", [128, 4, S], F16)
        krot = Rot(mksem, [sb(f"kbuf{i}", [128, 512], F16) for i in range(2)], "d_kb")
        vrot = Rot(mksem, [sb(f"vbuf{i}", [128, 4, 128], F16) for i in range(2)], "d_vb")
        ptr_sb = sb("ptr_sb", [128, 1024], F16)
        attn_sb = sb("attn_sb", [128, D], F16)
        attnT_sb = sb("attnT_sb", [128, D], F16)
        negmax = sb("negmax", [128, 2])
        sums = sb("sums", [128, 2])
        recip1 = sb("recip1", [128, 2])
        x1_t = sb("x1_t", [128, D])
        m_b = sb("m_b", [128, 1])
        ss_b = sb("ss_b", [128, 1])
        xn2_16 = sb("xn2_16", [128, D], F16)
        xn2T16 = attnT_sb
        xn2_32 = sb("xn2_32", [128, D])
        xn2T32 = sb("xn2T32", [128, D])
        rlog = sb("rlog", [128, E])
        topv = sb("topv", [128, TOPK])
        rtmp = sb("rtmp", [128, E])
        rsml = sb("rsml", [128, TOPK])
        rw_sb = sb("rw_sb", [128, TOPK])
        U_sb = sb("U_sb", [128, E])
        Ub_sb = sb("Ub_sb", [128, E], F16)

        masks8 = logits4[:, 0, 0:TOPK * E].rearrange("p (a b) -> p a b", a=TOPK)
        slot_sb = sb("slot_sb", [128, E])
        carb_sb = sb("carb_sb", [128, E])
        flat_sb = sb("flat_sb", [128, E])
        carry_row = sb("carry_row", [1, E])
        tokc = sb("tokc", [128, 1])
        trsc = sb("trsc", [128, 2])
        d_sct = mksem("d_sct")
        d_xn2tok = mksem("d_xn2tok")
        e_cr0 = vstep(vec.memset(carry_row[:, :], 0.0))

        ps_log = psm("ps_log", [128, 1024])
        ps_ptr = psm("ps_ptr", [128, 1024], F16)
        ps_att = psm("ps_att", [128, 128])
        ps_out = psm("ps_out", [128, 2048])
        ps_ptr_b = ps_ptr.ap()

        qsy.wait(*e_A_done)
        qpe.wait(e_bo, e_lastB, e_g32)

        e_x1s = None
        e_xns = None
        e_logits_free = [None] * 4
        e_evac_prev = [None, None]
        e_att_read = None
        e_att_prev = None
        e_x1_prev = None
        e_ptr_read = None
        e_qk_last = None
        e_e_hb = [None, None]
        e_r_hb = [None, None]
        e_xn232 = None
        e_xtok = None
        e_f3 = None
        e_sct = None
        for j in range(n_nt):
            nchunks = j + 1
            qsy.wait(e_qk_last)
            e_qTl = inc(d_qt, sync.dma_start(out=qT_b[:, :, :], in_=Q_sc[j, :, :, :]), 16)
            qsy.wait(e_xn232)
            e_xl = inc(d_xb, sync.dma_start(out=x_t[:, :],
                                            in_=x_own[j * 128:(j + 1) * 128, :]), 16)
            qve.wait(e_xl, e_bo, e_x1_prev)
            e_xb = vstep(vec.tensor_tensor(out=x_t[:, :], in0=x_t[:, :], in1=bo_b[:, :],
                                           op=ALU.add))

            e_op = None
            for g in range(GROUPS):
                e_log_done = [None] * 4
                for c in range(nchunks):
                    slot = krot.nxt()
                    qsy.wait(krot.free[slot])
                    e_k = inc(krot.sems[slot], sync.dma_start(
                        out=krot.t[slot][:, :], in_=K_sc[:, g, c * 512:(c + 1) * 512]), 16)
                    last_chunk = c == nchunks - 1
                    for h4 in range(4):
                        h = 4 * g + h4
                        half = h4 % 2
                        qpe.wait(e_k, e_qTl, e_evac_prev[half])
                        e_mm = pestep(ten.matmul(ps_log[:, half * 512:(half + 1) * 512],
                                                 qT_b[:, h, :], krot.t[slot][:, :],
                                                 start=True, stop=True))
                        if h4 == 3:
                            krot.free[slot] = e_mm
                        e_qk_last = e_mm
                        if last_chunk:
                            qve.wait(e_mm, e_logits_free[h4])
                            e_ev = vstep(vec.scalar_tensor_tensor(
                                out=logits4[:, h4, c * 512:(c + 1) * 512],
                                in0=ps_log[:, half * 512:(half + 1) * 512],
                                scalar=INV128, in1=mask_sb[:, :],
                                op0=ALU.mult, op1=ALU.add))
                        else:
                            qsc.wait(e_mm, e_logits_free[h4])
                            e_ev = scstep(sca.activation(
                                out=logits4[:, h4, c * 512:(c + 1) * 512],
                                in_=ps_log[:, half * 512:(half + 1) * 512],
                                func=AF.Copy, scale=INV128))
                        e_evac_prev[half] = e_ev
                        e_log_done[h4] = e_ev

                for h4 in range(4):
                    h = 4 * g + h4
                    hb = h % 2
                    qve.wait(e_log_done[h4], e_e_hb[hb])
                    e_m = vstep(vec.reduce_max(out=negmax[:, hb:hb + 1],
                                               in_=logits4[:, h4, 0:nchunks * 512],
                                               axis=AX.X, negate=True))
                    qsc.wait(e_m, e_r_hb[hb])
                    e_e = scstep(sca.activation(out=logits4[:, h4, 0:nchunks * 512],
                                                in_=logits4[:, h4, 0:nchunks * 512],
                                                func=AF.Exp, bias=negmax[:, hb:hb + 1],
                                                accum_out=sums[:, hb:hb + 1]))
                    e_e_hb[hb] = e_e
                    qve.wait(e_e)
                    e_r = vstep(vec.reciprocal(out=recip1[:, hb:hb + 1], in_=sums[:, hb:hb + 1]))
                    e_r_hb[hb] = e_r
                    nsc = nchunks * 4
                    e_pt_ev = [None, None]
                    e_ptr_rd = [e_ptr_read, e_ptr_read]
                    e_pv = None
                    for sc in range(0, nsc, 4):
                        take = min(4, nsc - sc)
                        ph = (sc // 4) % 2
                        vslot = vrot.nxt()
                        qsy.wait(vrot.free[vslot])
                        e_v = inc(vrot.sems[vslot], sync.dma_start(
                            out=vrot.t[vslot][:, 0:take, :],
                            in_=V_sc[sc * 128:(sc + take) * 128, g * 128:(g + 1) * 128]
                            .rearrange("(u p) m -> p u m", p=128)), 16)
                        qpe.wait(e_e)
                        if e_pt_ev[ph] is not None:
                            qpe.wait(e_pt_ev[ph])
                        for u in range(take):
                            e_t = pestep(ten.transpose(
                                ps_ptr[:, ph * 512 + u * 128:ph * 512 + (u + 1) * 128],
                                logits4[:, h4, (sc + u) * 128:(sc + u + 1) * 128],
                                ident16[:, :]))
                        qve.wait(e_t, e_ptr_rd[ph])
                        e_pt_ev[ph] = vstep(vec.tensor_copy(
                            out=ptr_sb[:, ph * 512:ph * 512 + take * 128],
                            in_=ps_ptr[:, ph * 512:ph * 512 + take * 128]))
                        qpe.wait(e_pt_ev[ph], e_v)
                        if sc == 0:
                            qpe.wait(e_att_prev)
                        for u in range(take):
                            e_pv = ten.matmul(ps_att[:, :],
                                              ptr_sb[:, ph * 512 + u * 128:ph * 512 + (u + 1) * 128],
                                              vrot.t[vslot][:, u, :],
                                              start=(sc + u == 0), stop=(sc + u == nsc - 1))
                        e_pv = pestep(e_pv)
                        vrot.free[vslot] = e_pv
                        e_ptr_rd[ph] = e_pv
                    e_ptr_read = e_pv
                    e_logits_free[h4] = pe_now()
                    qve.wait(e_pv, e_r, e_att_read)
                    e_att = vstep(vec.tensor_scalar(out=attn_sb[:, h * 128:(h + 1) * 128],
                                                    in0=ps_att[:, :], scalar1=recip1[:, hb:hb + 1],
                                                    scalar2=None, op0=ALU.mult))
                    e_att_prev = e_att
                    qpe.wait(e_att)
                    if h == 0:
                        qpe.wait(e_x1_prev)
                    e_trh = pestep(ten.transpose(ps_ptr[:, 0:128],
                                                 attn_sb[:, h * 128:(h + 1) * 128],
                                                 ident16[:, :]))
                    e_att_read = e_trh
                    qve.wait(e_trh, e_xns)
                    e_evh = vstep(vec.tensor_copy(out=attnT_sb[:, h * 128:(h + 1) * 128],
                                                  in_=ps_ptr[:, 0:128]))
                    qpe.wait(e_evh)
                    e_op = mm512(ps_out, attnT_sb[:, h * 128:(h + 1) * 128],
                                 wo16[:, h, :], 2048, h == 0, h == HEADS - 1)
                    e_op = pestep(e_op)
                    e_ptr_read = e_op

            qve.wait(e_op, e_xb)
            e_x1 = vstep(vec.scalar_tensor_tensor(out=x1_t[:, :], in0=ps_out[:, :],
                                                  scalar=1.0, in1=x_t[:, :],
                                                  op0=ALU.mult, op1=ALU.add))
            qgp.wait(e_x1)
            e_x1s = inc(d_x1s, gp.dma_start(out=X1_sc[j * 128:(j + 1) * 128, :],
                                            in_=x1_t[:, :]), 16)
            qve.wait(e_x1s)
            qve.wait(e_xtok)
            e_xn2, _, e_xn232 = layernorm_tile(x1_t[:, :], x_t[:, :], xn2_16[:, :],
                                               m_b, ss_b, xn2T32[:, :], e_x1,
                                               extra32=xn2_32[:, :])
            qgp.wait(e_xn2, e_xtok if j > 0 else None)
            e_xtok = inc(d_xn2tok, gp.dma_start(
                out=XN2_sc[j * 128:(j + 1) * 128, :], in_=xn2_16[:, :]), 16)
            e_xn2T = transpose16(lambda c: xn2_16[:, c * 128:(c + 1) * 128], KC,
                                 ps_ptr_b, xn2T16, e_xn2,
                                 extra_pe_wait=e_ptr_read, extra_ve_wait=e_xns,
                                 ident=identb)
            qgp.wait(e_xn2T)
            e_xns = inc(d_xns, gp.dma_start(
                out=XN2T_sc[j, :, :, :],
                in_=xn2T16[:, :].rearrange("p (c x) -> p c x", c=KC)), 16)
            last32 = None
            for r in range(0, KC, 4):
                qpe.wait(e_xn232, e_x1)
                if last32 is not None:
                    qpe.wait(last32)
                for u in range(4):
                    e_t32 = pestep(ten.transpose(ps_out[:, u * 128:(u + 1) * 128],
                                                 xn2_32[:, (r + u) * 128:(r + u + 1) * 128],
                                                 ident32[:, :]))
                qve.wait(e_t32)
                last32 = vstep(vec.tensor_copy(out=xn2T32[:, r * 128:(r + 4) * 128],
                                               in_=ps_out[:, 0:512]))
            e_x1_prev = last32
            qpe.wait(last32, e_g32, e_att_prev)
            for kc in range(KC):
                e_rm = ten.matmul(ps_att[:, 0:E], xn2T32[:, kc * 128:(kc + 1) * 128],
                                  gate32[:, kc, :], start=(kc == 0), stop=(kc == KC - 1))
            e_rm = pestep(e_rm)
            qve.wait(e_rm)
            e_rl = vstep(vec.tensor_copy(out=rlog[:, :], in_=ps_att[:, 0:E]))
            e_att_prev = e_rl
            qve.wait(e_rl)
            e_tv = vstep(vec.max(out=topv[:, :], in_=rlog[:, :]))
            qve.wait(e_tv)
            e_ng = vstep(vec.tensor_scalar(out=m_b[:, :], in0=topv[:, 0:1], scalar1=-1.0,
                                           scalar2=None, op0=ALU.mult))
            qsc.wait(e_ng)
            e_px = scstep(sca.activation(out=rsml[:, :], in_=topv[:, :], func=AF.Exp,
                                         bias=m_b[:, 0:1], accum_out=ss_b[:, :]))
            qve.wait(e_px)
            e_rr = vstep(vec.reciprocal(out=ss_b[:, :], in_=ss_b[:, :]))
            qve.wait(e_rr)
            e_rw = vstep(vec.tensor_scalar(out=rw_sb[:, :], in0=rsml[:, :],
                                           scalar1=ss_b[:, 0:1], scalar2=None, op0=ALU.mult))
            e_uz = vstep(vec.memset(U_sb[:, :], 0.0))
            for jj in range(TOPK):
                qve.wait(e_uz)
                e_mk = vstep(vec.tensor_scalar(out=masks8[:, jj, :], in0=rlog[:, :],
                                               scalar1=topv[:, jj:jj + 1], scalar2=None,
                                               op0=ALU.is_equal))
                qve.wait(e_mk)
                e_uz = vstep(vec.tensor_tensor(out=U_sb[:, :], in0=U_sb[:, :],
                                               in1=masks8[:, jj, :], op=ALU.add))
            qve.wait(e_uz)
            e_ub = vstep(vec.tensor_copy(out=Ub_sb[:, :], in_=U_sb[:, :]))
            qpe.wait(e_ub, e_att_prev)
            e_trimm = pestep(ten.matmul(ps_att[:, 0:E], tri_sb[:, :], Ub_sb[:, :],
                                        start=True, stop=True))
            qpe.wait(e_evac_prev[1], e_cr0)
            e_cbm = pestep(ten.matmul(ps_log[:, 512:512 + E], ones_row[:, :],
                                      carry_row[:, :], start=True, stop=True))
            qpe.wait(e_evac_prev[0])
            e_csm = pestep(ten.matmul(ps_log[0:1, 0:E], onescol[:, :], Ub_sb[:, :],
                                      start=True, stop=True))
            qve.wait(e_trimm)
            e_sl = vstep(vec.tensor_copy(out=slot_sb[:, :], in_=ps_att[:, 0:E]))
            e_att_prev = e_sl
            qve.wait(e_cbm)
            e_cb = vstep(vec.tensor_copy(out=carb_sb[:, :], in_=ps_log[:, 512:512 + E]))
            e_evac_prev[1] = e_cb
            qve.wait(e_csm, e_cb)
            e_cr0 = vstep(vec.tensor_tensor(out=carry_row[:, :], in0=carry_row[:, :],
                                            in1=ps_log[0:1, 0:E], op=ALU.add))
            e_evac_prev[0] = e_cr0
            qve.wait(e_sl)
            e_sl2 = vstep(vec.tensor_tensor(out=slot_sb[:, :], in0=slot_sb[:, :],
                                            in1=carb_sb[:, :], op=ALU.add))
            qve.wait(e_sl2)
            e_fl = vstep(vec.scalar_tensor_tensor(out=flat_sb[:, :], in0=iota64_sb[:, :],
                                                  scalar=float(CAP), in1=slot_sb[:, :],
                                                  op0=ALU.mult, op1=ALU.add))
            qve.wait(e_fl)
            e_ov = vstep(vec.tensor_scalar(out=slot_sb[:, :], in0=slot_sb[:, :],
                                           scalar1=float(CAP), scalar2=None, op0=ALU.is_ge))
            qve.wait(e_ov)
            e_fl2 = vstep(vec.scalar_tensor_tensor(out=flat_sb[:, :], in0=slot_sb[:, :],
                                                   scalar=1.0e9, in1=flat_sb[:, :],
                                                   op0=ALU.mult, op1=ALU.add))
            qve.wait(e_fl2)
            qve.wait(e_sct)
            e_tc = vstep(vec.tensor_scalar(out=tokc[:, :], in0=pidx_sb[:, :],
                                           scalar1=float(j * 128), scalar2=None, op0=ALU.add))
            for jj in range(TOPK):
                qve.wait(e_fl2)
                e_ei = vstep(vec.tensor_tensor(out=rtmp[:, :], in0=masks8[:, jj, :],
                                               in1=flat_sb[:, :], op=ALU.mult))
                qve.wait(e_ei)
                e_es = vstep(vec.reduce_sum(out=m_b[:, :], in_=rtmp[:, :], axis=AX.X))
                qve.wait(e_es)
                e_f3 = vstep(vec.tensor_copy(out=fidx_sb[:, j, jj:jj + 1], in_=m_b[:, :]))
                qve.wait(e_sct, e_tc)
                e_rc = vstep(vec.tensor_copy(out=trsc[:, 0:1], in_=tokc[:, :]))
                qve.wait(e_rc)
                e_rc = vstep(vec.tensor_copy(out=trsc[:, 1:2], in_=rw_sb[:, jj:jj + 1]))
                qgp.wait(e_f3, e_toki, e_rc, e_sct)
                e_sct = inc(d_sct, gp.indirect_dma_start(
                    out=TR_d, out_offset=IndirectOffsetOnAxis(fidx_sb[:, j, jj:jj + 1], 0),
                    in_=trsc[:, :], in_offset=None,
                    bounds_check=rb_tab, oob_is_err=False), 16)
            e_logits_free[0] = (s_ve, s_ve.v)

        ctrow = xn2T32[0:1, 1800:1864]
        tmpr = xn2T32[0:1, 1864:1928]
        t128 = xn2T32[:, 0:128]
        ctcolf = xn2T32[:, 128:129]
        cumcol = xn2T32[:, 130:131]
        s2col = xn2T32[:, 132:133]
        cmp_f = xn2_32[:, 0:NCT]
        idxt_f = xn2_32[:, 256:256 + NCT]
        ekrow = xn2T32[0:1, 1024:1024 + NCT]
        cumk = xn2T32[0:1, 1216:1216 + NCT]
        bases = xn2T32[0:1, 1408:1408 + NCT]
        over_r = xn2T32[0:1, 1600:1600 + NCT]
        tts = xn2T32[0:1, 1798:1799]
        ctcol16 = sb("ctcol16", [128, 1], F16)
        cumcol16 = sb("cumcol16", [128, 1], F16)
        cmp16 = sb("cmp16", [128, NCT], F16)
        oh16 = sb("oh16", [128, NCT], F16)

        qve.wait(e_cr0)
        ev = vstep(vec.memset(ctrow, 0.0))
        for m in range(16):
            qve.wait(ev)
            ev = vstep(vec.tensor_scalar(out=tmpr, in0=carry_row[:, :],
                                         scalar1=float(128 * m), scalar2=None, op0=ALU.is_gt))
            qve.wait(ev)
            ev = vstep(vec.tensor_tensor(out=ctrow, in0=ctrow,
                                         in1=tmpr, op=ALU.add))
        qve.wait(ev)
        e_z = vstep(vec.memset(t128, 0.0))
        qve.wait(e_z)
        e_cp = vstep(vec.tensor_copy(out=t128[0:1, 0:E], in_=ctrow))
        qpe.wait(e_cp, e_x1_prev, e_att_prev)
        e_tr = pestep(ten.transpose(ps_out[:, 0:128], t128, ident32[:, :]))
        qve.wait(e_tr)
        e_cc = vstep(vec.tensor_copy(out=ctcolf, in_=ps_out[:, 0:1]))
        qve.wait(e_cc)
        e_c16 = vstep(vec.tensor_copy(out=ctcol16[:, :], in_=ctcolf))
        qpe.wait(e_c16)
        e_cm = pestep(ten.matmul(ps_att[0:E, 0:1], tri_sb[0:E, 0:E], ctcol16[0:E, 0:1],
                                 start=True, stop=True))
        qve.wait(e_cm)
        ev = vstep(vec.memset(cumcol, 0.0))
        qve.wait(ev)
        e_cu = vstep(vec.tensor_copy(out=xn2T32[0:E, 130:131], in_=ps_att[0:E, 0:1]))
        qve.wait(e_cu)
        e_cu16 = vstep(vec.tensor_copy(out=cumcol16[:, :], in_=cumcol))
        e_s2 = vstep(vec.tensor_tensor(out=s2col, in0=cumcol,
                                       in1=ctcolf, op=ALU.add))
        qve.wait(e_cu16, e_s2)
        ev = vstep(vec.tensor_scalar(out=cmp_f[0:E, :], in0=iotaB192[0:E, :],
                                     scalar1=xn2T32[0:E, 130:131], scalar2=None, op0=ALU.is_ge))
        qve.wait(ev)
        e_cmp16 = vstep(vec.tensor_copy(out=cmp16[0:E, :], in_=cmp_f[0:E, :]))
        qve.wait(e_cmp16)
        ev = vstep(vec.tensor_scalar(out=cmp_f[0:E, :], in0=iotaB192[0:E, :],
                                     scalar1=xn2T32[0:E, 132:133], scalar2=None, op0=ALU.is_ge))
        qve.wait(ev)
        e_cN16 = vstep(vec.tensor_copy(out=oh16[0:E, :], in_=cmp_f[0:E, :]))
        qve.wait(e_cN16)
        e_oh = vstep(vec.tensor_tensor(out=oh16[0:E, :], in0=cmp16[0:E, :],
                                       in1=oh16[0:E, :], op=ALU.subtract))
        qpe.wait(e_cmp16, e_oh)
        e_ek = pestep(ten.matmul(ps_out[0:1, 0:NCT], onescol[0:E, :], cmp16[0:E, :],
                                 start=True, stop=True))
        e_ck = pestep(ten.matmul(ps_out[0:1, 512:512 + NCT], cumcol16[0:E, :], oh16[0:E, :],
                                 start=True, stop=True))
        qve.wait(e_ek)
        e_ekr = vstep(vec.tensor_scalar(out=ekrow, in0=ps_out[0:1, 0:NCT],
                                        scalar1=1.0, scalar2=None, op0=ALU.subtract))
        qve.wait(e_ck)
        e_ckr = vstep(vec.tensor_copy(out=cumk, in_=ps_out[0:1, 512:512 + NCT]))
        e_tt = vstep(vec.reduce_sum(out=tts, in_=ctrow, axis=AX.X))
        qve.wait(e_tt)
        e_ov = vstep(vec.tensor_scalar(out=over_r, in0=iotaB192[0:1, 0:NCT],
                                       scalar1=tts, scalar2=None, op0=ALU.is_ge))
        qve.wait(e_ckr, e_ov, e_ekr)
        e_of = vstep(vec.tensor_tensor(out=cumk, in0=iotaB192[0:1, 0:NCT],
                                       in1=cumk, op=ALU.subtract))
        qve.wait(e_of)
        ev = vstep(vec.tensor_scalar(out=bases, in0=ekrow,
                                     scalar1=2048.0, scalar2=None, op0=ALU.mult))
        qve.wait(ev)
        ev = vstep(vec.scalar_tensor_tensor(out=bases, in0=cumk, scalar=128.0,
                                            in1=bases, op0=ALU.mult, op1=ALU.add))
        qve.wait(ev)
        e_ba = vstep(vec.scalar_tensor_tensor(out=bases, in0=over_r, scalar=1.0e9,
                                              in1=bases, op0=ALU.mult, op1=ALU.add))
        qpe.wait(e_ba)
        e_bc = pestep(ten.matmul(ps_out[:, 512:512 + NCT], ones_row[:, :], bases,
                                 start=True, stop=True))
        qve.wait(e_bc)
        ev = vstep(vec.tensor_tensor(out=idxt_f, in0=ps_out[:, 512:512 + NCT],
                                     in1=pidx_sb[:, 0:1].broadcast_to([128, NCT]), op=ALU.add))
        qve.wait(ev)
        e_ix1 = vstep(vec.tensor_copy(out=IDXT_u[:, :], in_=idxt_f))
        qve.wait(e_ix1)
        ev = vstep(vec.tensor_scalar(out=bases, in0=ekrow,
                                     scalar1=128.0, scalar2=None, op0=ALU.mult))
        qve.wait(ev)
        e_bd = vstep(vec.scalar_tensor_tensor(out=bases, in0=over_r, scalar=1.0e9,
                                              in1=bases, op0=ALU.mult, op1=ALU.add))
        qpe.wait(e_bd, e_ix1)
        e_bc2 = pestep(ten.matmul(ps_out[:, 512:512 + NCT], ones_row[:, :], bases,
                                  start=True, stop=True))
        qve.wait(e_bc2)
        ev = vstep(vec.tensor_tensor(out=idxt_f, in0=ps_out[:, 512:512 + NCT],
                                     in1=pidx_sb[:, 0:1].broadcast_to([128, NCT]), op=ALU.add))
        qve.wait(ev)
        e_ix2 = vstep(vec.tensor_copy(out=IDXD_u[:, :], in_=idxt_f))

        e_B_done = (e_x1s, e_xns, e_f3, e_sct, e_xtok, e_ix1, e_ix2)
        if "B1" in dbg:
            dX1 = dbg_out("dbg_X1", [NT * 128, D], F32)
            dXN = dbg_out("dbg_XN2T", [NT, 128, KC, 128], F16)
            dT = dbg_out("dbg_TR", [E * CAP, 2], F32)
            dF = dbg_out("dbg_fidx", [128, NT, TOPK], U32)
            dI1 = dbg_out("dbg_IDXT", [128, NCT], U32)
            dI2 = dbg_out("dbg_IDXD", [128, NCT], U32)
            qsy.wait(*e_B_done)
            inc(d_dbg, sync.dma_start(out=dX1, in_=X1_sc), 16)
            inc(d_dbg, sync.dma_start(out=dXN, in_=XN2T_sc), 16)
            inc(d_dbg, sync.dma_start(out=dT, in_=TR_d), 16)
            inc(d_dbg, sync.dma_start(out=dI1, in_=IDXT_u[:, :]), 16)
            inc(d_dbg, sync.dma_start(out=dI2, in_=IDXD_u[:, :]), 16)
            ed = inc(d_dbg, sync.dma_start(out=dF, in_=fidx_sb[:, :, :]), 16)
            qsy.wait(ed)
        esB.close()

    e_C_done = None
    if "C" in phases:
        esC = ExitStack()
        sb = lambda n, s, d=F32: esC.enter_context(nc.sbuf_tensor(n, s, d))
        psm = lambda n, s, d=F32: esC.enter_context(nc.psum_tensor(n, s, d))
        wrot = Rot(mksem, [sb(f"wbuf{i}", [128, 24576], F16) for i in range(2)], "d_wb")
        xrotC = Rot(mksem, [sb(f"xg{i}", [128, D], F16) for i in range(2)], "d_xg")
        d_ys = mksem("d_ys")
        d_trb = mksem("d_trb")
        trall = sb("trall", [128, NCT, 2])
        tokall = sb("tokall", [128, NCT], U32)
        xgT = sb("xgT", [128, D], F16)
        sg_c = sb("sg_c", [128, I])
        intb = sb("intb", [128, I], F16)
        intT = sb("intT", [128, I], F16)
        ps_gu = psm("ps_gu", [128, I2])
        ps_trc = psm("ps_trc", [128, 512], F16)
        ps_y = psm("ps_y", [128, D])

        run_precast_jobs(len(precast_jobs))
        phase_barrier(*(list(e_B_done) + list(e_A_done)), (d_wcs, d_wcs.v) if d_wcs.v else None)

        for k in range(NCT):
            e_trb = inc(d_trb, gp.indirect_dma_start(
                out=trall[:, k, :], out_offset=None,
                in_=TR_d, in_offset=IndirectOffsetOnAxis(IDXT_u[:, k:k + 1], 0),
                bounds_check=rb_tab, oob_is_err=False), 16)
        qve.wait(e_trb)
        e_tokall = vstep(vec.tensor_copy(out=tokall[:, :], in_=trall[:, :, 0]))

        WALL_flat = WALL_sc.rearrange("e p n -> (e p) n")
        yrot = Rot(mksem, [sb(f"ytb{i}", [128, D], F16) for i in range(2)], "d_yt")

        e_w = [None] * NCT
        e_xg = [None] * NCT
        e_ye_k = [None] * NCT

        def issue_wg(k):
            ws = k % 2
            qgp.wait(wrot.free[ws])
            e_w[k] = inc(wrot.sems[ws], gp.indirect_dma_start(
                out=wrot.t[ws][:, :], out_offset=None,
                in_=WALL_flat, in_offset=IndirectOffsetOnAxis(IDXD_u[:, k:k + 1], 0),
                bounds_check=rb_we, oob_is_err=False), 16)

        def issue_xg(k):
            xs = k % 2
            qgp.wait(xrotC.free[xs], e_tokall)
            e_xg[k] = inc(xrotC.sems[xs], gp.indirect_dma_start(
                out=xrotC.t[xs][:, :], out_offset=None,
                in_=XN2_sc, in_offset=IndirectOffsetOnAxis(tokall[:, k:k + 1], 0),
                bounds_check=rb_tok, oob_is_err=False), 16)

        e_ys = None

        def issue_ys(k):
            nonlocal e_ys
            qgp.wait(e_ye_k[k])
            e_ys = inc(yrot.sems[k % 2], gp.indirect_dma_start(
                out=Y_sc, out_offset=IndirectOffsetOnAxis(IDXT_u[:, k:k + 1], 0),
                in_=yrot.t[k % 2][:, :], in_offset=None,
                bounds_check=rb_tab, oob_is_err=False), 16)
            yrot.free[k % 2] = e_ys

        e_sg_prev = None
        e_in_prev = None
        e_itv_prev = None
        e_tT_prev = None
        e_xgT_read = None
        issue_wg(0)
        issue_xg(0)
        for k in range(NCT):
            if k + 1 < NCT:
                issue_wg(k + 1)
                issue_xg(k + 1)
            if k >= 1:
                issue_ys(k - 1)
            ws = k % 2
            xs = k % 2
            qpe.wait(e_itv_prev)
            e_xgT = transpose16(lambda cc: xrotC.t[xs][:, cc * 128:(cc + 1) * 128], KC,
                                ps_trc, xgT, e_xg[k], extra_pe_wait=e_xgT_read,
                                ident=identb)
            xrotC.free[xs] = e_xgT
            qpe.wait(e_xgT, e_w[k], e_sg_prev, e_in_prev)
            for kc in range(KC):
                e_gu = mm512(ps_gu, xgT[:, kc * 128:(kc + 1) * 128],
                             wrot.t[ws][:, kc * 1024:(kc + 1) * 1024], 1024,
                             kc == 0, kc == KC - 1)
            e_gu = pestep(e_gu)
            e_xgT_read = e_gu
            qsc.wait(e_gu, e_in_prev)
            e_sg = scstep(sca.activation(out=sg_c[:, :], in_=ps_gu[:, 0:I], func=AF.Silu))
            e_sg_prev = e_sg
            qve.wait(e_sg, e_tT_prev)
            e_in = vstep(vec.scalar_tensor_tensor(out=intb[:, :], in0=ps_gu[:, I:I2],
                                                  scalar=trall[:, k, 1:2],
                                                  in1=sg_c[:, :],
                                                  op0=ALU.mult, op1=ALU.mult))
            e_in_prev = e_in
            qpe.wait(e_in, e_itv_prev)
            for u in range(4):
                e_t = pestep(ten.transpose(ps_trc[:, u * 128:(u + 1) * 128],
                                           intb[:, u * 128:(u + 1) * 128], identb[:, :]))
            e_tT_prev = e_t
            qve.wait(e_t)
            e_itv = vstep(vec.tensor_copy(out=intT[:, :], in_=ps_trc[:, :]))
            e_itv_prev = e_itv
            qpe.wait(e_itv, e_ye_k[k - 2] if k >= 2 else None)
            for ic in range(4):
                e_dn = mm512(ps_y, intT[:, ic * 128:(ic + 1) * 128],
                             wrot.t[ws][:, 16384 + ic * 2048:16384 + (ic + 1) * 2048],
                             2048, ic == 0, ic == 3)
            e_dn = pestep(e_dn)
            wrot.free[ws] = e_dn
            qsc.wait(e_dn, yrot.free[k % 2])
            e_ye_k[k] = scstep(sca.activation(out=yrot.t[k % 2][:, :], in_=ps_y[:, :],
                                              func=AF.Copy))
        issue_ys(NCT - 1)
        e_C_done = None
        e_C_dones = (yrot.free[0], yrot.free[1])
        esC.close()

    if "D" in phases:
        esD = ExitStack()
        sb = lambda n, s, d=F32: esD.enter_context(nc.sbuf_tensor(n, s, d))
        psm = lambda n, s, d=F32: esD.enter_context(nc.psum_tensor(n, s, d))
        shgu16 = sb("shgu16", [128, KC, 1024], F16)
        shd16 = sb("shd16", [128, 4, 2048], F16)
        g116 = sb("g116", [128, KC, 1], F16)
        phase_barrier(*(list(e_B_done) + list(e_A_done) +
                        (list(e_C_dones) if "C" in phases else [])))
        lrotD = Rot(mksem, [sb(f"landD{i}", [128, 4096], F32) for i in range(2)], "d_lD")
        d_xtd = mksem("d_xtd")
        d_x1d = mksem("d_x1d")
        d_gth = mksem("d_gth")
        stream_cast(shgu_in, D, 1024, shgu16, 0, lrotD)
        for cch in range(2):
            ls = lrotD.nxt()
            qsy.wait(lrotD.free[ls])
            e_l = inc(lrotD.sems[ls], sync.dma_start(
                out=lrotD.t[ls][:, :].rearrange("p (c n) -> p c n", c=2),
                in_=shd_in[cch * 256:(cch + 1) * 256, :].rearrange("(c p) n -> p c n", p=128)), 16)
            qve.wait(e_l)
            e_c = vstep(vec.tensor_copy(
                out=shd16[:, cch * 2:(cch + 1) * 2, :],
                in_=lrotD.t[ls][:, :].rearrange("p (c n) -> p c n", c=2)))
            lrotD.free[ls] = e_c
        ls = lrotD.nxt()
        qsy.wait(lrotD.free[ls])
        e_l = inc(lrotD.sems[ls], sync.dma_start(out=lrotD.t[ls][:, 0:16], in_=g1_in), 16)
        qve.wait(e_l)
        e_shw = vstep(vec.tensor_copy(out=g116[:, :, :],
                                      in_=lrotD.t[ls][:, 0:16].rearrange("p (c n) -> p c n", n=1)))

        xtd = sb("xtd", [128, D], F16)
        x1d = sb("x1d", [128, D])
        acc = sb("acc", [128, D])
        gbuf = sb("gbuf", [128, D])
        sgd = sb("sgd", [128, I])
        intd = sb("intd", [128, I], F16)
        intTd = sb("intTd", [128, I], F16)
        shg_d = sb("shg_d", [128, 1])
        outt = sb("outt", [128, D])
        ps_shgu = psm("ps_shgu", [128, 1024])
        ps_trd = psm("ps_trd", [128, 512], F16)
        ps_shd = psm("ps_shd", [128, 2048])
        ps_g1 = psm("ps_g1", [128, 1])

        if "C" in phases:
            qgp.wait(*e_C_dones)
            qsy.wait(*e_C_dones)
        if e_B_done:
            qgp.wait(e_B_done[2], e_B_done[3])

        prev_out_dma = None
        e_out_prev = None
        e_g_d = None
        e_sg_d = None
        e_int_d = None
        e_itv_d = None
        e_sig_d = None
        e_acc1 = None
        for j in range(n_nt):
            qsy.wait(e_g_d)
            e_xt = inc(d_xtd, sync.dma_start(
                out=xtd[:, :], in_=XN2T_sc[j, :, :, :].rearrange("p c x -> p (c x)")), 16)
            qsy.wait(e_acc1)
            e_x1l = inc(d_x1d, sync.dma_start(out=x1d[:, :],
                                              in_=X1_sc[j * 128:(j + 1) * 128, :]), 16)
            qpe.wait(e_xt, e_shw, e_sg_d, e_int_d)
            for kc in range(KC):
                e_g = mm512(ps_shgu, xtd[:, kc * 128:(kc + 1) * 128],
                            shgu16[:, kc, :], 1024, kc == 0, kc == KC - 1)
            e_g = pestep(e_g)
            qpe.wait(e_sig_d)
            for kc in range(KC):
                e_g1m = ten.matmul(ps_g1[:, :], xtd[:, kc * 128:(kc + 1) * 128],
                                   g116[:, kc, :], start=(kc == 0), stop=(kc == KC - 1))
            e_g1m = pestep(e_g1m)
            e_g_d = e_g1m
            qsc.wait(e_g, e_int_d)
            e_sg = scstep(sca.activation(out=sgd[:, :], in_=ps_shgu[:, 0:I], func=AF.Silu))
            e_sg_d = e_sg
            qve.wait(e_sg, e_itv_d)
            e_int = vstep(vec.tensor_tensor(out=intd[:, :], in0=ps_shgu[:, I:I2],
                                            in1=sgd[:, :], op=ALU.mult))
            e_int_d = e_int
            qpe.wait(e_int, e_itv_d)
            for u in range(4):
                e_t = pestep(ten.transpose(ps_trd[:, u * 128:(u + 1) * 128],
                                           intd[:, u * 128:(u + 1) * 128], identb[:, :]))
            qve.wait(e_t)
            e_itv = vstep(vec.tensor_copy(out=intTd[:, :], in_=ps_trd[:, :]))
            e_itv_d = e_itv
            qpe.wait(e_itv, e_out_prev)
            for ic in range(4):
                e_dn = mm512(ps_shd, intTd[:, ic * 128:(ic + 1) * 128],
                             shd16[:, ic, :], 2048, ic == 0, ic == 3)
            e_dn = pestep(e_dn)
            qsc.wait(e_g1m)
            e_sig = scstep(sca.activation(out=shg_d[:, :], in_=ps_g1[:, :], func=AF.Sigmoid))
            e_sig_d = e_sig

            qve.wait(e_x1l)
            e_acc = vstep(vec.tensor_copy(out=acc[:, :], in_=x1d[:, :]))
            e_acc1 = e_acc
            for jj in range(TOPK):
                qgp.wait(e_acc)
                e_gth = inc(d_gth, gp.indirect_dma_start(
                    out=gbuf[:, :], out_offset=None,
                    in_=Y_sc, in_offset=IndirectOffsetOnAxis(fidx_sb[:, j, jj:jj + 1], 0),
                    bounds_check=rb_tab, oob_is_err=False), 16)
                qve.wait(e_gth, e_acc)
                e_acc = vstep(vec.tensor_tensor(out=acc[:, :], in0=acc[:, :],
                                                in1=gbuf[:, :], op=ALU.add))
            qve.wait(e_acc, e_dn, e_sig, prev_out_dma)
            e_out = vstep(vec.scalar_tensor_tensor(out=outt[:, :], in0=ps_shd[:, :],
                                                   scalar=shg_d[:, 0:1], in1=acc[:, :],
                                                   op0=ALU.mult, op1=ALU.add))
            e_out_prev = e_out
            qsy.wait(e_out)
            prev_out_dma = inc(s_out, sync.dma_start(out=y_out[j * 128:(j + 1) * 128, :],
                                                     in_=outt[:, :]), 16)
        qsy.wait(prev_out_dma)
        esD.close()

    for s in all_sems:
        if s.v:
            qsy.wait((s, s.v))
            qgp.wait((s, s.v))

    return nc, dbg_outs



def _prep_in_maps(inputs):
    x = np.asarray(inputs["L0_attn_x"], dtype=np.float32)
    wq = np.ascontiguousarray(inputs["L0_attn_wq"], dtype=np.float32)
    wkv = np.ascontiguousarray(np.concatenate(
        [np.asarray(inputs["L0_attn_wk"], np.float32),
         np.asarray(inputs["L0_attn_wv"], np.float32)], axis=1))
    wo = np.ascontiguousarray(inputs["L0_attn_wo"], dtype=np.float32)
    bq = np.asarray(inputs["L0_attn_bq"], dtype=np.float32).reshape(1, D)
    bkv = np.concatenate([np.asarray(inputs["L0_attn_bk"], np.float32),
                          np.asarray(inputs["L0_attn_bv"], np.float32)]).reshape(1, 1024)
    bo = np.asarray(inputs["L0_attn_bo"], dtype=np.float32).reshape(1, D)
    qns = np.asarray(inputs["L0_attn_q_norm_scale"], np.float32).reshape(1, 128)
    kns = np.asarray(inputs["L0_attn_k_norm_scale"], np.float32).reshape(1, 128)
    gate_T = np.ascontiguousarray(np.asarray(inputs["L0_ffn_gate_w"], np.float32).T)
    guw_T = np.ascontiguousarray(np.asarray(inputs["L0_ffn_gu_w"], np.float32).transpose(0, 2, 1))
    dpw_T = np.ascontiguousarray(np.asarray(inputs["L0_ffn_dp_w"], np.float32).transpose(0, 2, 1))
    shgu_T = np.ascontiguousarray(
        np.concatenate([np.asarray(inputs["L0_ffn_sh_gate_w"], np.float32),
                        np.asarray(inputs["L0_ffn_sh_up_w"], np.float32)], axis=0).T)
    shd_T = np.ascontiguousarray(np.asarray(inputs["L0_ffn_sh_down_w"], np.float32).T)
    g1_T = np.ascontiguousarray(
        np.asarray(inputs["L0_ffn_sh_gate1_w"], np.float32).reshape(KC, 128).T)

    inv_freq = 1.0 / (10000.0 ** (np.arange(0, HD, 2, dtype=np.float32) / HD))
    pos = np.arange(S, dtype=np.float32)
    freqs = np.outer(pos, inv_freq)
    cos_t = np.ascontiguousarray(np.cos(freqs), dtype=np.float32)
    sin_t = np.ascontiguousarray(np.sin(freqs), dtype=np.float32)

    in_maps = []
    for c in range(8):
        b, r = c // 4, c % 4
        rows = (np.arange(NT)[:, None] * 512 + r * 128 + np.arange(128)[None, :]).ravel()
        mask = np.zeros((128, 512), dtype=np.float32)
        mask[:, r * 128 + 128:] = MASK_NEG / 128.0
        tri = np.where(np.arange(128)[None, :] > np.arange(128)[:, None],
                       MASK_NEG / 128.0, 0.0).astype(np.float32)
        mask[:, r * 128:r * 128 + 128] = tri
        in_maps.append({
            "x_pre": np.ascontiguousarray(x[b]),
            "x_own": np.ascontiguousarray(x[b][rows]),
            "cos_pre": cos_t, "sin_pre": sin_t,
            "cos_own": np.ascontiguousarray(cos_t[rows]),
            "sin_own": np.ascontiguousarray(sin_t[rows]),
            "mask_in": mask,
            "wq_in": wq, "wkv_in": wkv, "wo_in": wo,
            "bq_in": bq, "bkv_in": bkv, "bo_in": bo,
            "qns_in": qns, "kns_in": kns,
            "gate_in": gate_T, "guw_in": guw_T, "dpw_in": dpw_T,
            "shgu_in": shgu_T, "shd_in": shd_T, "g1_in": g1_T,
        })
    return in_maps


def _assemble(results):
    out = np.empty((B, S, D), dtype=np.float32)
    for c in range(8):
        b, r = c // 4, c % 4
        rows = (np.arange(NT)[:, None] * 512 + r * 128 + np.arange(128)[None, :]).ravel()
        out[b][rows] = results[c]["y"]
    return out


def kernel(**inputs):
    nc, _ = build()
    in_maps = _prep_in_maps(inputs)
    res = run_bass_kernel_spmd(nc, in_maps, core_ids=list(range(8)))
    return _assemble(res.results)


if __name__ == "__main__":
    import time
    t0 = time.time()
    nc, _ = build()
    print("build ok", time.time() - t0, "s; instructions:", nc.n_instructions())


# version 11
# speedup vs baseline: 1.0283x; 1.0283x over previous
import os
os.environ.setdefault("NEURON_SCRATCHPAD_PAGE_SIZE", "1024")

from contextlib import ExitStack

import numpy as np
import ml_dtypes
import concourse.bass as bass
import concourse.mybir as mybir
from concourse.bass_utils import run_bass_kernel_spmd
from concourse.bass_core_rust import IndirectOffsetOnAxis

F32 = mybir.dt.float32
F16 = mybir.dt.float16
U32 = mybir.dt.uint32
AF = mybir.ActivationFunctionType
ALU = mybir.AluOpType
AX = mybir.AxisListType

B, S, D = 2, 8192, 2048
HEADS, GROUPS, HD = 16, 4, 128
NT = 16
PT = S // 128
KC = D // 128
E, I2, I = 64, 1024, 512
SI = 512
TOPK = 8
CAP = 2048
NCT = 192
BF16 = mybir.dt.bfloat16
LN_EPS = 1e-5
RMS_EPS = 1e-6
MASK_NEG = -30000.0
INV128 = 1.0 / 128.0


class Sem:
    __slots__ = ("h", "v", "name")

    def __init__(self, nc, name):
        self.h = nc.alloc_semaphore(name=name)
        self.v = 0
        self.name = name


def inc(sem, ins, n=1):
    ins.then_inc(sem.h, n)
    sem.v += n
    assert sem.v < 2 * 10 ** 9
    return (sem, sem.v)


class Q:
    def __init__(self, eng):
        self.e = eng
        self.seen = {}

    def wait(self, *evts):
        for evt in evts:
            if evt is None:
                continue
            sem, v = evt
            if self.seen.get(sem, 0) < v:
                self.e.wait_ge(sem.h, v)
                self.seen[sem] = v


class Rot:

    def __init__(self, mksem, tiles, name):
        self.t = tiles
        self.n = len(tiles)
        self.sems = [mksem(f"{name}{i}") for i in range(self.n)]
        self.ready = [None] * self.n
        self.free = [None] * self.n
        self.i = -1

    def nxt(self):
        self.i += 1
        return self.i % self.n


def build(phases=("A", "B", "C", "D"), dbg=(), n_pt=PT, n_nt=NT, n_e=E):
    nc = bass.Bass("TRN2", target_bir_lowering=False, debug=False)
    dbg = set(dbg)
    ap = lambda t: t.ap()
    all_sems = []

    def mksem(name):
        s = Sem(nc, name)
        all_sems.append(s)
        return s

    x_pre = ap(nc.dram_tensor("x_pre", [S, D], F32, kind="ExternalInput"))
    x_own = ap(nc.dram_tensor("x_own", [NT * 128, D], F32, kind="ExternalInput"))
    cos_pre = ap(nc.dram_tensor("cos_pre", [S, 64], F32, kind="ExternalInput"))
    sin_pre = ap(nc.dram_tensor("sin_pre", [S, 64], F32, kind="ExternalInput"))
    cos_own = ap(nc.dram_tensor("cos_own", [NT * 128, 64], F32, kind="ExternalInput"))
    sin_own = ap(nc.dram_tensor("sin_own", [NT * 128, 64], F32, kind="ExternalInput"))
    mask_in = ap(nc.dram_tensor("mask_in", [128, 512], F32, kind="ExternalInput"))
    wq_in = ap(nc.dram_tensor("wq_in", [D, D], F32, kind="ExternalInput"))
    wkv_in = ap(nc.dram_tensor("wkv_in", [D, 1024], F32, kind="ExternalInput"))
    wo_in = ap(nc.dram_tensor("wo_in", [D, D], F32, kind="ExternalInput"))
    bq_in = ap(nc.dram_tensor("bq_in", [1, D], F32, kind="ExternalInput"))
    bkv_in = ap(nc.dram_tensor("bkv_in", [1, 1024], F32, kind="ExternalInput"))
    bo_in = ap(nc.dram_tensor("bo_in", [1, D], F32, kind="ExternalInput"))
    qns_in = ap(nc.dram_tensor("qns_in", [1, 128], F32, kind="ExternalInput"))
    kns_in = ap(nc.dram_tensor("kns_in", [1, 128], F32, kind="ExternalInput"))
    gate_in = ap(nc.dram_tensor("gate_in", [D, E], F32, kind="ExternalInput"))
    guw_in = ap(nc.dram_tensor("guw_in", [E, D, I2], F32, kind="ExternalInput"))
    dpw_in = ap(nc.dram_tensor("dpw_in", [E, I, D], F32, kind="ExternalInput"))
    shgu_in = ap(nc.dram_tensor("shgu_in", [D, 1024], F32, kind="ExternalInput"))
    shd_in = ap(nc.dram_tensor("shd_in", [SI, D], F32, kind="ExternalInput"))
    g1_in = ap(nc.dram_tensor("g1_in", [128, KC], F32, kind="ExternalInput"))

    y_out = ap(nc.dram_tensor("y", [NT * 128, D], F32, kind="ExternalOutput"))

    K_sc = ap(nc.dram_tensor("K_sc", [128, GROUPS, S], F16, kind="Internal"))
    V_sc = ap(nc.dram_tensor("V_sc", [S, 512], F16, kind="Internal"))
    Q_sc = ap(nc.dram_tensor("Q_sc", [NT, 128, HEADS, 128], F16, kind="Internal"))
    X1_sc = ap(nc.dram_tensor("X1_sc", [NT * 128, D], F32, kind="Internal"))
    XN2T_sc = ap(nc.dram_tensor("XN2T_sc", [NT, 128, KC, 128], F16, kind="Internal"))
    XN2_sc = ap(nc.dram_tensor("XN2_sc", [NT * 128, D], F16, kind="Internal"))
    Y_sc = ap(nc.dram_tensor("Y_sc", [E * CAP, D], F16, kind="Internal"))
    TR_d = ap(nc.dram_tensor("TR_d", [E * CAP, 2], F32, kind="Internal"))
    WALL_sc = ap(nc.dram_tensor("WALL_sc", [E, 128, 24576], F16, kind="Internal"))

    dbg_outs = {}

    def dbg_out(name, shape, dt=F32):
        dbg_outs[name] = ap(nc.dram_tensor(name, shape, dt, kind="ExternalOutput"))
        return dbg_outs[name]

    ident16_d = nc.inline_tensor(np.eye(128, dtype=np.float16), "ident16")
    ident32_d = nc.inline_tensor(np.eye(128, dtype=np.float32), "ident32")
    ones_row_d = nc.inline_tensor(np.ones((1, 128), dtype=np.float32), "ones_row")
    iota64_d = nc.inline_tensor(
        np.broadcast_to(np.arange(64, dtype=np.float32), (128, 64)).copy(), "iota64")
    pidx_d = nc.inline_tensor(np.arange(128, dtype=np.float32).reshape(128, 1), "pidx")
    identb_d = nc.inline_tensor(np.eye(128, dtype=np.float16), "identb")
    iotaB192_d = nc.inline_tensor(
        np.broadcast_to(np.arange(NCT, dtype=np.float32), (128, NCT)).copy(), "iotaB192")
    tri_d = nc.inline_tensor(np.triu(np.ones((128, 128)), 1).astype(np.float16), "tri")
    onescol_d = nc.inline_tensor(np.ones((128, 1), dtype=np.float16), "onescol")

    s_pe = mksem("s_pe")
    s_ve = mksem("s_ve")
    s_sc = mksem("s_sc")
    d_pre = mksem("d_pre")
    d_brow = mksem("d_brow")
    d_dbg = mksem("d_dbg")
    s_out = mksem("s_out")

    sync, vec, sca, ten, gp = nc.sync, nc.vector, nc.scalar, nc.tensor, nc.gpsimd
    qsy, qve, qsc, qpe, qgp = Q(sync), Q(vec), Q(sca), Q(ten), Q(gp)
    rb_tab = gp.alloc_register("rb_tab")
    gp.reg_mov(rb_tab, E * CAP - 1)
    rb_tok = gp.alloc_register("rb_tok")
    gp.reg_mov(rb_tok, NT * 128 - 1)
    rb_we = gp.alloc_register("rb_we")
    gp.reg_mov(rb_we, E * 128 - 1)

    vstep = lambda ins: inc(s_ve, ins)
    scstep = lambda ins: inc(s_sc, ins)
    pestep = lambda ins: inc(s_pe, ins)
    pe_now = lambda: (s_pe, s_pe.v)

    def phase_barrier(*extras):
        evs = [(s_pe, s_pe.v), (s_ve, s_ve.v), (s_sc, s_sc.v)] + [e for e in extras if e]
        for q in (qsy, qve, qsc, qpe, qgp):
            q.wait(*evs)

    def mm512(ps_t, lhsT_ap, rhs_ap, n, start, stop, step=512):
        for w0 in range(0, n, step):
            ins = ten.matmul(ps_t[:, w0:w0 + step], lhsT_ap, rhs_ap[:, w0:w0 + step],
                             start=start, stop=stop)
        return ins

    ident16 = nc.alloc_sbuf_tensor("ident16_sb", [128, 128], F16)
    ident32 = nc.alloc_sbuf_tensor("ident32_sb", [128, 128], F32)
    ones_row = nc.alloc_sbuf_tensor("ones_row_sb", [1, 128], F32)
    iota64_sb = nc.alloc_sbuf_tensor("iota64_sb", [128, 64], F32)
    pidx_sb = nc.alloc_sbuf_tensor("pidx_sb", [128, 1], F32)
    mask_sb = nc.alloc_sbuf_tensor("mask_sb", [128, 512], F32)
    qns_b = nc.alloc_sbuf_tensor("qns_b", [128, 128], F32)
    kns_b = nc.alloc_sbuf_tensor("kns_b", [128, 128], F32)
    fidx_sb = nc.alloc_sbuf_tensor("fidx_sb", [128, NT, TOPK], U32)
    IDXT_u = nc.alloc_sbuf_tensor("IDXT_u", [128, NCT], U32)
    IDXD_u = nc.alloc_sbuf_tensor("IDXD_u", [128, NCT], U32)
    identb = nc.alloc_sbuf_tensor("identb_sb", [128, 128], F16)
    iotaB192 = nc.alloc_sbuf_tensor("iotaB192_sb", [128, NCT], F32)
    tri_sb = nc.alloc_sbuf_tensor("tri_sb", [128, 128], F16)
    onescol = nc.alloc_sbuf_tensor("onescol_sb", [128, 1], F16)
    nsrow = nc.alloc_sbuf_tensor("nsrow", [1, 256], F32)

    for dst, src in [(ident16, ident16_d), (ident32, ident32_d), (ones_row, ones_row_d),
                     (iota64_sb, iota64_d), (pidx_sb, pidx_d), (identb, identb_d),
                     (tri_sb, tri_d), (onescol, onescol_d), (iotaB192, iotaB192_d)]:
        inc(d_pre, sync.dma_start(out=dst[:, :], in_=src.ap()), 16)
    inc(d_pre, sync.dma_start(out=mask_sb[:, :], in_=mask_in), 16)
    inc(d_pre, sync.dma_start(out=nsrow[:, 0:128], in_=qns_in), 16)
    e_pre = inc(d_pre, sync.dma_start(out=nsrow[:, 128:256], in_=kns_in), 16)

    with nc.psum_tensor("ps_b0", [128, 256], F32) as ps_b0:
        qpe.wait(e_pre)
        e_p = pestep(ten.matmul(ps_b0[:, :], ones_row[:, :], nsrow[:, :], start=True, stop=True))
        qve.wait(e_p)
        vstep(vec.tensor_copy(out=qns_b[:, :], in_=ps_b0[:, 0:128]))
        vstep(vec.tensor_copy(out=kns_b[:, :], in_=ps_b0[:, 128:256]))

    d_toki = mksem("d_toki")
    with nc.sbuf_tensor("tokinit", [128, 2048], F32) as toki:
        e_mi = vstep(vec.memset(toki[:, :], 1.0e9))
        qgp.wait(e_mi)
        e_toki = inc(d_toki, gp.dma_start(
            out=TR_d.rearrange("(p c) n -> p (c n)", p=128), in_=toki[:, :]), 16)
    qve.wait(e_toki)
    qsy.wait(e_toki)

    d_wcs = mksem("d_wcs")
    precast_jobs = []
    if "C" in phases:
        def mk_pc(src_ap, cc, out_ap):
            def f():
                return inc(d_wcs, gp.dma_start(
                    out=out_ap.rearrange("p (c n) -> p c n", c=cc),
                    in_=src_ap.rearrange("(c p) n -> p c n", p=128)), 16)
            return f

        for e in range(E):
            for cb in range(4):
                precast_jobs.append(mk_pc(guw_in[e, cb * 512:(cb + 1) * 512, :], 4,
                                          WALL_sc[e, :, cb * 4096:(cb + 1) * 4096]))
            for cb in range(2):
                precast_jobs.append(mk_pc(dpw_in[e, cb * 256:(cb + 1) * 256, :], 2,
                                          WALL_sc[e, :, 16384 + cb * 4096:16384 + (cb + 1) * 4096]))

    def run_precast_jobs(n):
        for _ in range(n):
            if precast_jobs:
                precast_jobs.pop(0)()

    brow_consumed = [e_pre]
    bb_prev_evac = [None]
    _bb = [0]

    def bias_bcast(row_ap, width, dst_sb, dst_off):
        _bb[0] += 1
        with nc.sbuf_tensor(f"brow{_bb[0]}", [1, width], F32) as brow, \
             nc.psum_tensor(f"ps_bb{_bb[0]}", [128, width], F32) as ps_bb:
            qsy.wait(brow_consumed[0])
            e_b = inc(d_brow, sync.dma_start(out=brow[:, :], in_=row_ap), 16)
            qpe.wait(e_b, bb_prev_evac[0])
            for w0 in range(0, width, 512):
                e_p = pestep(ten.matmul(ps_bb[:, w0:w0 + 512], ones_row[:, :],
                                        brow[:, w0:w0 + 512], start=True, stop=True))
            brow_consumed[0] = e_p
            qve.wait(e_p)
            e_ev = vstep(vec.tensor_copy(out=dst_sb[:, dst_off:dst_off + width],
                                         in_=ps_bb[:, :]))
            bb_prev_evac[0] = e_ev
            return e_ev

    def stream_cast(src_ap, rows, cols, dst_sb, kc0, lrot):
        n = rows // 128
        e_c = None
        for r in range(0, n, 2):
            take = min(2, n - r)
            ls = lrot.nxt()
            qsy.wait(lrot.free[ls])
            e_l = inc(lrot.sems[ls], sync.dma_start(
                out=lrot.t[ls][:, 0:take * cols].rearrange("p (c n) -> p c n", c=take),
                in_=src_ap[r * 128:(r + take) * 128, :].rearrange("(c p) n -> p c n", p=128)), 16)
            qve.wait(e_l)
            e_c = vstep(vec.tensor_copy(
                out=dst_sb[:, kc0 + r:kc0 + r + take, 0:cols],
                in_=lrot.t[ls][:, 0:take * cols].rearrange("p (c n) -> p c n", c=take)))
            lrot.free[ls] = e_c
        return e_c

    def layernorm_tile(xb, xc, xn16, m, ss, sqscr, evt_x, extra32=None):
        qve.wait(evt_x)
        e1 = vstep(vec.reduce_sum(out=m[:, :], in_=xb, axis=AX.X))
        qve.wait(e1)
        e2 = vstep(vec.tensor_scalar(out=m[:, :], in0=m[:, :], scalar1=1.0 / 2048.0,
                                     scalar2=None, op0=ALU.mult))
        qve.wait(e2)
        e3 = vstep(vec.tensor_scalar(out=xc, in0=xb, scalar1=m[:, 0:1],
                                     scalar2=None, op0=ALU.subtract))
        qsc.wait(e3)
        e4 = scstep(sca.activation(out=sqscr, in_=xc, func=AF.Square, accum_out=ss[:, :]))
        qve.wait(e4)
        e5 = vstep(vec.tensor_scalar(out=ss[:, :], in0=ss[:, :], scalar1=1.0 / 2048.0,
                                     scalar2=LN_EPS, op0=ALU.mult, op1=ALU.add))
        qsc.wait(e5)
        e6 = scstep(sca.activation(out=ss[:, :], in_=ss[:, :], func=AF.Sqrt))
        qve.wait(e6)
        e7 = vstep(vec.reciprocal(out=ss[:, :], in_=ss[:, :]))
        qve.wait(e7)
        e8 = vstep(vec.tensor_scalar(out=xn16, in0=xc, scalar1=ss[:, 0:1],
                                     scalar2=None, op0=ALU.mult))
        e9 = None
        if extra32 is not None:
            e9 = vstep(vec.tensor_scalar(out=extra32, in0=xc, scalar1=ss[:, 0:1],
                                         scalar2=None, op0=ALU.mult))
        return e8, e3, e9

    def transpose16(src_col, n_chunks, ps_tr, dst16, src_evt,
                    extra_pe_wait=None, extra_ve_wait=None, ident=None):
        last = None
        lastv = None
        for r in range(0, n_chunks, 4):
            take = min(4, n_chunks - r)
            qpe.wait(src_evt, extra_pe_wait)
            if last is not None:
                qpe.wait(last)
            for u in range(take):
                e_t = pestep(ten.transpose(ps_tr[:, u * 128:(u + 1) * 128],
                                           src_col(r + u),
                                           (ident if ident is not None else ident16)[:, :]))
            if (r // 4) % 2 == 0:
                qve.wait(e_t, extra_ve_wait)
                last = vstep(vec.tensor_copy(out=dst16[:, r * 128:(r + take) * 128],
                                             in_=ps_tr[:, 0:take * 128]))
                lastv = last
            else:
                qsc.wait(e_t, extra_ve_wait)
                last = scstep(sca.activation(out=dst16[:, r * 128:(r + take) * 128],
                                             in_=ps_tr[:, 0:take * 128], func=AF.Copy))
        return last

    e_A_done = ()
    if "A" in phases:
        esA = ExitStack()
        sb = lambda n, s, d=F32: esA.enter_context(nc.sbuf_tensor(n, s, d))
        psm = lambda n, s, d=F32: esA.enter_context(nc.psum_tensor(n, s, d))
        wq16 = sb("wq16", [128, KC, D], F16)
        wkv16 = sb("wkv16", [128, KC, 1024], F16)
        bqkv_b = sb("bqkv_b", [128, 3072], F32)
        with nc.sbuf_tensor("landA0", [128, 4096], F32) as lA0, \
             nc.sbuf_tensor("landA1", [128, 4096], F32) as lA1:
            lrotA = Rot(mksem, [lA0, lA1], "d_lA")
            stream_cast(wq_in, D, 2048, wq16, 0, lrotA)
            e_lc = stream_cast(wkv_in, D, 1024, wkv16, 0, lrotA)
            bias_bcast(bq_in, 2048, bqkv_b, 0)
            e_bias = bias_bcast(bkv_in, 1024, bqkv_b, 2048)
        qsy.wait(e_lc, brow_consumed[0])

        xrot = Rot(mksem, [sb(f"xa{i}", [128, D]) for i in range(2)], "d_xa")
        csrot = Rot(mksem, [sb(f"cs{i}", [128, 128]) for i in range(2)], "d_cs")
        d_ks = mksem("d_ks")
        d_vs = mksem("d_vs")
        d_qs = mksem("d_qs")
        xc_a = sb("xc_a", [128, D])
        m_a = sb("m_a", [128, 1])
        ss_a = sb("ss_a", [128, 1])
        xn16_a = sb("xn16_a", [128, D], F16)
        xnT_a = sb("xnT_a", [128, D], F16)
        kv16 = sb("kv16", [128, 1024], F16)
        q16_t = sb("q16_t", [128, D], F16)
        msn = sb("msn", [128, 16])
        ropen = sb("ropen", [128, D])
        rope2 = sb("rope2", [128, D])
        sqn = rope2.ap()
        o16 = sb("o16", [128, D], F16)
        kT_sb = sb("kT_sb", [128, 512], F16)
        ps_tr_a = psm("ps_tr_a", [128, 512], F16)
        ps_kv = psm("ps_kv", [128, 1024])
        ps_q = psm("ps_q", [128, 2048])

        def rope_norm(src16, ngrp, ns_b, cs_t, out_f16, src_evt, cs_evt, war_evt):
            w = ngrp * 128
            qve.wait(src_evt, war_evt)
            e1 = vstep(vec.tensor_tensor(out=sqn[:, 0:w], in0=src16[:, 0:w],
                                         in1=src16[:, 0:w], op=ALU.mult))
            qve.wait(e1)
            e2 = vstep(vec.reduce_sum(out=msn[:, 0:ngrp],
                                      in_=sqn[:, 0:w].rearrange("p (g x) -> p g x", g=ngrp),
                                      axis=AX.X))
            qve.wait(e2)
            e3 = vstep(vec.tensor_scalar(out=msn[:, 0:ngrp], in0=msn[:, 0:ngrp],
                                         scalar1=1.0 / 128.0, scalar2=RMS_EPS,
                                         op0=ALU.mult, op1=ALU.add))
            qsc.wait(e3)
            e4 = scstep(sca.activation(out=msn[:, 0:ngrp], in_=msn[:, 0:ngrp], func=AF.Sqrt))
            qve.wait(e4)
            e5 = vstep(vec.reciprocal(out=msn[:, 0:ngrp], in_=msn[:, 0:ngrp]))
            qve.wait(e5)
            r2 = ropen[:, 0:w].rearrange("p (g x) -> p g x", g=ngrp)
            e6 = vstep(vec.tensor_tensor(
                out=r2, in0=src16[:, 0:w].rearrange("p (g x) -> p g x", g=ngrp),
                in1=msn[:, 0:ngrp].unsqueeze(2).broadcast_to([128, ngrp, 128]), op=ALU.mult))
            qve.wait(e6)
            e7 = vstep(vec.tensor_tensor(
                out=r2, in0=r2,
                in1=ns_b[:, :].unsqueeze(1).broadcast_to([128, ngrp, 128]), op=ALU.mult))
            r3 = ropen[:, 0:w].rearrange("p (g h x) -> p g h x", g=ngrp, h=2)
            o3 = rope2[:, 0:w].rearrange("p (g h x) -> p g h x", g=ngrp, h=2)
            f3 = out_f16[:, 0:w].rearrange("p (g h x) -> p g h x", g=ngrp, h=2)
            cosb = cs_t[:, 0:64].unsqueeze(1).broadcast_to([128, ngrp, 64])
            sinb = cs_t[:, 64:128].unsqueeze(1).broadcast_to([128, ngrp, 64])
            qve.wait(e7, cs_evt)
            ea = vstep(vec.tensor_tensor(out=o3[:, :, 0, :], in0=r3[:, :, 0, :], in1=cosb, op=ALU.mult))
            eb = vstep(vec.tensor_tensor(out=o3[:, :, 1, :], in0=r3[:, :, 1, :], in1=sinb, op=ALU.mult))
            qve.wait(ea, eb)
            ec = vstep(vec.tensor_tensor(out=f3[:, :, 0, :], in0=o3[:, :, 0, :],
                                         in1=o3[:, :, 1, :], op=ALU.subtract))
            qve.wait(ec)
            ed = vstep(vec.tensor_tensor(out=o3[:, :, 0, :], in0=r3[:, :, 0, :], in1=sinb, op=ALU.mult))
            ee = vstep(vec.tensor_tensor(out=o3[:, :, 1, :], in0=r3[:, :, 1, :], in1=cosb, op=ALU.mult))
            qve.wait(ed, ee)
            ef = vstep(vec.tensor_tensor(out=f3[:, :, 1, :], in0=o3[:, :, 0, :],
                                         in1=o3[:, :, 1, :], op=ALU.add))
            qve.wait(ef)
            return ef

        e_ks = e_vs = e_qs = None
        prev_kv_evac = None
        prev_kT = None
        prev_xnT_read = None
        for p in range(n_pt):
            run_precast_jobs(10)
            slot = xrot.nxt()
            qsy.wait(xrot.free[slot])
            e_x = inc(xrot.sems[slot], sync.dma_start(
                out=xrot.t[slot][:, :], in_=x_pre[p * 128:(p + 1) * 128, :]), 16)
            cslot = csrot.nxt()
            qsy.wait(csrot.free[cslot])
            inc(csrot.sems[cslot], sync.dma_start(
                out=csrot.t[cslot][:, 0:64], in_=cos_pre[p * 128:(p + 1) * 128, :]), 16)
            e_cs = inc(csrot.sems[cslot], sync.dma_start(
                out=csrot.t[cslot][:, 64:128], in_=sin_pre[p * 128:(p + 1) * 128, :]), 16)
            e_xn, e_relx, _ = layernorm_tile(xrot.t[slot][:, :], xc_a[:, :], xn16_a[:, :],
                                             m_a, ss_a, ropen[:, :], e_x)
            xrot.free[slot] = e_relx
            e_xnT = transpose16(lambda c: xn16_a[:, c * 128:(c + 1) * 128], KC,
                                ps_tr_a, xnT_a, e_xn, extra_pe_wait=prev_xnT_read)
            qpe.wait(e_xnT, e_bias, prev_kv_evac)
            for kc in range(KC):
                e_mm = mm512(ps_kv, xnT_a[:, kc * 128:(kc + 1) * 128],
                             wkv16[:, kc, :], 1024, kc == 0, kc == KC - 1)
            e_mm = pestep(e_mm)
            prev_xnT_read = e_mm
            qve.wait(e_mm, e_vs)
            e_kv = vstep(vec.tensor_tensor(out=kv16[:, :], in0=ps_kv[:, :],
                                           in1=bqkv_b[:, 2048:3072], op=ALU.add))
            prev_kv_evac = e_kv
            e_k = rope_norm(kv16, GROUPS, kns_b, csrot.t[cslot], o16, e_kv, e_cs, prev_kT)
            csrot.free[cslot] = e_k
            e_kT = transpose16(lambda c: o16[:, c * 128:(c + 1) * 128], GROUPS,
                               ps_tr_a, kT_sb, e_k, extra_ve_wait=e_ks)
            prev_kT = pe_now()
            qgp.wait(e_kT)
            e_ks = inc(d_ks, gp.dma_start(
                out=K_sc[:, :, p * 128:(p + 1) * 128],
                in_=kT_sb[:, :].rearrange("p (g x) -> p g x", g=GROUPS)), 16)
            qgp.wait(e_kv)
            e_vs = inc(d_vs, gp.dma_start(out=V_sc[p * 128:(p + 1) * 128, :],
                                          in_=kv16[:, 512:1024]), 16)

        prev_q_evac = None
        for j in range(n_nt):
            run_precast_jobs(10)
            slot = xrot.nxt()
            qsy.wait(xrot.free[slot])
            e_x = inc(xrot.sems[slot], sync.dma_start(
                out=xrot.t[slot][:, :], in_=x_own[j * 128:(j + 1) * 128, :]), 16)
            cslot = csrot.nxt()
            qsy.wait(csrot.free[cslot])
            inc(csrot.sems[cslot], sync.dma_start(
                out=csrot.t[cslot][:, 0:64], in_=cos_own[j * 128:(j + 1) * 128, :]), 16)
            e_cs = inc(csrot.sems[cslot], sync.dma_start(
                out=csrot.t[cslot][:, 64:128], in_=sin_own[j * 128:(j + 1) * 128, :]), 16)
            qve.wait(e_qs)
            e_xn, e_relx, _ = layernorm_tile(xrot.t[slot][:, :], xc_a[:, :], xn16_a[:, :],
                                             m_a, ss_a, ropen[:, :], e_x)
            xrot.free[slot] = e_relx
            e_xnT = transpose16(lambda c: xn16_a[:, c * 128:(c + 1) * 128], KC,
                                ps_tr_a, xnT_a, e_xn, extra_pe_wait=prev_xnT_read)
            qpe.wait(e_xnT, prev_q_evac)
            for kc in range(KC):
                e_mm = mm512(ps_q, xnT_a[:, kc * 128:(kc + 1) * 128],
                             wq16[:, kc, :], 2048, kc == 0, kc == KC - 1)
            e_mm = pestep(e_mm)
            prev_xnT_read = e_mm
            qve.wait(e_mm)
            e_q = vstep(vec.tensor_tensor(out=q16_t[:, :], in0=ps_q[:, :],
                                          in1=bqkv_b[:, 0:2048], op=ALU.add))
            prev_q_evac = e_q
            e_qr = rope_norm(q16_t, HEADS, qns_b, csrot.t[cslot], o16, e_q, e_cs, prev_kT)
            csrot.free[cslot] = e_qr
            e_qT = transpose16(lambda c: o16[:, c * 128:(c + 1) * 128], HEADS,
                               ps_tr_a, xn16_a, e_qr,
                               extra_pe_wait=prev_xnT_read, extra_ve_wait=e_qs)
            prev_kT = pe_now()
            qgp.wait(e_qT)
            e_qs = inc(d_qs, gp.dma_start(
                out=Q_sc[j, :, :, :],
                in_=xn16_a[:, :].rearrange("p (h x) -> p h x", h=HEADS)), 16)

        run_precast_jobs(len(precast_jobs))
        e_A_done = (e_ks, e_vs, e_qs)
        if "KV" in dbg:
            d1 = dbg_out("dbg_K", [128, GROUPS, S], F16)
            d2 = dbg_out("dbg_V", [S, 512], F16)
            d3 = dbg_out("dbg_Q", [NT, 128, HEADS, 128], F16)
            qsy.wait(*e_A_done)
            inc(d_dbg, sync.dma_start(out=d1, in_=K_sc), 16)
            inc(d_dbg, sync.dma_start(out=d2, in_=V_sc), 16)
            ec = inc(d_dbg, sync.dma_start(out=d3, in_=Q_sc), 16)
            qsy.wait(ec)
        esA.close()

    e_B_done = ()
    if "B" in phases:
        esB = ExitStack()
        sb = lambda n, s, d=F32: esB.enter_context(nc.sbuf_tensor(n, s, d))
        psm = lambda n, s, d=F32: esB.enter_context(nc.psum_tensor(n, s, d))
        wo16 = sb("wo16", [128, KC, D], F16)
        bo_b = sb("bo_b", [128, D], F32)
        gate32 = sb("gate32", [128, KC, E], F32)
        d_g32 = mksem("d_g32")
        phase_barrier(*e_A_done)
        with nc.sbuf_tensor("landB0", [128, 4096], F32) as lB0, \
             nc.sbuf_tensor("landB1", [128, 4096], F32) as lB1:
            lrotB = Rot(mksem, [lB0, lB1], "d_lB")
            e_lastB = stream_cast(wo_in, D, 2048, wo16, 0, lrotB)
            e_bo = bias_bcast(bo_in, 2048, bo_b, 0)
        qsy.wait(e_lastB, brow_consumed[0])
        e_g32 = inc(d_g32, sync.dma_start(out=gate32[:, :, :],
                                          in_=gate_in.rearrange("(c p) n -> p c n", p=128)), 16)

        d_qt = mksem("d_qt")
        d_xb = mksem("d_xb")
        d_x1s = mksem("d_x1s")
        d_xns = mksem("d_xns")
        qT_b = sb("qT_b", [128, HEADS, 128], F16)
        x_t = sb("x_t", [128, D])
        logits4 = sb("logits4", [128, 4, S], F16)
        krot = Rot(mksem, [sb(f"kbuf{i}", [128, 512], F16) for i in range(2)], "d_kb")
        vrot = Rot(mksem, [sb(f"vbuf{i}", [128, 4, 128], F16) for i in range(2)], "d_vb")
        ptr_sb = sb("ptr_sb", [128, 1024], F16)
        attn_sb = sb("attn_sb", [128, D], F16)
        attnT_sb = sb("attnT_sb", [128, D], F16)
        negmax = sb("negmax", [128, 4])
        sums = sb("sums", [128, 4])
        recip1 = sb("recip1", [128, 4])
        x1_t = sb("x1_t", [128, D])
        m_b = sb("m_b", [128, 1])
        ss_b = sb("ss_b", [128, 1])
        xn2_16 = sb("xn2_16", [128, D], F16)
        xn2T16 = attnT_sb
        xn2_32 = sb("xn2_32", [128, D])
        xn2T32 = sb("xn2T32", [128, D])
        rlog = sb("rlog", [128, E])
        topv = sb("topv", [128, TOPK])
        rtmp = sb("rtmp", [128, E])
        rsml = sb("rsml", [128, TOPK])
        rw_sb = sb("rw_sb", [128, TOPK])
        U_sb = sb("U_sb", [128, E])
        Ub_sb = sb("Ub_sb", [128, E], F16)

        masks8 = logits4[:, 0, 0:TOPK * E].rearrange("p (a b) -> p a b", a=TOPK)
        slot_sb = sb("slot_sb", [128, E])
        carb_sb = sb("carb_sb", [128, E])
        flat_sb = sb("flat_sb", [128, E])
        carry_row = sb("carry_row", [1, E])
        tokc = sb("tokc", [128, 1])
        trsc = sb("trsc", [128, 2])
        d_sct = mksem("d_sct")
        d_xn2tok = mksem("d_xn2tok")
        e_cr0 = vstep(vec.memset(carry_row[:, :], 0.0))

        ps_log = psm("ps_log", [128, 1024])
        ps_ptr = psm("ps_ptr", [128, 1024], F16)
        ps_att = psm("ps_att", [128, 128])
        ps_out = psm("ps_out", [128, 2048])
        ps_ptr_b = ps_ptr.ap()

        qsy.wait(*e_A_done)
        qpe.wait(e_bo, e_lastB, e_g32)

        e_x1s = None
        e_xns = None
        e_logits_free = [None] * 4
        e_evac_prev = [None, None]
        e_att_read = None
        e_att_prev = None
        e_x1_prev = None
        e_ptr_read = None
        e_qk_last = None
        e_xn232 = None
        e_xtok = None
        e_f3 = None
        e_sct = None
        for j in range(n_nt):
            nchunks = j + 1
            qsy.wait(e_qk_last)
            e_qTl = inc(d_qt, sync.dma_start(out=qT_b[:, :, :], in_=Q_sc[j, :, :, :]), 16)
            qsy.wait(e_xn232)
            e_xl = inc(d_xb, sync.dma_start(out=x_t[:, :],
                                            in_=x_own[j * 128:(j + 1) * 128, :]), 16)
            qve.wait(e_xl, e_bo, e_x1_prev)
            e_xb = vstep(vec.tensor_tensor(out=x_t[:, :], in0=x_t[:, :], in1=bo_b[:, :],
                                           op=ALU.add))

            e_op = None
            for g in range(GROUPS):
                e_log_done = [None] * 4
                for c in range(nchunks):
                    slot = krot.nxt()
                    qsy.wait(krot.free[slot])
                    e_k = inc(krot.sems[slot], sync.dma_start(
                        out=krot.t[slot][:, :], in_=K_sc[:, g, c * 512:(c + 1) * 512]), 16)
                    last_chunk = c == nchunks - 1
                    for h4 in range(4):
                        h = 4 * g + h4
                        half = h4 % 2
                        qpe.wait(e_k, e_qTl, e_evac_prev[half])
                        e_mm = pestep(ten.matmul(ps_log[:, half * 512:(half + 1) * 512],
                                                 qT_b[:, h, :], krot.t[slot][:, :],
                                                 start=True, stop=True))
                        if h4 == 3:
                            krot.free[slot] = e_mm
                        e_qk_last = e_mm
                        if last_chunk:
                            qve.wait(e_mm, e_logits_free[h4])
                            e_ev = vstep(vec.scalar_tensor_tensor(
                                out=logits4[:, h4, c * 512:(c + 1) * 512],
                                in0=ps_log[:, half * 512:(half + 1) * 512],
                                scalar=INV128, in1=mask_sb[:, :],
                                op0=ALU.mult, op1=ALU.add))
                        else:
                            qsc.wait(e_mm, e_logits_free[h4])
                            e_ev = scstep(sca.activation(
                                out=logits4[:, h4, c * 512:(c + 1) * 512],
                                in_=ps_log[:, half * 512:(half + 1) * 512],
                                func=AF.Copy, scale=INV128))
                        e_evac_prev[half] = e_ev
                        e_log_done[h4] = e_ev

                e_e4 = [None] * 4
                e_r4 = [None] * 4
                for h4 in range(4):
                    h = 4 * g + h4
                    qve.wait(e_log_done[h4], e_e4[h4])
                    e_m = vstep(vec.reduce_max(out=negmax[:, h4:h4 + 1],
                                               in_=logits4[:, h4, 0:nchunks * 512],
                                               axis=AX.X, negate=True))
                    qsc.wait(e_m)
                    e_e4[h4] = scstep(sca.activation(out=logits4[:, h4, 0:nchunks * 512],
                                                     in_=logits4[:, h4, 0:nchunks * 512],
                                                     func=AF.Exp, bias=negmax[:, h4:h4 + 1],
                                                     accum_out=sums[:, h4:h4 + 1]))
                for h4 in range(4):
                    h = 4 * g + h4
                    qve.wait(e_e4[h4])
                    e_r4[h4] = vstep(vec.reciprocal(out=recip1[:, h4:h4 + 1],
                                                    in_=sums[:, h4:h4 + 1]))
                for h4 in range(4):
                    h = 4 * g + h4
                    e_e = e_e4[h4]
                    e_r = e_r4[h4]
                    nsc = nchunks * 4
                    e_pt_ev = [None, None]
                    e_ptr_rd = [e_ptr_read, e_ptr_read]
                    e_pv = None
                    for sc in range(0, nsc, 4):
                        take = min(4, nsc - sc)
                        ph = (sc // 4) % 2
                        vslot = vrot.nxt()
                        qsy.wait(vrot.free[vslot])
                        e_v = inc(vrot.sems[vslot], sync.dma_start(
                            out=vrot.t[vslot][:, 0:take, :],
                            in_=V_sc[sc * 128:(sc + take) * 128, g * 128:(g + 1) * 128]
                            .rearrange("(u p) m -> p u m", p=128)), 16)
                        qpe.wait(e_e)
                        if e_pt_ev[ph] is not None:
                            qpe.wait(e_pt_ev[ph])
                        for u in range(take):
                            e_t = pestep(ten.transpose(
                                ps_ptr[:, ph * 512 + u * 128:ph * 512 + (u + 1) * 128],
                                logits4[:, h4, (sc + u) * 128:(sc + u + 1) * 128],
                                ident16[:, :]))
                        if ph == 0:
                            qve.wait(e_t, e_ptr_rd[ph])
                            e_pt_ev[ph] = vstep(vec.tensor_copy(
                                out=ptr_sb[:, 0:take * 128],
                                in_=ps_ptr[:, 0:take * 128]))
                        else:
                            qsc.wait(e_t, e_ptr_rd[ph])
                            e_pt_ev[ph] = scstep(sca.activation(
                                out=ptr_sb[:, 512:512 + take * 128],
                                in_=ps_ptr[:, 512:512 + take * 128], func=AF.Copy))
                        qpe.wait(e_pt_ev[ph], e_v)
                        if sc == 0:
                            qpe.wait(e_att_prev)
                        for u in range(take):
                            e_pv = ten.matmul(ps_att[:, :],
                                              ptr_sb[:, ph * 512 + u * 128:ph * 512 + (u + 1) * 128],
                                              vrot.t[vslot][:, u, :],
                                              start=(sc + u == 0), stop=(sc + u == nsc - 1))
                        e_pv = pestep(e_pv)
                        vrot.free[vslot] = e_pv
                        e_ptr_rd[ph] = e_pv
                    e_ptr_read = e_pv
                    e_logits_free[h4] = pe_now()
                    qve.wait(e_pv, e_r, e_att_read)
                    e_att = vstep(vec.tensor_scalar(out=attn_sb[:, h * 128:(h + 1) * 128],
                                                    in0=ps_att[:, :], scalar1=recip1[:, h4:h4 + 1],
                                                    scalar2=None, op0=ALU.mult))
                    e_att_prev = e_att
                    qpe.wait(e_att)
                    if h == 0:
                        qpe.wait(e_x1_prev)
                    e_trh = pestep(ten.transpose(ps_ptr[:, 0:128],
                                                 attn_sb[:, h * 128:(h + 1) * 128],
                                                 ident16[:, :]))
                    e_att_read = e_trh
                    qve.wait(e_trh, e_xns)
                    e_evh = vstep(vec.tensor_copy(out=attnT_sb[:, h * 128:(h + 1) * 128],
                                                  in_=ps_ptr[:, 0:128]))
                    qpe.wait(e_evh)
                    e_op = mm512(ps_out, attnT_sb[:, h * 128:(h + 1) * 128],
                                 wo16[:, h, :], 2048, h == 0, h == HEADS - 1)
                    e_op = pestep(e_op)
                    e_ptr_read = e_op

            qve.wait(e_op, e_xb)
            e_x1 = vstep(vec.scalar_tensor_tensor(out=x1_t[:, :], in0=ps_out[:, :],
                                                  scalar=1.0, in1=x_t[:, :],
                                                  op0=ALU.mult, op1=ALU.add))
            qgp.wait(e_x1)
            e_x1s = inc(d_x1s, gp.dma_start(out=X1_sc[j * 128:(j + 1) * 128, :],
                                            in_=x1_t[:, :]), 16)
            qve.wait(e_x1s)
            qve.wait(e_xtok)
            e_xn2, _, e_xn232 = layernorm_tile(x1_t[:, :], x_t[:, :], xn2_16[:, :],
                                               m_b, ss_b, xn2T32[:, :], e_x1,
                                               extra32=xn2_32[:, :])
            qgp.wait(e_xn2, e_xtok if j > 0 else None)
            e_xtok = inc(d_xn2tok, gp.dma_start(
                out=XN2_sc[j * 128:(j + 1) * 128, :], in_=xn2_16[:, :]), 16)
            e_xn2T = transpose16(lambda c: xn2_16[:, c * 128:(c + 1) * 128], KC,
                                 ps_ptr_b, xn2T16, e_xn2,
                                 extra_pe_wait=e_ptr_read, extra_ve_wait=e_xns,
                                 ident=identb)
            qgp.wait(e_xn2T)
            e_xns = inc(d_xns, gp.dma_start(
                out=XN2T_sc[j, :, :, :],
                in_=xn2T16[:, :].rearrange("p (c x) -> p c x", c=KC)), 16)
            last32 = None
            for r in range(0, KC, 4):
                qpe.wait(e_xn232, e_x1)
                if last32 is not None:
                    qpe.wait(last32)
                for u in range(4):
                    e_t32 = pestep(ten.transpose(ps_out[:, u * 128:(u + 1) * 128],
                                                 xn2_32[:, (r + u) * 128:(r + u + 1) * 128],
                                                 ident32[:, :]))
                qve.wait(e_t32)
                last32 = vstep(vec.tensor_copy(out=xn2T32[:, r * 128:(r + 4) * 128],
                                               in_=ps_out[:, 0:512]))
            e_x1_prev = last32
            qpe.wait(last32, e_g32, e_att_prev)
            for kc in range(KC):
                e_rm = ten.matmul(ps_att[:, 0:E], xn2T32[:, kc * 128:(kc + 1) * 128],
                                  gate32[:, kc, :], start=(kc == 0), stop=(kc == KC - 1))
            e_rm = pestep(e_rm)
            qve.wait(e_rm)
            e_rl = vstep(vec.tensor_copy(out=rlog[:, :], in_=ps_att[:, 0:E]))
            e_att_prev = e_rl
            qve.wait(e_rl)
            e_tv = vstep(vec.max(out=topv[:, :], in_=rlog[:, :]))
            qve.wait(e_tv)
            e_ng = vstep(vec.tensor_scalar(out=m_b[:, :], in0=topv[:, 0:1], scalar1=-1.0,
                                           scalar2=None, op0=ALU.mult))
            qsc.wait(e_ng)
            e_px = scstep(sca.activation(out=rsml[:, :], in_=topv[:, :], func=AF.Exp,
                                         bias=m_b[:, 0:1], accum_out=ss_b[:, :]))
            qve.wait(e_px)
            e_rr = vstep(vec.reciprocal(out=ss_b[:, :], in_=ss_b[:, :]))
            qve.wait(e_rr)
            e_rw = vstep(vec.tensor_scalar(out=rw_sb[:, :], in0=rsml[:, :],
                                           scalar1=ss_b[:, 0:1], scalar2=None, op0=ALU.mult))
            e_uz = vstep(vec.memset(U_sb[:, :], 0.0))
            for jj in range(TOPK):
                qve.wait(e_uz)
                e_mk = vstep(vec.tensor_scalar(out=masks8[:, jj, :], in0=rlog[:, :],
                                               scalar1=topv[:, jj:jj + 1], scalar2=None,
                                               op0=ALU.is_equal))
                qve.wait(e_mk)
                e_uz = vstep(vec.tensor_tensor(out=U_sb[:, :], in0=U_sb[:, :],
                                               in1=masks8[:, jj, :], op=ALU.add))
            qve.wait(e_uz)
            e_ub = vstep(vec.tensor_copy(out=Ub_sb[:, :], in_=U_sb[:, :]))
            qpe.wait(e_ub, e_att_prev)
            e_trimm = pestep(ten.matmul(ps_att[:, 0:E], tri_sb[:, :], Ub_sb[:, :],
                                        start=True, stop=True))
            qpe.wait(e_evac_prev[1], e_cr0)
            e_cbm = pestep(ten.matmul(ps_log[:, 512:512 + E], ones_row[:, :],
                                      carry_row[:, :], start=True, stop=True))
            qpe.wait(e_evac_prev[0])
            e_csm = pestep(ten.matmul(ps_log[0:1, 0:E], onescol[:, :], Ub_sb[:, :],
                                      start=True, stop=True))
            qve.wait(e_trimm)
            e_sl = vstep(vec.tensor_copy(out=slot_sb[:, :], in_=ps_att[:, 0:E]))
            e_att_prev = e_sl
            qve.wait(e_cbm)
            e_cb = vstep(vec.tensor_copy(out=carb_sb[:, :], in_=ps_log[:, 512:512 + E]))
            e_evac_prev[1] = e_cb
            qve.wait(e_csm, e_cb)
            e_cr0 = vstep(vec.tensor_tensor(out=carry_row[:, :], in0=carry_row[:, :],
                                            in1=ps_log[0:1, 0:E], op=ALU.add))
            e_evac_prev[0] = e_cr0
            qve.wait(e_sl)
            e_sl2 = vstep(vec.tensor_tensor(out=slot_sb[:, :], in0=slot_sb[:, :],
                                            in1=carb_sb[:, :], op=ALU.add))
            qve.wait(e_sl2)
            e_fl = vstep(vec.scalar_tensor_tensor(out=flat_sb[:, :], in0=iota64_sb[:, :],
                                                  scalar=float(CAP), in1=slot_sb[:, :],
                                                  op0=ALU.mult, op1=ALU.add))
            qve.wait(e_fl)
            e_ov = vstep(vec.tensor_scalar(out=slot_sb[:, :], in0=slot_sb[:, :],
                                           scalar1=float(CAP), scalar2=None, op0=ALU.is_ge))
            qve.wait(e_ov)
            e_fl2 = vstep(vec.scalar_tensor_tensor(out=flat_sb[:, :], in0=slot_sb[:, :],
                                                   scalar=1.0e9, in1=flat_sb[:, :],
                                                   op0=ALU.mult, op1=ALU.add))
            qve.wait(e_fl2)
            qve.wait(e_sct)
            e_tc = vstep(vec.tensor_scalar(out=tokc[:, :], in0=pidx_sb[:, :],
                                           scalar1=float(j * 128), scalar2=None, op0=ALU.add))
            for jj in range(TOPK):
                qve.wait(e_fl2)
                e_ei = vstep(vec.tensor_tensor(out=rtmp[:, :], in0=masks8[:, jj, :],
                                               in1=flat_sb[:, :], op=ALU.mult))
                qve.wait(e_ei)
                e_es = vstep(vec.reduce_sum(out=m_b[:, :], in_=rtmp[:, :], axis=AX.X))
                qve.wait(e_es)
                e_f3 = vstep(vec.tensor_copy(out=fidx_sb[:, j, jj:jj + 1], in_=m_b[:, :]))
                qve.wait(e_sct, e_tc)
                e_rc = vstep(vec.tensor_copy(out=trsc[:, 0:1], in_=tokc[:, :]))
                qve.wait(e_rc)
                e_rc = vstep(vec.tensor_copy(out=trsc[:, 1:2], in_=rw_sb[:, jj:jj + 1]))
                qgp.wait(e_f3, e_toki, e_rc, e_sct)
                e_sct = inc(d_sct, gp.indirect_dma_start(
                    out=TR_d, out_offset=IndirectOffsetOnAxis(fidx_sb[:, j, jj:jj + 1], 0),
                    in_=trsc[:, :], in_offset=None,
                    bounds_check=rb_tab, oob_is_err=False), 16)
            e_logits_free[0] = (s_ve, s_ve.v)

        ctrow = xn2T32[0:1, 1800:1864]
        tmpr = xn2T32[0:1, 1864:1928]
        t128 = xn2T32[:, 0:128]
        ctcolf = xn2T32[:, 128:129]
        cumcol = xn2T32[:, 130:131]
        s2col = xn2T32[:, 132:133]
        cmp_f = xn2_32[:, 0:NCT]
        idxt_f = xn2_32[:, 256:256 + NCT]
        ekrow = xn2T32[0:1, 1024:1024 + NCT]
        cumk = xn2T32[0:1, 1216:1216 + NCT]
        bases = xn2T32[0:1, 1408:1408 + NCT]
        over_r = xn2T32[0:1, 1600:1600 + NCT]
        tts = xn2T32[0:1, 1798:1799]
        ctcol16 = sb("ctcol16", [128, 1], F16)
        cumcol16 = sb("cumcol16", [128, 1], F16)
        cmp16 = sb("cmp16", [128, NCT], F16)
        oh16 = sb("oh16", [128, NCT], F16)

        qve.wait(e_cr0)
        ev = vstep(vec.memset(ctrow, 0.0))
        for m in range(16):
            qve.wait(ev)
            ev = vstep(vec.tensor_scalar(out=tmpr, in0=carry_row[:, :],
                                         scalar1=float(128 * m), scalar2=None, op0=ALU.is_gt))
            qve.wait(ev)
            ev = vstep(vec.tensor_tensor(out=ctrow, in0=ctrow,
                                         in1=tmpr, op=ALU.add))
        qve.wait(ev)
        e_z = vstep(vec.memset(t128, 0.0))
        qve.wait(e_z)
        e_cp = vstep(vec.tensor_copy(out=t128[0:1, 0:E], in_=ctrow))
        qpe.wait(e_cp, e_x1_prev, e_att_prev)
        e_tr = pestep(ten.transpose(ps_out[:, 0:128], t128, ident32[:, :]))
        qve.wait(e_tr)
        e_cc = vstep(vec.tensor_copy(out=ctcolf, in_=ps_out[:, 0:1]))
        qve.wait(e_cc)
        e_c16 = vstep(vec.tensor_copy(out=ctcol16[:, :], in_=ctcolf))
        qpe.wait(e_c16)
        e_cm = pestep(ten.matmul(ps_att[0:E, 0:1], tri_sb[0:E, 0:E], ctcol16[0:E, 0:1],
                                 start=True, stop=True))
        qve.wait(e_cm)
        ev = vstep(vec.memset(cumcol, 0.0))
        qve.wait(ev)
        e_cu = vstep(vec.tensor_copy(out=xn2T32[0:E, 130:131], in_=ps_att[0:E, 0:1]))
        qve.wait(e_cu)
        e_cu16 = vstep(vec.tensor_copy(out=cumcol16[:, :], in_=cumcol))
        e_s2 = vstep(vec.tensor_tensor(out=s2col, in0=cumcol,
                                       in1=ctcolf, op=ALU.add))
        qve.wait(e_cu16, e_s2)
        ev = vstep(vec.tensor_scalar(out=cmp_f[0:E, :], in0=iotaB192[0:E, :],
                                     scalar1=xn2T32[0:E, 130:131], scalar2=None, op0=ALU.is_ge))
        qve.wait(ev)
        e_cmp16 = vstep(vec.tensor_copy(out=cmp16[0:E, :], in_=cmp_f[0:E, :]))
        qve.wait(e_cmp16)
        ev = vstep(vec.tensor_scalar(out=cmp_f[0:E, :], in0=iotaB192[0:E, :],
                                     scalar1=xn2T32[0:E, 132:133], scalar2=None, op0=ALU.is_ge))
        qve.wait(ev)
        e_cN16 = vstep(vec.tensor_copy(out=oh16[0:E, :], in_=cmp_f[0:E, :]))
        qve.wait(e_cN16)
        e_oh = vstep(vec.tensor_tensor(out=oh16[0:E, :], in0=cmp16[0:E, :],
                                       in1=oh16[0:E, :], op=ALU.subtract))
        qpe.wait(e_cmp16, e_oh)
        e_ek = pestep(ten.matmul(ps_out[0:1, 0:NCT], onescol[0:E, :], cmp16[0:E, :],
                                 start=True, stop=True))
        e_ck = pestep(ten.matmul(ps_out[0:1, 512:512 + NCT], cumcol16[0:E, :], oh16[0:E, :],
                                 start=True, stop=True))
        qve.wait(e_ek)
        e_ekr = vstep(vec.tensor_scalar(out=ekrow, in0=ps_out[0:1, 0:NCT],
                                        scalar1=1.0, scalar2=None, op0=ALU.subtract))
        qve.wait(e_ck)
        e_ckr = vstep(vec.tensor_copy(out=cumk, in_=ps_out[0:1, 512:512 + NCT]))
        e_tt = vstep(vec.reduce_sum(out=tts, in_=ctrow, axis=AX.X))
        qve.wait(e_tt)
        e_ov = vstep(vec.tensor_scalar(out=over_r, in0=iotaB192[0:1, 0:NCT],
                                       scalar1=tts, scalar2=None, op0=ALU.is_ge))
        qve.wait(e_ckr, e_ov, e_ekr)
        e_of = vstep(vec.tensor_tensor(out=cumk, in0=iotaB192[0:1, 0:NCT],
                                       in1=cumk, op=ALU.subtract))
        qve.wait(e_of)
        ev = vstep(vec.tensor_scalar(out=bases, in0=ekrow,
                                     scalar1=2048.0, scalar2=None, op0=ALU.mult))
        qve.wait(ev)
        ev = vstep(vec.scalar_tensor_tensor(out=bases, in0=cumk, scalar=128.0,
                                            in1=bases, op0=ALU.mult, op1=ALU.add))
        qve.wait(ev)
        e_ba = vstep(vec.scalar_tensor_tensor(out=bases, in0=over_r, scalar=1.0e9,
                                              in1=bases, op0=ALU.mult, op1=ALU.add))
        qpe.wait(e_ba)
        e_bc = pestep(ten.matmul(ps_out[:, 512:512 + NCT], ones_row[:, :], bases,
                                 start=True, stop=True))
        qve.wait(e_bc)
        ev = vstep(vec.tensor_tensor(out=idxt_f, in0=ps_out[:, 512:512 + NCT],
                                     in1=pidx_sb[:, 0:1].broadcast_to([128, NCT]), op=ALU.add))
        qve.wait(ev)
        e_ix1 = vstep(vec.tensor_copy(out=IDXT_u[:, :], in_=idxt_f))
        qve.wait(e_ix1)
        ev = vstep(vec.tensor_scalar(out=bases, in0=ekrow,
                                     scalar1=128.0, scalar2=None, op0=ALU.mult))
        qve.wait(ev)
        e_bd = vstep(vec.scalar_tensor_tensor(out=bases, in0=over_r, scalar=1.0e9,
                                              in1=bases, op0=ALU.mult, op1=ALU.add))
        qpe.wait(e_bd, e_ix1)
        e_bc2 = pestep(ten.matmul(ps_out[:, 512:512 + NCT], ones_row[:, :], bases,
                                  start=True, stop=True))
        qve.wait(e_bc2)
        ev = vstep(vec.tensor_tensor(out=idxt_f, in0=ps_out[:, 512:512 + NCT],
                                     in1=pidx_sb[:, 0:1].broadcast_to([128, NCT]), op=ALU.add))
        qve.wait(ev)
        e_ix2 = vstep(vec.tensor_copy(out=IDXD_u[:, :], in_=idxt_f))

        e_B_done = (e_x1s, e_xns, e_f3, e_sct, e_xtok, e_ix1, e_ix2)
        if "B1" in dbg:
            dX1 = dbg_out("dbg_X1", [NT * 128, D], F32)
            dXN = dbg_out("dbg_XN2T", [NT, 128, KC, 128], F16)
            dT = dbg_out("dbg_TR", [E * CAP, 2], F32)
            dF = dbg_out("dbg_fidx", [128, NT, TOPK], U32)
            dI1 = dbg_out("dbg_IDXT", [128, NCT], U32)
            dI2 = dbg_out("dbg_IDXD", [128, NCT], U32)
            qsy.wait(*e_B_done)
            inc(d_dbg, sync.dma_start(out=dX1, in_=X1_sc), 16)
            inc(d_dbg, sync.dma_start(out=dXN, in_=XN2T_sc), 16)
            inc(d_dbg, sync.dma_start(out=dT, in_=TR_d), 16)
            inc(d_dbg, sync.dma_start(out=dI1, in_=IDXT_u[:, :]), 16)
            inc(d_dbg, sync.dma_start(out=dI2, in_=IDXD_u[:, :]), 16)
            ed = inc(d_dbg, sync.dma_start(out=dF, in_=fidx_sb[:, :, :]), 16)
            qsy.wait(ed)
        esB.close()

    e_C_done = None
    if "C" in phases:
        esC = ExitStack()
        sb = lambda n, s, d=F32: esC.enter_context(nc.sbuf_tensor(n, s, d))
        psm = lambda n, s, d=F32: esC.enter_context(nc.psum_tensor(n, s, d))
        wrot = Rot(mksem, [sb(f"wbuf{i}", [128, 24576], F16) for i in range(2)], "d_wb")
        xrotC = Rot(mksem, [sb(f"xg{i}", [128, D], F16) for i in range(2)], "d_xg")
        d_ys = mksem("d_ys")
        d_trb = mksem("d_trb")
        trall = sb("trall", [128, NCT, 2])
        tokall = sb("tokall", [128, NCT], U32)
        xgT = sb("xgT", [128, D], F16)
        sg_c = sb("sg_c", [128, I])
        intb = sb("intb", [128, I], F16)
        intT = sb("intT", [128, I], F16)
        ps_gu = psm("ps_gu", [128, I2])
        ps_trc = psm("ps_trc", [128, 512], F16)
        ps_y = psm("ps_y", [128, D])

        run_precast_jobs(len(precast_jobs))
        phase_barrier(*(list(e_B_done) + list(e_A_done)), (d_wcs, d_wcs.v) if d_wcs.v else None)

        for k in range(NCT):
            e_trb = inc(d_trb, gp.indirect_dma_start(
                out=trall[:, k, :], out_offset=None,
                in_=TR_d, in_offset=IndirectOffsetOnAxis(IDXT_u[:, k:k + 1], 0),
                bounds_check=rb_tab, oob_is_err=False), 16)
        qve.wait(e_trb)
        e_tokall = vstep(vec.tensor_copy(out=tokall[:, :], in_=trall[:, :, 0]))

        WALL_flat = WALL_sc.rearrange("e p n -> (e p) n")
        yrot = Rot(mksem, [sb(f"ytb{i}", [128, D], F16) for i in range(2)], "d_yt")

        e_w = [None] * NCT
        e_xg = [None] * NCT
        e_ye_k = [None] * NCT

        def issue_wg(k):
            ws = k % 2
            qgp.wait(wrot.free[ws])
            e_w[k] = inc(wrot.sems[ws], gp.indirect_dma_start(
                out=wrot.t[ws][:, :], out_offset=None,
                in_=WALL_flat, in_offset=IndirectOffsetOnAxis(IDXD_u[:, k:k + 1], 0),
                bounds_check=rb_we, oob_is_err=False), 16)

        def issue_xg(k):
            xs = k % 2
            qgp.wait(xrotC.free[xs], e_tokall)
            e_xg[k] = inc(xrotC.sems[xs], gp.indirect_dma_start(
                out=xrotC.t[xs][:, :], out_offset=None,
                in_=XN2_sc, in_offset=IndirectOffsetOnAxis(tokall[:, k:k + 1], 0),
                bounds_check=rb_tok, oob_is_err=False), 16)

        e_ys = None

        def issue_ys(k):
            nonlocal e_ys
            qgp.wait(e_ye_k[k])
            e_ys = inc(yrot.sems[k % 2], gp.indirect_dma_start(
                out=Y_sc, out_offset=IndirectOffsetOnAxis(IDXT_u[:, k:k + 1], 0),
                in_=yrot.t[k % 2][:, :], in_offset=None,
                bounds_check=rb_tab, oob_is_err=False), 16)
            yrot.free[k % 2] = e_ys

        e_sg_prev = None
        e_in_prev = None
        e_itv_prev = None
        e_tT_prev = None
        e_xgT_read = None
        issue_wg(0)
        issue_xg(0)
        for k in range(NCT):
            if k + 1 < NCT:
                issue_xg(k + 1)
                issue_wg(k + 1)
            if k >= 1:
                issue_ys(k - 1)
            ws = k % 2
            xs = k % 2
            qpe.wait(e_itv_prev)
            e_xgT = transpose16(lambda cc: xrotC.t[xs][:, cc * 128:(cc + 1) * 128], KC,
                                ps_trc, xgT, e_xg[k], extra_pe_wait=e_xgT_read,
                                ident=identb)
            xrotC.free[xs] = e_xgT
            qpe.wait(e_xgT, e_w[k], e_sg_prev, e_in_prev)
            for kc in range(KC):
                e_gu = mm512(ps_gu, xgT[:, kc * 128:(kc + 1) * 128],
                             wrot.t[ws][:, kc * 1024:(kc + 1) * 1024], 1024,
                             kc == 0, kc == KC - 1)
            e_gu = pestep(e_gu)
            e_xgT_read = e_gu
            qsc.wait(e_gu, e_in_prev)
            e_sg = scstep(sca.activation(out=sg_c[:, :], in_=ps_gu[:, 0:I], func=AF.Silu))
            e_sg_prev = e_sg
            qve.wait(e_sg, e_tT_prev)
            e_in = vstep(vec.scalar_tensor_tensor(out=intb[:, :], in0=ps_gu[:, I:I2],
                                                  scalar=trall[:, k, 1:2],
                                                  in1=sg_c[:, :],
                                                  op0=ALU.mult, op1=ALU.mult))
            e_in_prev = e_in
            qpe.wait(e_in, e_itv_prev)
            for u in range(4):
                e_t = pestep(ten.transpose(ps_trc[:, u * 128:(u + 1) * 128],
                                           intb[:, u * 128:(u + 1) * 128], identb[:, :]))
            e_tT_prev = e_t
            qve.wait(e_t)
            e_itv = vstep(vec.tensor_copy(out=intT[:, :], in_=ps_trc[:, :]))
            e_itv_prev = e_itv
            qpe.wait(e_itv, e_ye_k[k - 2] if k >= 2 else None)
            for ic in range(4):
                e_dn = mm512(ps_y, intT[:, ic * 128:(ic + 1) * 128],
                             wrot.t[ws][:, 16384 + ic * 2048:16384 + (ic + 1) * 2048],
                             2048, ic == 0, ic == 3)
            e_dn = pestep(e_dn)
            wrot.free[ws] = e_dn
            qsc.wait(e_dn, yrot.free[k % 2])
            e_ye_k[k] = scstep(sca.activation(out=yrot.t[k % 2][:, :], in_=ps_y[:, :],
                                              func=AF.Copy))
        issue_ys(NCT - 1)
        e_C_done = None
        e_C_dones = (yrot.free[0], yrot.free[1])
        esC.close()

    if "D" in phases:
        esD = ExitStack()
        sb = lambda n, s, d=F32: esD.enter_context(nc.sbuf_tensor(n, s, d))
        psm = lambda n, s, d=F32: esD.enter_context(nc.psum_tensor(n, s, d))
        shgu16 = sb("shgu16", [128, KC, 1024], F16)
        shd16 = sb("shd16", [128, 4, 2048], F16)
        g116 = sb("g116", [128, KC, 1], F16)
        phase_barrier(*(list(e_B_done) + list(e_A_done) +
                        (list(e_C_dones) if "C" in phases else [])))
        lrotD = Rot(mksem, [sb(f"landD{i}", [128, 4096], F32) for i in range(2)], "d_lD")
        d_xtd = mksem("d_xtd")
        d_x1d = mksem("d_x1d")
        d_gth = mksem("d_gth")
        stream_cast(shgu_in, D, 1024, shgu16, 0, lrotD)
        for cch in range(2):
            ls = lrotD.nxt()
            qsy.wait(lrotD.free[ls])
            e_l = inc(lrotD.sems[ls], sync.dma_start(
                out=lrotD.t[ls][:, :].rearrange("p (c n) -> p c n", c=2),
                in_=shd_in[cch * 256:(cch + 1) * 256, :].rearrange("(c p) n -> p c n", p=128)), 16)
            qve.wait(e_l)
            e_c = vstep(vec.tensor_copy(
                out=shd16[:, cch * 2:(cch + 1) * 2, :],
                in_=lrotD.t[ls][:, :].rearrange("p (c n) -> p c n", c=2)))
            lrotD.free[ls] = e_c
        ls = lrotD.nxt()
        qsy.wait(lrotD.free[ls])
        e_l = inc(lrotD.sems[ls], sync.dma_start(out=lrotD.t[ls][:, 0:16], in_=g1_in), 16)
        qve.wait(e_l)
        e_shw = vstep(vec.tensor_copy(out=g116[:, :, :],
                                      in_=lrotD.t[ls][:, 0:16].rearrange("p (c n) -> p c n", n=1)))

        xtd = sb("xtd", [128, D], F16)
        x1d = sb("x1d", [128, D])
        acc = sb("acc", [128, D])
        gbuf = sb("gbuf", [128, D])
        sgd = sb("sgd", [128, I])
        intd = sb("intd", [128, I], F16)
        intTd = sb("intTd", [128, I], F16)
        shg_d = sb("shg_d", [128, 1])
        outt = sb("outt", [128, D])
        ps_shgu = psm("ps_shgu", [128, 1024])
        ps_trd = psm("ps_trd", [128, 512], F16)
        ps_shd = psm("ps_shd", [128, 2048])
        ps_g1 = psm("ps_g1", [128, 1])

        if "C" in phases:
            qgp.wait(*e_C_dones)
            qsy.wait(*e_C_dones)
        if e_B_done:
            qgp.wait(e_B_done[2], e_B_done[3])

        prev_out_dma = None
        e_out_prev = None
        e_g_d = None
        e_sg_d = None
        e_int_d = None
        e_itv_d = None
        e_sig_d = None
        e_acc1 = None
        for j in range(n_nt):
            qsy.wait(e_g_d)
            e_xt = inc(d_xtd, sync.dma_start(
                out=xtd[:, :], in_=XN2T_sc[j, :, :, :].rearrange("p c x -> p (c x)")), 16)
            qsy.wait(e_acc1)
            e_x1l = inc(d_x1d, sync.dma_start(out=x1d[:, :],
                                              in_=X1_sc[j * 128:(j + 1) * 128, :]), 16)
            qpe.wait(e_xt, e_shw, e_sg_d, e_int_d)
            for kc in range(KC):
                e_g = mm512(ps_shgu, xtd[:, kc * 128:(kc + 1) * 128],
                            shgu16[:, kc, :], 1024, kc == 0, kc == KC - 1)
            e_g = pestep(e_g)
            qpe.wait(e_sig_d)
            for kc in range(KC):
                e_g1m = ten.matmul(ps_g1[:, :], xtd[:, kc * 128:(kc + 1) * 128],
                                   g116[:, kc, :], start=(kc == 0), stop=(kc == KC - 1))
            e_g1m = pestep(e_g1m)
            e_g_d = e_g1m
            qsc.wait(e_g, e_int_d)
            e_sg = scstep(sca.activation(out=sgd[:, :], in_=ps_shgu[:, 0:I], func=AF.Silu))
            e_sg_d = e_sg
            qve.wait(e_sg, e_itv_d)
            e_int = vstep(vec.tensor_tensor(out=intd[:, :], in0=ps_shgu[:, I:I2],
                                            in1=sgd[:, :], op=ALU.mult))
            e_int_d = e_int
            qpe.wait(e_int, e_itv_d)
            for u in range(4):
                e_t = pestep(ten.transpose(ps_trd[:, u * 128:(u + 1) * 128],
                                           intd[:, u * 128:(u + 1) * 128], identb[:, :]))
            qve.wait(e_t)
            e_itv = vstep(vec.tensor_copy(out=intTd[:, :], in_=ps_trd[:, :]))
            e_itv_d = e_itv
            qpe.wait(e_itv, e_out_prev)
            for ic in range(4):
                e_dn = mm512(ps_shd, intTd[:, ic * 128:(ic + 1) * 128],
                             shd16[:, ic, :], 2048, ic == 0, ic == 3)
            e_dn = pestep(e_dn)
            qsc.wait(e_g1m)
            e_sig = scstep(sca.activation(out=shg_d[:, :], in_=ps_g1[:, :], func=AF.Sigmoid))
            e_sig_d = e_sig

            qve.wait(e_x1l)
            e_acc = vstep(vec.tensor_copy(out=acc[:, :], in_=x1d[:, :]))
            e_acc1 = e_acc
            for jj in range(TOPK):
                qgp.wait(e_acc)
                e_gth = inc(d_gth, gp.indirect_dma_start(
                    out=gbuf[:, :], out_offset=None,
                    in_=Y_sc, in_offset=IndirectOffsetOnAxis(fidx_sb[:, j, jj:jj + 1], 0),
                    bounds_check=rb_tab, oob_is_err=False), 16)
                qve.wait(e_gth, e_acc)
                e_acc = vstep(vec.tensor_tensor(out=acc[:, :], in0=acc[:, :],
                                                in1=gbuf[:, :], op=ALU.add))
            qve.wait(e_acc, e_dn, e_sig, prev_out_dma)
            e_out = vstep(vec.scalar_tensor_tensor(out=outt[:, :], in0=ps_shd[:, :],
                                                   scalar=shg_d[:, 0:1], in1=acc[:, :],
                                                   op0=ALU.mult, op1=ALU.add))
            e_out_prev = e_out
            qsy.wait(e_out)
            prev_out_dma = inc(s_out, sync.dma_start(out=y_out[j * 128:(j + 1) * 128, :],
                                                     in_=outt[:, :]), 16)
        qsy.wait(prev_out_dma)
        esD.close()

    for s in all_sems:
        if s.v:
            qsy.wait((s, s.v))
            qgp.wait((s, s.v))

    return nc, dbg_outs



def _prep_in_maps(inputs):
    x = np.asarray(inputs["L0_attn_x"], dtype=np.float32)
    wq = np.ascontiguousarray(inputs["L0_attn_wq"], dtype=np.float32)
    wkv = np.ascontiguousarray(np.concatenate(
        [np.asarray(inputs["L0_attn_wk"], np.float32),
         np.asarray(inputs["L0_attn_wv"], np.float32)], axis=1))
    wo = np.ascontiguousarray(inputs["L0_attn_wo"], dtype=np.float32)
    bq = np.asarray(inputs["L0_attn_bq"], dtype=np.float32).reshape(1, D)
    bkv = np.concatenate([np.asarray(inputs["L0_attn_bk"], np.float32),
                          np.asarray(inputs["L0_attn_bv"], np.float32)]).reshape(1, 1024)
    bo = np.asarray(inputs["L0_attn_bo"], dtype=np.float32).reshape(1, D)
    qns = np.asarray(inputs["L0_attn_q_norm_scale"], np.float32).reshape(1, 128)
    kns = np.asarray(inputs["L0_attn_k_norm_scale"], np.float32).reshape(1, 128)
    gate_T = np.ascontiguousarray(np.asarray(inputs["L0_ffn_gate_w"], np.float32).T)
    guw_T = np.ascontiguousarray(np.asarray(inputs["L0_ffn_gu_w"], np.float32).transpose(0, 2, 1))
    dpw_T = np.ascontiguousarray(np.asarray(inputs["L0_ffn_dp_w"], np.float32).transpose(0, 2, 1))
    shgu_T = np.ascontiguousarray(
        np.concatenate([np.asarray(inputs["L0_ffn_sh_gate_w"], np.float32),
                        np.asarray(inputs["L0_ffn_sh_up_w"], np.float32)], axis=0).T)
    shd_T = np.ascontiguousarray(np.asarray(inputs["L0_ffn_sh_down_w"], np.float32).T)
    g1_T = np.ascontiguousarray(
        np.asarray(inputs["L0_ffn_sh_gate1_w"], np.float32).reshape(KC, 128).T)

    inv_freq = 1.0 / (10000.0 ** (np.arange(0, HD, 2, dtype=np.float32) / HD))
    pos = np.arange(S, dtype=np.float32)
    freqs = np.outer(pos, inv_freq)
    cos_t = np.ascontiguousarray(np.cos(freqs), dtype=np.float32)
    sin_t = np.ascontiguousarray(np.sin(freqs), dtype=np.float32)

    in_maps = []
    for c in range(8):
        b, r = c // 4, c % 4
        rows = (np.arange(NT)[:, None] * 512 + r * 128 + np.arange(128)[None, :]).ravel()
        mask = np.zeros((128, 512), dtype=np.float32)
        mask[:, r * 128 + 128:] = MASK_NEG / 128.0
        tri = np.where(np.arange(128)[None, :] > np.arange(128)[:, None],
                       MASK_NEG / 128.0, 0.0).astype(np.float32)
        mask[:, r * 128:r * 128 + 128] = tri
        in_maps.append({
            "x_pre": np.ascontiguousarray(x[b]),
            "x_own": np.ascontiguousarray(x[b][rows]),
            "cos_pre": cos_t, "sin_pre": sin_t,
            "cos_own": np.ascontiguousarray(cos_t[rows]),
            "sin_own": np.ascontiguousarray(sin_t[rows]),
            "mask_in": mask,
            "wq_in": wq, "wkv_in": wkv, "wo_in": wo,
            "bq_in": bq, "bkv_in": bkv, "bo_in": bo,
            "qns_in": qns, "kns_in": kns,
            "gate_in": gate_T, "guw_in": guw_T, "dpw_in": dpw_T,
            "shgu_in": shgu_T, "shd_in": shd_T, "g1_in": g1_T,
        })
    return in_maps


def _assemble(results):
    out = np.empty((B, S, D), dtype=np.float32)
    for c in range(8):
        b, r = c // 4, c % 4
        rows = (np.arange(NT)[:, None] * 512 + r * 128 + np.arange(128)[None, :]).ravel()
        out[b][rows] = results[c]["y"]
    return out


def kernel(**inputs):
    nc, _ = build()
    in_maps = _prep_in_maps(inputs)
    res = run_bass_kernel_spmd(nc, in_maps, core_ids=list(range(8)))
    return _assemble(res.results)


if __name__ == "__main__":
    import time
    t0 = time.time()
    nc, _ = build()
    print("build ok", time.time() - t0, "s; instructions:", nc.n_instructions())


# version 19 (final)
# speedup vs baseline: 1.4583x; 1.4182x over previous
import os
os.environ.setdefault("NEURON_SCRATCHPAD_PAGE_SIZE", "1024")

from contextlib import ExitStack

import numpy as np
import ml_dtypes
import concourse.bass as bass
import concourse.mybir as mybir
from concourse.bass_utils import run_bass_kernel_spmd
from concourse.bass_core_rust import IndirectOffsetOnAxis

F32 = mybir.dt.float32
F16 = mybir.dt.float16
U32 = mybir.dt.uint32
AF = mybir.ActivationFunctionType
ALU = mybir.AluOpType
AX = mybir.AxisListType

B, S, D = 2, 8192, 2048
HEADS, GROUPS, HD = 16, 4, 128
NT = 16
PT = S // 128
KC = D // 128
E, I2, I = 64, 1024, 512
SI = 512
TOPK = 8
CAP = 2048
NCT = 192
BF16 = mybir.dt.bfloat16
LN_EPS = 1e-5
RMS_EPS = 1e-6
MASK_NEG = -30000.0
INV128 = 1.0 / 128.0


class Sem:
    __slots__ = ("h", "v", "name")

    def __init__(self, nc, name):
        self.h = nc.alloc_semaphore(name=name)
        self.v = 0
        self.name = name


def inc(sem, ins, n=1):
    ins.then_inc(sem.h, n)
    sem.v += n
    assert sem.v < 2 * 10 ** 9
    return (sem, sem.v)


class Q:
    def __init__(self, eng):
        self.e = eng
        self.seen = {}

    def wait(self, *evts):
        for evt in evts:
            if evt is None:
                continue
            sem, v = evt
            if self.seen.get(sem, 0) < v:
                self.e.wait_ge(sem.h, v)
                self.seen[sem] = v


class Rot:

    def __init__(self, mksem, tiles, name):
        self.t = tiles
        self.n = len(tiles)
        self.sems = [mksem(f"{name}{i}") for i in range(self.n)]
        self.ready = [None] * self.n
        self.free = [None] * self.n
        self.i = -1

    def nxt(self):
        self.i += 1
        return self.i % self.n


def build(phases=("A", "B", "C", "D"), dbg=(), n_pt=PT, n_nt=NT, n_e=E):
    nc = bass.Bass("TRN2", target_bir_lowering=False, debug=False)
    dbg = set(dbg)
    ap = lambda t: t.ap()
    all_sems = []

    def mksem(name):
        s = Sem(nc, name)
        all_sems.append(s)
        return s

    x_pre = ap(nc.dram_tensor("x_pre", [S, D], F32, kind="ExternalInput"))
    x_own = ap(nc.dram_tensor("x_own", [NT * 128, D], F32, kind="ExternalInput"))
    cos_pre = ap(nc.dram_tensor("cos_pre", [S, 64], F32, kind="ExternalInput"))
    sin_pre = ap(nc.dram_tensor("sin_pre", [S, 64], F32, kind="ExternalInput"))
    cos_own = ap(nc.dram_tensor("cos_own", [NT * 128, 64], F32, kind="ExternalInput"))
    sin_own = ap(nc.dram_tensor("sin_own", [NT * 128, 64], F32, kind="ExternalInput"))
    mask_in = ap(nc.dram_tensor("mask_in", [128, 512], F16, kind="ExternalInput"))
    wq_in = ap(nc.dram_tensor("wq_in", [D, D], F32, kind="ExternalInput"))
    wkv_in = ap(nc.dram_tensor("wkv_in", [D, 1024], F32, kind="ExternalInput"))
    wo_in = ap(nc.dram_tensor("wo_in", [D, D], F32, kind="ExternalInput"))
    bq_in = ap(nc.dram_tensor("bq_in", [1, D], F32, kind="ExternalInput"))
    bkv_in = ap(nc.dram_tensor("bkv_in", [1, 1024], F32, kind="ExternalInput"))
    bo_in = ap(nc.dram_tensor("bo_in", [1, D], F32, kind="ExternalInput"))
    qns_in = ap(nc.dram_tensor("qns_in", [1, 128], F32, kind="ExternalInput"))
    kns_in = ap(nc.dram_tensor("kns_in", [1, 128], F32, kind="ExternalInput"))
    gate_in = ap(nc.dram_tensor("gate_in", [D, E], F32, kind="ExternalInput"))
    guw_in = ap(nc.dram_tensor("guw_in", [E, D, I2], F32, kind="ExternalInput"))
    dpw_in = ap(nc.dram_tensor("dpw_in", [E, I, D], F32, kind="ExternalInput"))
    shgu_in = ap(nc.dram_tensor("shgu_in", [D, 1024], F32, kind="ExternalInput"))
    shd_in = ap(nc.dram_tensor("shd_in", [SI, D], F32, kind="ExternalInput"))
    g1_in = ap(nc.dram_tensor("g1_in", [128, KC], F32, kind="ExternalInput"))

    y_out = ap(nc.dram_tensor("y", [NT * 128, D], F32, kind="ExternalOutput"))

    K_sc = ap(nc.dram_tensor("K_sc", [128, GROUPS, S], F16, kind="Internal"))
    V_sc = ap(nc.dram_tensor("V_sc", [S, 512], F16, kind="Internal"))
    Q_sc = ap(nc.dram_tensor("Q_sc", [NT, 128, HEADS, 128], F16, kind="Internal"))
    X1_sc = ap(nc.dram_tensor("X1_sc", [NT * 128, D], F32, kind="Internal"))
    XN2T_sc = ap(nc.dram_tensor("XN2T_sc", [NT, 128, KC, 128], F16, kind="Internal"))
    XN2_sc = ap(nc.dram_tensor("XN2_sc", [NT * 128, D], F16, kind="Internal"))
    Y_sc = ap(nc.dram_tensor("Y_sc", [E * CAP, D], F16, kind="Internal"))
    TR_d = ap(nc.dram_tensor("TR_d", [E * CAP, 2], F32, kind="Internal"))
    WALL_sc = ap(nc.dram_tensor("WALL_sc", [E, 128, 24576], F16, kind="Internal"))

    dbg_outs = {}

    def dbg_out(name, shape, dt=F32):
        dbg_outs[name] = ap(nc.dram_tensor(name, shape, dt, kind="ExternalOutput"))
        return dbg_outs[name]

    ident16_d = nc.inline_tensor(np.eye(128, dtype=np.float16), "ident16")
    ident32_d = nc.inline_tensor(np.eye(128, dtype=np.float32), "ident32")
    ones_row_d = nc.inline_tensor(np.ones((1, 128), dtype=np.float32), "ones_row")
    iota64_d = nc.inline_tensor(
        np.broadcast_to(np.arange(64, dtype=np.float32), (128, 64)).copy(), "iota64")
    pidx_d = nc.inline_tensor(np.arange(128, dtype=np.float32).reshape(128, 1), "pidx")
    identb_d = nc.inline_tensor(np.eye(128, dtype=np.float16), "identb")
    iotaB192_d = nc.inline_tensor(
        np.broadcast_to(np.arange(NCT, dtype=np.float32), (128, NCT)).copy(), "iotaB192")
    tri_d = nc.inline_tensor(np.triu(np.ones((128, 128)), 1).astype(np.float16), "tri")
    onescol_d = nc.inline_tensor(np.ones((128, 1), dtype=np.float16), "onescol")

    s_pe = mksem("s_pe")
    s_ve = mksem("s_ve")
    s_sc = mksem("s_sc")
    d_pre = mksem("d_pre")
    d_brow = mksem("d_brow")
    d_dbg = mksem("d_dbg")
    s_out = mksem("s_out")

    sync, vec, sca, ten, gp = nc.sync, nc.vector, nc.scalar, nc.tensor, nc.gpsimd
    qsy, qve, qsc, qpe, qgp = Q(sync), Q(vec), Q(sca), Q(ten), Q(gp)
    rb_tab = gp.alloc_register("rb_tab")
    gp.reg_mov(rb_tab, E * CAP - 1)
    rb_tok = gp.alloc_register("rb_tok")
    gp.reg_mov(rb_tok, NT * 128 - 1)
    rb_we = gp.alloc_register("rb_we")
    gp.reg_mov(rb_we, E * 128 - 1)

    vstep = lambda ins: inc(s_ve, ins)
    scstep = lambda ins: inc(s_sc, ins)
    pestep = lambda ins: inc(s_pe, ins)
    pe_now = lambda: (s_pe, s_pe.v)

    def phase_barrier(*extras):
        evs = [(s_pe, s_pe.v), (s_ve, s_ve.v), (s_sc, s_sc.v)] + [e for e in extras if e]
        for q in (qsy, qve, qsc, qpe, qgp):
            q.wait(*evs)

    def mm512(ps_t, lhsT_ap, rhs_ap, n, start, stop, step=512):
        for w0 in range(0, n, step):
            ins = ten.matmul(ps_t[:, w0:w0 + step], lhsT_ap, rhs_ap[:, w0:w0 + step],
                             start=start, stop=stop)
        return ins

    ident16 = nc.alloc_sbuf_tensor("ident16_sb", [128, 128], F16)
    ident32 = nc.alloc_sbuf_tensor("ident32_sb", [128, 128], F32)
    ones_row = nc.alloc_sbuf_tensor("ones_row_sb", [1, 128], F32)
    iota64_sb = nc.alloc_sbuf_tensor("iota64_sb", [128, 64], F32)
    pidx_sb = nc.alloc_sbuf_tensor("pidx_sb", [128, 1], F32)
    mask_sb = nc.alloc_sbuf_tensor("mask_sb", [128, 512], F16)
    negc_col = nc.alloc_sbuf_tensor("negc_col", [128, 1], F32)
    qns_b = nc.alloc_sbuf_tensor("qns_b", [128, 128], F32)
    kns_b = nc.alloc_sbuf_tensor("kns_b", [128, 128], F32)
    fidx_sb = nc.alloc_sbuf_tensor("fidx_sb", [128, NT, TOPK], U32)
    IDXT_u = nc.alloc_sbuf_tensor("IDXT_u", [128, NCT], U32)
    IDXD_u = nc.alloc_sbuf_tensor("IDXD_u", [128, NCT], U32)
    identb = nc.alloc_sbuf_tensor("identb_sb", [128, 128], F16)
    iotaB192 = nc.alloc_sbuf_tensor("iotaB192_sb", [128, NCT], F32)
    tri_sb = nc.alloc_sbuf_tensor("tri_sb", [128, 128], F16)
    onescol = nc.alloc_sbuf_tensor("onescol_sb", [128, 1], F16)
    nsrow = nc.alloc_sbuf_tensor("nsrow", [1, 256], F32)

    for dst, src in [(ident16, ident16_d), (ident32, ident32_d), (ones_row, ones_row_d),
                     (iota64_sb, iota64_d), (pidx_sb, pidx_d), (identb, identb_d),
                     (tri_sb, tri_d), (onescol, onescol_d), (iotaB192, iotaB192_d)]:
        inc(d_pre, sync.dma_start(out=dst[:, :], in_=src.ap()), 16)
    inc(d_pre, sync.dma_start(out=mask_sb[:, :], in_=mask_in), 16)
    inc(d_pre, sync.dma_start(out=nsrow[:, 0:128], in_=qns_in), 16)
    e_pre = inc(d_pre, sync.dma_start(out=nsrow[:, 128:256], in_=kns_in), 16)

    with nc.psum_tensor("ps_b0", [128, 256], F32) as ps_b0:
        qpe.wait(e_pre)
        e_p = pestep(ten.matmul(ps_b0[:, :], ones_row[:, :], nsrow[:, :], start=True, stop=True))
        qve.wait(e_p)
        vstep(vec.tensor_copy(out=qns_b[:, :], in_=ps_b0[:, 0:128]))
        vstep(vec.tensor_copy(out=kns_b[:, :], in_=ps_b0[:, 128:256]))

    d_toki = mksem("d_toki")
    with nc.sbuf_tensor("tokinit", [128, 2048], F32) as toki:
        e_mi = vstep(vec.memset(toki[:, :], 1.0e9))
        qgp.wait(e_mi)
        e_toki = inc(d_toki, gp.dma_start(
            out=TR_d.rearrange("(p c) n -> p (c n)", p=128), in_=toki[:, :]), 16)
    qve.wait(e_toki)
    qsy.wait(e_toki)

    d_wcs = mksem("d_wcs")
    precast_jobs = []
    if "C" in phases:
        def mk_pc(src_ap, cc, out_ap):
            def f():
                return inc(d_wcs, gp.dma_start(
                    out=out_ap.rearrange("p (c n) -> p c n", c=cc),
                    in_=src_ap.rearrange("(c p) n -> p c n", p=128)), 16)
            return f

        for e in range(E):
            for cb in range(4):
                precast_jobs.append(mk_pc(guw_in[e, cb * 512:(cb + 1) * 512, :], 4,
                                          WALL_sc[e, :, cb * 4096:(cb + 1) * 4096]))
            for cb in range(2):
                precast_jobs.append(mk_pc(dpw_in[e, cb * 256:(cb + 1) * 256, :], 2,
                                          WALL_sc[e, :, 16384 + cb * 4096:16384 + (cb + 1) * 4096]))

    def run_precast_jobs(n):
        for _ in range(n):
            if precast_jobs:
                precast_jobs.pop(0)()

    brow_consumed = [e_pre]
    bb_prev_evac = [None]
    _bb = [0]

    def bias_bcast(row_ap, width, dst_sb, dst_off):
        _bb[0] += 1
        with nc.sbuf_tensor(f"brow{_bb[0]}", [1, width], F32) as brow, \
             nc.psum_tensor(f"ps_bb{_bb[0]}", [128, width], F32) as ps_bb:
            qsy.wait(brow_consumed[0])
            e_b = inc(d_brow, sync.dma_start(out=brow[:, :], in_=row_ap), 16)
            qpe.wait(e_b, bb_prev_evac[0])
            for w0 in range(0, width, 512):
                e_p = pestep(ten.matmul(ps_bb[:, w0:w0 + 512], ones_row[:, :],
                                        brow[:, w0:w0 + 512], start=True, stop=True))
            brow_consumed[0] = e_p
            qve.wait(e_p)
            e_ev = vstep(vec.tensor_copy(out=dst_sb[:, dst_off:dst_off + width],
                                         in_=ps_bb[:, :]))
            bb_prev_evac[0] = e_ev
            return e_ev

    def stream_cast(src_ap, rows, cols, dst_sb, kc0, lrot):
        n = rows // 128
        e_c = None
        for r in range(0, n, 2):
            take = min(2, n - r)
            ls = lrot.nxt()
            qsy.wait(lrot.free[ls])
            e_l = inc(lrot.sems[ls], sync.dma_start(
                out=lrot.t[ls][:, 0:take * cols].rearrange("p (c n) -> p c n", c=take),
                in_=src_ap[r * 128:(r + take) * 128, :].rearrange("(c p) n -> p c n", p=128)), 16)
            qve.wait(e_l)
            e_c = vstep(vec.tensor_copy(
                out=dst_sb[:, kc0 + r:kc0 + r + take, 0:cols],
                in_=lrot.t[ls][:, 0:take * cols].rearrange("p (c n) -> p c n", c=take)))
            lrot.free[ls] = e_c
        return e_c

    def layernorm_tile(xb, xc, xn16, m, ss, sqscr, evt_x, extra32=None):
        qve.wait(evt_x)
        e1 = vstep(vec.reduce_sum(out=m[:, :], in_=xb, axis=AX.X))
        qve.wait(e1)
        e2 = vstep(vec.tensor_scalar(out=m[:, :], in0=m[:, :], scalar1=1.0 / 2048.0,
                                     scalar2=None, op0=ALU.mult))
        qve.wait(e2)
        e3 = vstep(vec.tensor_scalar(out=xc, in0=xb, scalar1=m[:, 0:1],
                                     scalar2=None, op0=ALU.subtract))
        qsc.wait(e3)
        e4 = scstep(sca.activation(out=sqscr, in_=xc, func=AF.Square, accum_out=ss[:, :]))
        qve.wait(e4)
        e5 = vstep(vec.tensor_scalar(out=ss[:, :], in0=ss[:, :], scalar1=1.0 / 2048.0,
                                     scalar2=LN_EPS, op0=ALU.mult, op1=ALU.add))
        qsc.wait(e5)
        e6 = scstep(sca.activation(out=ss[:, :], in_=ss[:, :], func=AF.Sqrt))
        qve.wait(e6)
        e7 = vstep(vec.reciprocal(out=ss[:, :], in_=ss[:, :]))
        qve.wait(e7)
        e8 = vstep(vec.tensor_scalar(out=xn16, in0=xc, scalar1=ss[:, 0:1],
                                     scalar2=None, op0=ALU.mult))
        e9 = None
        if extra32 is not None:
            e9 = vstep(vec.tensor_scalar(out=extra32, in0=xc, scalar1=ss[:, 0:1],
                                         scalar2=None, op0=ALU.mult))
        return e8, e3, e9

    def transpose16(src_col, n_chunks, ps_tr, dst16, src_evt,
                    extra_pe_wait=None, extra_ve_wait=None, ident=None):
        last = None
        lastv = None
        for r in range(0, n_chunks, 4):
            take = min(4, n_chunks - r)
            qpe.wait(src_evt, extra_pe_wait)
            if last is not None:
                qpe.wait(last)
            for u in range(take):
                e_t = pestep(ten.transpose(ps_tr[:, u * 128:(u + 1) * 128],
                                           src_col(r + u),
                                           (ident if ident is not None else ident16)[:, :]))
            if (r // 4) % 2 == 0:
                qve.wait(e_t, extra_ve_wait)
                last = vstep(vec.tensor_copy(out=dst16[:, r * 128:(r + take) * 128],
                                             in_=ps_tr[:, 0:take * 128]))
                lastv = last
            else:
                qsc.wait(e_t, extra_ve_wait)
                last = scstep(sca.activation(out=dst16[:, r * 128:(r + take) * 128],
                                             in_=ps_tr[:, 0:take * 128], func=AF.Copy))
        return last

    e_A_done = ()
    if "A" in phases:
        esA = ExitStack()
        sb = lambda n, s, d=F32: esA.enter_context(nc.sbuf_tensor(n, s, d))
        psm = lambda n, s, d=F32: esA.enter_context(nc.psum_tensor(n, s, d))
        wq16 = sb("wq16", [128, KC, D], F16)
        wkv16 = sb("wkv16", [128, KC, 1024], F16)
        bqkv_b = sb("bqkv_b", [128, 3072], F32)
        with nc.sbuf_tensor("landA0", [128, 4096], F32) as lA0, \
             nc.sbuf_tensor("landA1", [128, 4096], F32) as lA1:
            lrotA = Rot(mksem, [lA0, lA1], "d_lA")
            stream_cast(wq_in, D, 2048, wq16, 0, lrotA)
            e_lc = stream_cast(wkv_in, D, 1024, wkv16, 0, lrotA)
            bias_bcast(bq_in, 2048, bqkv_b, 0)
            e_bias = bias_bcast(bkv_in, 1024, bqkv_b, 2048)
        qsy.wait(e_lc, brow_consumed[0])

        xrot = Rot(mksem, [sb(f"xa{i}", [128, D]) for i in range(2)], "d_xa")
        csrot = Rot(mksem, [sb(f"cs{i}", [128, 128]) for i in range(2)], "d_cs")
        d_ks = mksem("d_ks")
        d_vs = mksem("d_vs")
        d_qs = mksem("d_qs")
        xc_a = sb("xc_a", [128, D])
        m_a = sb("m_a", [128, 1])
        ss_a = sb("ss_a", [128, 1])
        xn16_a = sb("xn16_a", [128, D], F16)
        xnT_a = sb("xnT_a", [128, D], F16)
        kv16 = sb("kv16", [128, 1024], F16)
        q16_t = sb("q16_t", [128, D], F16)
        msn = sb("msn", [128, 16])
        ropen = sb("ropen", [128, D])
        rope2 = sb("rope2", [128, D])
        sqn = rope2.ap()
        kqt = sb("kqt", [128, 16])
        kqacc = sb("kqacc", [128, 20])
        vstep(vec.memset(kqacc[:, :], 0.0))
        o16 = sb("o16", [128, D], F16)
        kT_sb = sb("kT_sb", [128, 512], F16)
        ps_tr_a = psm("ps_tr_a", [128, 512], F16)
        ps_kv = psm("ps_kv", [128, 1024])
        ps_q = psm("ps_q", [128, 2048])

        def rope_norm(src16, ngrp, ns_b, cs_t, out_f16, src_evt, cs_evt, war_evt):
            w = ngrp * 128
            qve.wait(src_evt, war_evt)
            e1 = vstep(vec.tensor_tensor(out=sqn[:, 0:w], in0=src16[:, 0:w],
                                         in1=src16[:, 0:w], op=ALU.mult))
            qve.wait(e1)
            e2 = vstep(vec.reduce_sum(out=msn[:, 0:ngrp],
                                      in_=sqn[:, 0:w].rearrange("p (g x) -> p g x", g=ngrp),
                                      axis=AX.X))
            qve.wait(e2)
            e3 = vstep(vec.tensor_scalar(out=msn[:, 0:ngrp], in0=msn[:, 0:ngrp],
                                         scalar1=1.0 / 128.0, scalar2=RMS_EPS,
                                         op0=ALU.mult, op1=ALU.add))
            qsc.wait(e3)
            e4 = scstep(sca.activation(out=msn[:, 0:ngrp], in_=msn[:, 0:ngrp], func=AF.Sqrt))
            qve.wait(e4)
            e5 = vstep(vec.reciprocal(out=msn[:, 0:ngrp], in_=msn[:, 0:ngrp]))
            qve.wait(e5)
            r2 = ropen[:, 0:w].rearrange("p (g x) -> p g x", g=ngrp)
            e6 = vstep(vec.tensor_tensor(
                out=r2, in0=src16[:, 0:w].rearrange("p (g x) -> p g x", g=ngrp),
                in1=msn[:, 0:ngrp].unsqueeze(2).broadcast_to([128, ngrp, 128]), op=ALU.mult))
            qve.wait(e6)
            e7 = vstep(vec.tensor_tensor(
                out=r2, in0=r2,
                in1=ns_b[:, :].unsqueeze(1).broadcast_to([128, ngrp, 128]), op=ALU.mult))
            r3 = ropen[:, 0:w].rearrange("p (g h x) -> p g h x", g=ngrp, h=2)
            o3 = rope2[:, 0:w].rearrange("p (g h x) -> p g h x", g=ngrp, h=2)
            f3 = out_f16[:, 0:w].rearrange("p (g h x) -> p g h x", g=ngrp, h=2)
            cosb = cs_t[:, 0:64].unsqueeze(1).broadcast_to([128, ngrp, 64])
            sinb = cs_t[:, 64:128].unsqueeze(1).broadcast_to([128, ngrp, 64])
            qve.wait(e7, cs_evt)
            ea = vstep(vec.tensor_tensor(out=o3[:, :, 0, :], in0=r3[:, :, 0, :], in1=cosb, op=ALU.mult))
            eb = vstep(vec.tensor_tensor(out=o3[:, :, 1, :], in0=r3[:, :, 1, :], in1=sinb, op=ALU.mult))
            qve.wait(ea, eb)
            ec = vstep(vec.tensor_tensor(out=f3[:, :, 0, :], in0=o3[:, :, 0, :],
                                         in1=o3[:, :, 1, :], op=ALU.subtract))
            qve.wait(ec)
            ed = vstep(vec.tensor_tensor(out=o3[:, :, 0, :], in0=r3[:, :, 0, :], in1=sinb, op=ALU.mult))
            ee = vstep(vec.tensor_tensor(out=o3[:, :, 1, :], in0=r3[:, :, 1, :], in1=cosb, op=ALU.mult))
            qve.wait(ed, ee)
            ef = vstep(vec.tensor_tensor(out=f3[:, :, 1, :], in0=o3[:, :, 0, :],
                                         in1=o3[:, :, 1, :], op=ALU.add))
            qve.wait(ef)
            return ef

        e_ks = e_vs = e_qs = None
        prev_kv_evac = None
        prev_kT = None
        prev_xnT_read = None
        for p in range(n_pt):
            run_precast_jobs(10)
            slot = xrot.nxt()
            qsy.wait(xrot.free[slot])
            e_x = inc(xrot.sems[slot], sync.dma_start(
                out=xrot.t[slot][:, :], in_=x_pre[p * 128:(p + 1) * 128, :]), 16)
            cslot = csrot.nxt()
            qsy.wait(csrot.free[cslot])
            inc(csrot.sems[cslot], sync.dma_start(
                out=csrot.t[cslot][:, 0:64], in_=cos_pre[p * 128:(p + 1) * 128, :]), 16)
            e_cs = inc(csrot.sems[cslot], sync.dma_start(
                out=csrot.t[cslot][:, 64:128], in_=sin_pre[p * 128:(p + 1) * 128, :]), 16)
            e_xn, e_relx, _ = layernorm_tile(xrot.t[slot][:, :], xc_a[:, :], xn16_a[:, :],
                                             m_a, ss_a, ropen[:, :], e_x)
            xrot.free[slot] = e_relx
            e_xnT = transpose16(lambda c: xn16_a[:, c * 128:(c + 1) * 128], KC,
                                ps_tr_a, xnT_a, e_xn, extra_pe_wait=prev_xnT_read)
            qpe.wait(e_xnT, e_bias, prev_kv_evac)
            for kc in range(KC):
                e_mm = mm512(ps_kv, xnT_a[:, kc * 128:(kc + 1) * 128],
                             wkv16[:, kc, :], 1024, kc == 0, kc == KC - 1)
            e_mm = pestep(e_mm)
            prev_xnT_read = e_mm
            qve.wait(e_mm, e_vs)
            e_kv = vstep(vec.tensor_tensor(out=kv16[:, :], in0=ps_kv[:, :],
                                           in1=bqkv_b[:, 2048:3072], op=ALU.add))
            prev_kv_evac = e_kv
            e_k = rope_norm(kv16, GROUPS, kns_b, csrot.t[cslot], o16, e_kv, e_cs, prev_kT)
            csrot.free[cslot] = e_k
            e_sq1 = vstep(vec.tensor_tensor(out=rope2[:, 0:512], in0=o16[:, 0:512],
                                            in1=o16[:, 0:512], op=ALU.mult))
            qve.wait(e_sq1)
            e_sq2 = vstep(vec.reduce_sum(out=kqt[:, 0:4],
                                         in_=rope2[:, 0:512].rearrange("p (g x) -> p g x", g=4),
                                         axis=AX.X))
            qve.wait(e_sq2)
            vstep(vec.tensor_max(out=kqacc[:, 0:4], in0=kqacc[:, 0:4], in1=kqt[:, 0:4]))
            e_kT = transpose16(lambda c: o16[:, c * 128:(c + 1) * 128], GROUPS,
                               ps_tr_a, kT_sb, e_k, extra_ve_wait=e_ks)
            prev_kT = pe_now()
            qgp.wait(e_kT)
            e_ks = inc(d_ks, gp.dma_start(
                out=K_sc[:, :, p * 128:(p + 1) * 128],
                in_=kT_sb[:, :].rearrange("p (g x) -> p g x", g=GROUPS)), 16)
            qgp.wait(e_kv)
            e_vs = inc(d_vs, gp.dma_start(out=V_sc[p * 128:(p + 1) * 128, :],
                                          in_=kv16[:, 512:1024]), 16)

        prev_q_evac = None
        for j in range(n_nt):
            run_precast_jobs(10)
            slot = xrot.nxt()
            qsy.wait(xrot.free[slot])
            e_x = inc(xrot.sems[slot], sync.dma_start(
                out=xrot.t[slot][:, :], in_=x_own[j * 128:(j + 1) * 128, :]), 16)
            cslot = csrot.nxt()
            qsy.wait(csrot.free[cslot])
            inc(csrot.sems[cslot], sync.dma_start(
                out=csrot.t[cslot][:, 0:64], in_=cos_own[j * 128:(j + 1) * 128, :]), 16)
            e_cs = inc(csrot.sems[cslot], sync.dma_start(
                out=csrot.t[cslot][:, 64:128], in_=sin_own[j * 128:(j + 1) * 128, :]), 16)
            qve.wait(e_qs)
            e_xn, e_relx, _ = layernorm_tile(xrot.t[slot][:, :], xc_a[:, :], xn16_a[:, :],
                                             m_a, ss_a, ropen[:, :], e_x)
            xrot.free[slot] = e_relx
            e_xnT = transpose16(lambda c: xn16_a[:, c * 128:(c + 1) * 128], KC,
                                ps_tr_a, xnT_a, e_xn, extra_pe_wait=prev_xnT_read)
            qpe.wait(e_xnT, prev_q_evac)
            for kc in range(KC):
                e_mm = mm512(ps_q, xnT_a[:, kc * 128:(kc + 1) * 128],
                             wq16[:, kc, :], 2048, kc == 0, kc == KC - 1)
            e_mm = pestep(e_mm)
            prev_xnT_read = e_mm
            qve.wait(e_mm)
            e_q = vstep(vec.tensor_tensor(out=q16_t[:, :], in0=ps_q[:, :],
                                          in1=bqkv_b[:, 0:2048], op=ALU.add))
            prev_q_evac = e_q
            e_qr = rope_norm(q16_t, HEADS, qns_b, csrot.t[cslot], o16, e_q, e_cs, prev_kT)
            csrot.free[cslot] = e_qr
            e_sq1 = vstep(vec.tensor_tensor(out=rope2[:, :], in0=o16[:, :],
                                            in1=o16[:, :], op=ALU.mult))
            qve.wait(e_sq1)
            e_sq2 = vstep(vec.reduce_sum(out=kqt[:, 0:16],
                                         in_=rope2[:, :].rearrange("p (g x) -> p g x", g=16),
                                         axis=AX.X))
            qve.wait(e_sq2)
            vstep(vec.tensor_max(out=kqacc[:, 4:20], in0=kqacc[:, 4:20], in1=kqt[:, 0:16]))
            e_qT = transpose16(lambda c: o16[:, c * 128:(c + 1) * 128], HEADS,
                               ps_tr_a, xn16_a, e_qr,
                               extra_pe_wait=prev_xnT_read, extra_ve_wait=e_qs)
            prev_kT = pe_now()
            qgp.wait(e_qT)
            e_qs = inc(d_qs, gp.dma_start(
                out=Q_sc[j, :, :, :],
                in_=xn16_a[:, :].rearrange("p (h x) -> p h x", h=HEADS)), 16)

        run_precast_jobs(len(precast_jobs))

        e_m1 = vstep(vec.reduce_max(out=kqt[:, 0:1], in_=kqacc[:, 0:4], axis=AX.X))
        e_m2 = vstep(vec.reduce_max(out=kqt[:, 1:2], in_=kqacc[:, 4:20], axis=AX.X))
        e_z = vstep(vec.memset(ropen[:, 0:128], 0.0))
        qve.wait(e_m1, e_m2, e_z)
        e_c1 = vstep(vec.tensor_copy(out=ropen[:, 0:2], in_=kqt[:, 0:2]))
        qpe.wait(e_c1, prev_q_evac)
        e_t1 = pestep(ten.transpose(ps_q[:, 0:128], ropen[:, 0:128], ident32[:, :]))
        qve.wait(e_t1)
        e_r1 = vstep(vec.tensor_copy(out=rope2[:, 0:128], in_=ps_q[:, 0:128]))
        qve.wait(e_r1)
        e_r2 = vstep(vec.reduce_max(out=rope2[:, 200:201], in_=rope2[:, 0:128], axis=AX.X))
        e_z2 = vstep(vec.memset(rope2[:, 256:384], 0.0))
        qve.wait(e_r2, e_z2)
        e_c2 = vstep(vec.tensor_copy(out=rope2[:, 256:257], in_=rope2[:, 200:201]))
        qpe.wait(e_c2)
        e_t2 = pestep(ten.transpose(ps_q[:, 128:256], rope2[:, 256:384], ident32[:, :]))
        qve.wait(e_t2)
        e_r3 = vstep(vec.tensor_copy(out=rope2[:, 384:512], in_=ps_q[:, 128:256]))
        qve.wait(e_r3)
        e_pr = vstep(vec.tensor_tensor(out=rope2[0:1, 400:401], in0=rope2[0:1, 384:385],
                                       in1=rope2[0:1, 385:386], op=ALU.mult))
        qsc.wait(e_pr)
        e_sq = scstep(sca.activation(out=rope2[0:1, 402:403], in_=rope2[0:1, 400:401],
                                     func=AF.Sqrt))
        qve.wait(e_sq)
        e_ng = vstep(vec.tensor_scalar(out=rope2[0:1, 404:405], in0=rope2[0:1, 402:403],
                                       scalar1=-(INV128 * 1.05), scalar2=None, op0=ALU.mult))
        qpe.wait(e_ng)
        e_bc = pestep(ten.matmul(ps_q[:, 256:257], ones_row[:, :], rope2[0:1, 404:405],
                                 start=True, stop=True))
        qve.wait(e_bc)
        e_negc = vstep(vec.tensor_copy(out=negc_col[:, :], in_=ps_q[:, 256:257]))

        e_A_done = (e_ks, e_vs, e_qs, e_negc)
        if "KV" in dbg:
            d1 = dbg_out("dbg_K", [128, GROUPS, S], F16)
            d2 = dbg_out("dbg_V", [S, 512], F16)
            d3 = dbg_out("dbg_Q", [NT, 128, HEADS, 128], F16)
            qsy.wait(*e_A_done)
            inc(d_dbg, sync.dma_start(out=d1, in_=K_sc), 16)
            inc(d_dbg, sync.dma_start(out=d2, in_=V_sc), 16)
            ec = inc(d_dbg, sync.dma_start(out=d3, in_=Q_sc), 16)
            qsy.wait(ec)
        esA.close()

    e_B_done = ()
    if "B" in phases:
        esB = ExitStack()
        sb = lambda n, s, d=F32: esB.enter_context(nc.sbuf_tensor(n, s, d))
        psm = lambda n, s, d=F32: esB.enter_context(nc.psum_tensor(n, s, d))
        wo16 = sb("wo16", [128, KC, D], F16)
        bo_b = sb("bo_b", [128, D], F32)
        gate32 = sb("gate32", [128, KC, E], F32)
        d_g32 = mksem("d_g32")
        phase_barrier(*e_A_done)
        with nc.sbuf_tensor("landB0", [128, 4096], F32) as lB0, \
             nc.sbuf_tensor("landB1", [128, 4096], F32) as lB1:
            lrotB = Rot(mksem, [lB0, lB1], "d_lB")
            e_lastB = stream_cast(wo_in, D, 2048, wo16, 0, lrotB)
            e_bo = bias_bcast(bo_in, 2048, bo_b, 0)
        qsy.wait(e_lastB, brow_consumed[0])
        e_g32 = inc(d_g32, sync.dma_start(out=gate32[:, :, :],
                                          in_=gate_in.rearrange("(c p) n -> p c n", p=128)), 16)

        d_qt = mksem("d_qt")
        d_xb = mksem("d_xb")
        d_x1s = mksem("d_x1s")
        d_xns = mksem("d_xns")
        qT_b = sb("qT_b", [128, HEADS, 128], F16)
        x_t = sb("x_t", [128, D])
        krot = Rot(mksem, [sb(f"kbuf{i}", [128, 512], F16) for i in range(3)], "d_kb")
        vrot = Rot(mksem, [sb(f"vbuf{i}", [128, 4, 132], F16) for i in range(3)], "d_vb")
        qve.wait(e_lastB, e_bo)
        for _vi in range(3):
            vrot.free[_vi] = vstep(vec.memset(vrot.t[_vi][:, :, 128:132], 1.0))
        ptr_sb = sb("ptr_sb", [128, 2, 512], F16)
        rmask_sb = sb("rmask_sb", [128, TOPK * E], F16)
        attn_sb = sb("attn_sb", [128, D], F16)
        attnT_sb = sb("attnT_sb", [128, D], F16)
        recip1 = sb("recip1", [128, 4])
        x1_t = sb("x1_t", [128, D])
        m_b = sb("m_b", [128, 1])
        ss_b = sb("ss_b", [128, 1])
        xn2_16 = sb("xn2_16", [128, D], F16)
        xn2T16 = attnT_sb
        xn2_32 = sb("xn2_32", [128, D])
        xn2T32 = sb("xn2T32", [128, D])
        rlog = sb("rlog", [128, E])
        topv = sb("topv", [128, TOPK])
        rtmp = sb("rtmp", [128, E])
        rsml = sb("rsml", [128, TOPK])
        rw_sb = sb("rw_sb", [128, TOPK])
        U_sb = sb("U_sb", [128, E])
        Ub_sb = sb("Ub_sb", [128, E], F16)

        masks8 = rmask_sb[:, :].rearrange("p (a b) -> p a b", a=TOPK)
        slot_sb = sb("slot_sb", [128, E])
        carb_sb = sb("carb_sb", [128, E])
        flat_sb = sb("flat_sb", [128, E])
        carry_row = sb("carry_row", [1, E])
        tokc = sb("tokc", [128, 1])
        trsc = sb("trsc", [128, 2])
        d_sct = mksem("d_sct")
        d_xn2tok = mksem("d_xn2tok")
        e_cr0 = vstep(vec.memset(carry_row[:, :], 0.0))

        ps_qkT = psm("ps_qkT", [128, 1024])
        ps_pv0 = psm("ps_pv0", [128, 512])
        ps_pv1 = psm("ps_pv1", [128, 512])
        ps_out = psm("ps_out", [128, 2048])
        ps_qkT_f16 = ps_qkT.ap().bitcast(F16)
        ps_pvs = [ps_pv0.ap(), ps_pv1.ap()]
        ps_pvs_f16 = [ps_pv0.ap().bitcast(F16), ps_pv1.ap().bitcast(F16)]
        ps_ptr_b = ps_qkT_f16

        qsy.wait(*e_A_done)
        qpe.wait(e_bo, e_lastB, e_g32)

        e_x1s = None
        e_xns = None
        e_exp_half = [None, None]
        e_ptr_free = [None, None]
        e_pvevac = [None, None]
        e_attT_evac = [None, None]
        e_qT_read = None
        e_x1_prev = None
        e_xn232 = None
        e_xtok = None
        e_f3 = None
        e_sct = None
        for j in range(n_nt):
            nchunks = j + 1
            qsy.wait(e_qT_read)
            e_qTl = inc(d_qt, sync.dma_start(out=qT_b[:, :, :], in_=Q_sc[j, :, :, :]), 16)
            qsy.wait(e_xn232)
            e_xl = inc(d_xb, sync.dma_start(out=x_t[:, :],
                                            in_=x_own[j * 128:(j + 1) * 128, :]), 16)
            qve.wait(e_xl, e_bo, e_x1_prev)
            e_xb = vstep(vec.tensor_tensor(out=x_t[:, :], in0=x_t[:, :], in1=bo_b[:, :],
                                           op=ALU.add))

            e_op = None
            for g in range(GROUPS):
                for hp in range(2):
                    kvmap = {}

                    def load_kv(c):
                        ks = krot.nxt()
                        qsy.wait(krot.free[ks])
                        e_k = inc(krot.sems[ks], sync.dma_start(
                            out=krot.t[ks][:, :], in_=K_sc[:, g, c * 512:(c + 1) * 512]), 16)
                        vs = vrot.nxt()
                        qsy.wait(vrot.free[vs])
                        e_v = inc(vrot.sems[vs], sync.dma_start(
                            out=vrot.t[vs][:, :, 0:128],
                            in_=V_sc[c * 512:(c + 1) * 512, g * 128:(g + 1) * 128]
                            .rearrange("(u p) m -> p u m", p=128)), 16)
                        kvmap[c] = (ks, vs, e_k, e_v)

                    def emit_pv(c, t, pslot, vs, e_p, e_v):
                        qpe.wait(e_p, e_v)
                        if c == 0:
                            qpe.wait(e_pvevac[t])
                        for u in range(4):
                            e_pv = ten.matmul(ps_pvs[t][:, 0:129],
                                              ptr_sb[:, pslot, u * 128:(u + 1) * 128],
                                              vrot.t[vs][:, u, 0:129],
                                              start=(c == 0 and u == 0),
                                              stop=(c == nchunks - 1 and u == 3),
                                              skip_group_check=True)
                        e_pv = pestep(e_pv)
                        e_ptr_free[pslot] = e_pv
                        if t == 1:
                            vrot.free[vs] = e_pv
                        return e_pv

                    load_kv(0)
                    pend = None
                    e_pv_t = [None, None]
                    for i in range(nchunks * 2):
                        c, t = i // 2, i % 2
                        if t == 0 and c + 1 < nchunks:
                            load_kv(c + 1)
                        ks, vs, e_k, e_v = kvmap[c]
                        half = i % 2
                        h = 4 * g + 2 * hp + t
                        qpe.wait(e_k, e_qTl, e_exp_half[half])
                        for u in range(4):
                            e_qk = ten.matmul(
                                ps_qkT[:, half * 512 + u * 128:half * 512 + (u + 1) * 128],
                                krot.t[ks][:, u * 128:(u + 1) * 128], qT_b[:, h, :],
                                start=True, stop=True)
                        e_qk = pestep(e_qk)
                        if t == 1:
                            krot.free[ks] = e_qk
                        e_qT_read = e_qk
                        pslot = i % 2
                        qsc.wait(e_qk, e_ptr_free[pslot])
                        e_exp = scstep(sca.activation(
                            out=ptr_sb[:, pslot, :],
                            in_=ps_qkT[:, half * 512:half * 512 + 512],
                            func=AF.Exp, scale=INV128, bias=negc_col[:, 0:1]))
                        e_exp_half[half] = e_exp
                        e_p = e_exp
                        if c == nchunks - 1:
                            qve.wait(e_exp)
                            e_p = vstep(vec.tensor_tensor(out=ptr_sb[:, pslot, :],
                                                          in0=ptr_sb[:, pslot, :],
                                                          in1=mask_sb[:, :], op=ALU.mult))
                        if pend is not None:
                            e_pv_t[pend[1]] = emit_pv(*pend)
                        pend = (c, t, pslot, vs, e_p, e_v)
                    e_pv_t[pend[1]] = emit_pv(*pend)

                    for t in range(2):
                        h4 = 2 * hp + t
                        h = 4 * g + h4
                        qve.wait(e_pv_t[t])
                        e_rc = vstep(vec.reciprocal(
                            out=recip1[:, h4:h4 + 1],
                            in_=ps_pvs[t][:, 128:129]))
                        qve.wait(e_rc)
                        e_att = vstep(vec.tensor_scalar(
                            out=attn_sb[:, h * 128:(h + 1) * 128],
                            in0=ps_pvs[t][:, 0:128],
                            scalar1=recip1[:, h4:h4 + 1], scalar2=None, op0=ALU.mult))
                        e_pvevac[t] = e_att
                        qpe.wait(e_att, e_attT_evac[t])
                        if h == 0:
                            qpe.wait(e_x1_prev)
                        e_trh = pestep(ten.transpose(
                            ps_pvs_f16[t][:, 896:1024],
                            attn_sb[:, h * 128:(h + 1) * 128], ident16[:, :]))
                        qve.wait(e_trh, e_xns)
                        e_evh = vstep(vec.tensor_copy(
                            out=attnT_sb[:, h * 128:(h + 1) * 128],
                            in_=ps_pvs_f16[t][:, 896:1024]))
                        e_attT_evac[t] = e_evh
                        qpe.wait(e_evh)
                        e_op = mm512(ps_out, attnT_sb[:, h * 128:(h + 1) * 128],
                                     wo16[:, h, :], 2048, h == 0, h == HEADS - 1)
                        e_op = pestep(e_op)

            qve.wait(e_op, e_xb)
            e_x1 = vstep(vec.scalar_tensor_tensor(out=x1_t[:, :], in0=ps_out[:, :],
                                                  scalar=1.0, in1=x_t[:, :],
                                                  op0=ALU.mult, op1=ALU.add))
            qgp.wait(e_x1)
            e_x1s = inc(d_x1s, gp.dma_start(out=X1_sc[j * 128:(j + 1) * 128, :],
                                            in_=x1_t[:, :]), 16)
            qve.wait(e_x1s)
            qve.wait(e_xtok)
            e_xn2, _, e_xn232 = layernorm_tile(x1_t[:, :], x_t[:, :], xn2_16[:, :],
                                               m_b, ss_b, xn2T32[:, :], e_x1,
                                               extra32=xn2_32[:, :])
            qgp.wait(e_xn2, e_xtok if j > 0 else None)
            e_xtok = inc(d_xn2tok, gp.dma_start(
                out=XN2_sc[j * 128:(j + 1) * 128, :], in_=xn2_16[:, :]), 16)
            qpe.wait(e_exp_half[0])
            e_xn2T = transpose16(lambda c: xn2_16[:, c * 128:(c + 1) * 128], KC,
                                 ps_ptr_b, xn2T16, e_xn2,
                                 extra_pe_wait=None, extra_ve_wait=e_xns,
                                 ident=identb)
            qgp.wait(e_xn2T)
            e_xns = inc(d_xns, gp.dma_start(
                out=XN2T_sc[j, :, :, :],
                in_=xn2T16[:, :].rearrange("p (c x) -> p c x", c=KC)), 16)
            last32 = None
            for r in range(0, KC, 4):
                qpe.wait(e_xn232, e_x1)
                if last32 is not None:
                    qpe.wait(last32)
                for u in range(4):
                    e_t32 = pestep(ten.transpose(ps_out[:, u * 128:(u + 1) * 128],
                                                 xn2_32[:, (r + u) * 128:(r + u + 1) * 128],
                                                 ident32[:, :]))
                qve.wait(e_t32)
                last32 = vstep(vec.tensor_copy(out=xn2T32[:, r * 128:(r + 4) * 128],
                                               in_=ps_out[:, 0:512]))
            e_x1_prev = last32
            qpe.wait(last32, e_g32, e_xn2T)
            for kc in range(KC):
                e_rm = ten.matmul(ps_qkT[:, 0:E], xn2T32[:, kc * 128:(kc + 1) * 128],
                                  gate32[:, kc, :], start=(kc == 0), stop=(kc == KC - 1))
            e_rm = pestep(e_rm)
            qve.wait(e_rm)
            e_rl = vstep(vec.tensor_copy(out=rlog[:, :], in_=ps_qkT[:, 0:E]))
            e_exp_half[0] = e_rl
            qve.wait(e_rl)
            e_tv = vstep(vec.max(out=topv[:, :], in_=rlog[:, :]))
            qve.wait(e_tv)
            e_ng = vstep(vec.tensor_scalar(out=m_b[:, :], in0=topv[:, 0:1], scalar1=-1.0,
                                           scalar2=None, op0=ALU.mult))
            qsc.wait(e_ng)
            e_px = scstep(sca.activation(out=rsml[:, :], in_=topv[:, :], func=AF.Exp,
                                         bias=m_b[:, 0:1], accum_out=ss_b[:, :]))
            qve.wait(e_px)
            e_rr = vstep(vec.reciprocal(out=ss_b[:, :], in_=ss_b[:, :]))
            qve.wait(e_rr)
            e_rw = vstep(vec.tensor_scalar(out=rw_sb[:, :], in0=rsml[:, :],
                                           scalar1=ss_b[:, 0:1], scalar2=None, op0=ALU.mult))
            e_uz = vstep(vec.memset(U_sb[:, :], 0.0))
            for jj in range(TOPK):
                qve.wait(e_uz)
                e_mk = vstep(vec.tensor_scalar(out=masks8[:, jj, :], in0=rlog[:, :],
                                               scalar1=topv[:, jj:jj + 1], scalar2=None,
                                               op0=ALU.is_equal))
                qve.wait(e_mk)
                e_uz = vstep(vec.tensor_tensor(out=U_sb[:, :], in0=U_sb[:, :],
                                               in1=masks8[:, jj, :], op=ALU.add))
            qve.wait(e_uz)
            e_ub = vstep(vec.tensor_copy(out=Ub_sb[:, :], in_=U_sb[:, :]))
            qpe.wait(e_ub, e_exp_half[1])
            e_trimm = pestep(ten.matmul(ps_qkT[:, 512:512 + E], tri_sb[:, :], Ub_sb[:, :],
                                        start=True, stop=True))
            qpe.wait(e_pvevac[0], e_cr0)
            e_cbm = pestep(ten.matmul(ps_pv0[:, 0:E], ones_row[:, :],
                                      carry_row[:, :], start=True, stop=True))
            qpe.wait(e_pvevac[1])
            e_csm = pestep(ten.matmul(ps_pv1[0:1, 0:E], onescol[:, :], Ub_sb[:, :],
                                      start=True, stop=True))
            qve.wait(e_trimm)
            e_sl = vstep(vec.tensor_copy(out=slot_sb[:, :], in_=ps_qkT[:, 512:512 + E]))
            e_exp_half[1] = e_sl
            qve.wait(e_cbm)
            e_cb = vstep(vec.tensor_copy(out=carb_sb[:, :], in_=ps_pv0[:, 0:E]))
            e_pvevac[0] = e_cb
            qve.wait(e_csm, e_cb)
            e_cr0 = vstep(vec.tensor_tensor(out=carry_row[:, :], in0=carry_row[:, :],
                                            in1=ps_pv1[0:1, 0:E], op=ALU.add))
            e_pvevac[1] = e_cr0
            qve.wait(e_sl)
            e_sl2 = vstep(vec.tensor_tensor(out=slot_sb[:, :], in0=slot_sb[:, :],
                                            in1=carb_sb[:, :], op=ALU.add))
            qve.wait(e_sl2)
            e_fl = vstep(vec.scalar_tensor_tensor(out=flat_sb[:, :], in0=iota64_sb[:, :],
                                                  scalar=float(CAP), in1=slot_sb[:, :],
                                                  op0=ALU.mult, op1=ALU.add))
            qve.wait(e_fl)
            e_ov = vstep(vec.tensor_scalar(out=slot_sb[:, :], in0=slot_sb[:, :],
                                           scalar1=float(CAP), scalar2=None, op0=ALU.is_ge))
            qve.wait(e_ov)
            e_fl2 = vstep(vec.scalar_tensor_tensor(out=flat_sb[:, :], in0=slot_sb[:, :],
                                                   scalar=1.0e9, in1=flat_sb[:, :],
                                                   op0=ALU.mult, op1=ALU.add))
            qve.wait(e_fl2)
            qve.wait(e_sct)
            e_tc = vstep(vec.tensor_scalar(out=tokc[:, :], in0=pidx_sb[:, :],
                                           scalar1=float(j * 128), scalar2=None, op0=ALU.add))
            for jj in range(TOPK):
                qve.wait(e_fl2)
                e_ei = vstep(vec.tensor_tensor(out=rtmp[:, :], in0=masks8[:, jj, :],
                                               in1=flat_sb[:, :], op=ALU.mult))
                qve.wait(e_ei)
                e_es = vstep(vec.reduce_sum(out=m_b[:, :], in_=rtmp[:, :], axis=AX.X))
                qve.wait(e_es)
                e_f3 = vstep(vec.tensor_copy(out=fidx_sb[:, j, jj:jj + 1], in_=m_b[:, :]))
                qve.wait(e_sct, e_tc)
                e_rc = vstep(vec.tensor_copy(out=trsc[:, 0:1], in_=tokc[:, :]))
                qve.wait(e_rc)
                e_rc = vstep(vec.tensor_copy(out=trsc[:, 1:2], in_=rw_sb[:, jj:jj + 1]))
                qgp.wait(e_f3, e_toki, e_rc, e_sct)
                e_sct = inc(d_sct, gp.indirect_dma_start(
                    out=TR_d, out_offset=IndirectOffsetOnAxis(fidx_sb[:, j, jj:jj + 1], 0),
                    in_=trsc[:, :], in_offset=None,
                    bounds_check=rb_tab, oob_is_err=False), 16)

        ctrow = xn2T32[0:1, 1800:1864]
        tmpr = xn2T32[0:1, 1864:1928]
        t128 = xn2T32[:, 0:128]
        ctcolf = xn2T32[:, 128:129]
        cumcol = xn2T32[:, 130:131]
        s2col = xn2T32[:, 132:133]
        cmp_f = xn2_32[:, 0:NCT]
        idxt_f = xn2_32[:, 256:256 + NCT]
        ekrow = xn2T32[0:1, 1024:1024 + NCT]
        cumk = xn2T32[0:1, 1216:1216 + NCT]
        bases = xn2T32[0:1, 1408:1408 + NCT]
        over_r = xn2T32[0:1, 1600:1600 + NCT]
        tts = xn2T32[0:1, 1798:1799]
        ctcol16 = sb("ctcol16", [128, 1], F16)
        cumcol16 = sb("cumcol16", [128, 1], F16)
        cmp16 = sb("cmp16", [128, NCT], F16)
        oh16 = sb("oh16", [128, NCT], F16)

        qve.wait(e_cr0)
        ev = vstep(vec.memset(ctrow, 0.0))
        for m in range(16):
            qve.wait(ev)
            ev = vstep(vec.tensor_scalar(out=tmpr, in0=carry_row[:, :],
                                         scalar1=float(128 * m), scalar2=None, op0=ALU.is_gt))
            qve.wait(ev)
            ev = vstep(vec.tensor_tensor(out=ctrow, in0=ctrow,
                                         in1=tmpr, op=ALU.add))
        qve.wait(ev)
        e_z = vstep(vec.memset(t128, 0.0))
        qve.wait(e_z)
        e_cp = vstep(vec.tensor_copy(out=t128[0:1, 0:E], in_=ctrow))
        qpe.wait(e_cp, e_x1_prev, e_exp_half[1], e_pvevac[0], e_pvevac[1])
        e_tr = pestep(ten.transpose(ps_out[:, 0:128], t128, ident32[:, :]))
        qve.wait(e_tr)
        e_cc = vstep(vec.tensor_copy(out=ctcolf, in_=ps_out[:, 0:1]))
        qve.wait(e_cc)
        e_c16 = vstep(vec.tensor_copy(out=ctcol16[:, :], in_=ctcolf))
        qpe.wait(e_c16)
        e_cm = pestep(ten.matmul(ps_pv0[0:E, 0:1], tri_sb[0:E, 0:E], ctcol16[0:E, 0:1],
                                 start=True, stop=True))
        qve.wait(e_cm)
        ev = vstep(vec.memset(cumcol, 0.0))
        qve.wait(ev)
        e_cu = vstep(vec.tensor_copy(out=xn2T32[0:E, 130:131], in_=ps_pv0[0:E, 0:1]))
        qve.wait(e_cu)
        e_cu16 = vstep(vec.tensor_copy(out=cumcol16[:, :], in_=cumcol))
        e_s2 = vstep(vec.tensor_tensor(out=s2col, in0=cumcol,
                                       in1=ctcolf, op=ALU.add))
        qve.wait(e_cu16, e_s2)
        ev = vstep(vec.tensor_scalar(out=cmp_f[0:E, :], in0=iotaB192[0:E, :],
                                     scalar1=xn2T32[0:E, 130:131], scalar2=None, op0=ALU.is_ge))
        qve.wait(ev)
        e_cmp16 = vstep(vec.tensor_copy(out=cmp16[0:E, :], in_=cmp_f[0:E, :]))
        qve.wait(e_cmp16)
        ev = vstep(vec.tensor_scalar(out=cmp_f[0:E, :], in0=iotaB192[0:E, :],
                                     scalar1=xn2T32[0:E, 132:133], scalar2=None, op0=ALU.is_ge))
        qve.wait(ev)
        e_cN16 = vstep(vec.tensor_copy(out=oh16[0:E, :], in_=cmp_f[0:E, :]))
        qve.wait(e_cN16)
        e_oh = vstep(vec.tensor_tensor(out=oh16[0:E, :], in0=cmp16[0:E, :],
                                       in1=oh16[0:E, :], op=ALU.subtract))
        qpe.wait(e_cmp16, e_oh)
        e_ek = pestep(ten.matmul(ps_out[0:1, 0:NCT], onescol[0:E, :], cmp16[0:E, :],
                                 start=True, stop=True))
        e_ck = pestep(ten.matmul(ps_out[0:1, 512:512 + NCT], cumcol16[0:E, :], oh16[0:E, :],
                                 start=True, stop=True))
        qve.wait(e_ek)
        e_ekr = vstep(vec.tensor_scalar(out=ekrow, in0=ps_out[0:1, 0:NCT],
                                        scalar1=1.0, scalar2=None, op0=ALU.subtract))
        qve.wait(e_ck)
        e_ckr = vstep(vec.tensor_copy(out=cumk, in_=ps_out[0:1, 512:512 + NCT]))
        e_tt = vstep(vec.reduce_sum(out=tts, in_=ctrow, axis=AX.X))
        qve.wait(e_tt)
        e_ov = vstep(vec.tensor_scalar(out=over_r, in0=iotaB192[0:1, 0:NCT],
                                       scalar1=tts, scalar2=None, op0=ALU.is_ge))
        qve.wait(e_ckr, e_ov, e_ekr)
        e_of = vstep(vec.tensor_tensor(out=cumk, in0=iotaB192[0:1, 0:NCT],
                                       in1=cumk, op=ALU.subtract))
        qve.wait(e_of)
        ev = vstep(vec.tensor_scalar(out=bases, in0=ekrow,
                                     scalar1=2048.0, scalar2=None, op0=ALU.mult))
        qve.wait(ev)
        ev = vstep(vec.scalar_tensor_tensor(out=bases, in0=cumk, scalar=128.0,
                                            in1=bases, op0=ALU.mult, op1=ALU.add))
        qve.wait(ev)
        e_ba = vstep(vec.scalar_tensor_tensor(out=bases, in0=over_r, scalar=1.0e9,
                                              in1=bases, op0=ALU.mult, op1=ALU.add))
        qpe.wait(e_ba)
        e_bc = pestep(ten.matmul(ps_out[:, 512:512 + NCT], ones_row[:, :], bases,
                                 start=True, stop=True))
        qve.wait(e_bc)
        ev = vstep(vec.tensor_tensor(out=idxt_f, in0=ps_out[:, 512:512 + NCT],
                                     in1=pidx_sb[:, 0:1].broadcast_to([128, NCT]), op=ALU.add))
        qve.wait(ev)
        e_ix1 = vstep(vec.tensor_copy(out=IDXT_u[:, :], in_=idxt_f))
        qve.wait(e_ix1)
        ev = vstep(vec.tensor_scalar(out=bases, in0=ekrow,
                                     scalar1=128.0, scalar2=None, op0=ALU.mult))
        qve.wait(ev)
        e_bd = vstep(vec.scalar_tensor_tensor(out=bases, in0=over_r, scalar=1.0e9,
                                              in1=bases, op0=ALU.mult, op1=ALU.add))
        qpe.wait(e_bd, e_ix1)
        e_bc2 = pestep(ten.matmul(ps_out[:, 512:512 + NCT], ones_row[:, :], bases,
                                  start=True, stop=True))
        qve.wait(e_bc2)
        ev = vstep(vec.tensor_tensor(out=idxt_f, in0=ps_out[:, 512:512 + NCT],
                                     in1=pidx_sb[:, 0:1].broadcast_to([128, NCT]), op=ALU.add))
        qve.wait(ev)
        e_ix2 = vstep(vec.tensor_copy(out=IDXD_u[:, :], in_=idxt_f))

        e_B_done = (e_x1s, e_xns, e_f3, e_sct, e_xtok, e_ix1, e_ix2)
        if "B1" in dbg:
            dNC = dbg_out("dbg_negc", [128, 1], F32)
            inc(d_dbg, sync.dma_start(out=dNC, in_=negc_col[:, :]), 16)
            dX1 = dbg_out("dbg_X1", [NT * 128, D], F32)
            dXN = dbg_out("dbg_XN2T", [NT, 128, KC, 128], F16)
            dT = dbg_out("dbg_TR", [E * CAP, 2], F32)
            dF = dbg_out("dbg_fidx", [128, NT, TOPK], U32)
            dI1 = dbg_out("dbg_IDXT", [128, NCT], U32)
            dI2 = dbg_out("dbg_IDXD", [128, NCT], U32)
            qsy.wait(*e_B_done)
            inc(d_dbg, sync.dma_start(out=dX1, in_=X1_sc), 16)
            inc(d_dbg, sync.dma_start(out=dXN, in_=XN2T_sc), 16)
            inc(d_dbg, sync.dma_start(out=dT, in_=TR_d), 16)
            inc(d_dbg, sync.dma_start(out=dI1, in_=IDXT_u[:, :]), 16)
            inc(d_dbg, sync.dma_start(out=dI2, in_=IDXD_u[:, :]), 16)
            ed = inc(d_dbg, sync.dma_start(out=dF, in_=fidx_sb[:, :, :]), 16)
            qsy.wait(ed)
        esB.close()

    e_C_done = None
    if "C" in phases:
        esC = ExitStack()
        sb = lambda n, s, d=F32: esC.enter_context(nc.sbuf_tensor(n, s, d))
        psm = lambda n, s, d=F32: esC.enter_context(nc.psum_tensor(n, s, d))
        wrot = Rot(mksem, [sb(f"wbuf{i}", [128, 24576], F16) for i in range(2)], "d_wb")
        xrotC = Rot(mksem, [sb(f"xg{i}", [128, D], F16) for i in range(2)], "d_xg")
        d_ys = mksem("d_ys")
        d_trb = mksem("d_trb")
        trall = sb("trall", [128, NCT, 2])
        tokall = sb("tokall", [128, NCT], U32)
        xgT = sb("xgT", [128, D], F16)
        sg_c = sb("sg_c", [128, I])
        intb = sb("intb", [128, I], F16)
        intT = sb("intT", [128, I], F16)
        ps_gu = psm("ps_gu", [128, I2])
        ps_trc = psm("ps_trc", [128, 512], F16)
        ps_y = psm("ps_y", [128, D])

        run_precast_jobs(len(precast_jobs))
        phase_barrier(*(list(e_B_done) + list(e_A_done)), (d_wcs, d_wcs.v) if d_wcs.v else None)

        for k in range(NCT):
            e_trb = inc(d_trb, gp.indirect_dma_start(
                out=trall[:, k, :], out_offset=None,
                in_=TR_d, in_offset=IndirectOffsetOnAxis(IDXT_u[:, k:k + 1], 0),
                bounds_check=rb_tab, oob_is_err=False), 16)
        qve.wait(e_trb)
        e_tokall = vstep(vec.tensor_copy(out=tokall[:, :], in_=trall[:, :, 0]))

        WALL_flat = WALL_sc.rearrange("e p n -> (e p) n")
        yrot = Rot(mksem, [sb(f"ytb{i}", [128, D], F16) for i in range(2)], "d_yt")

        e_w = [None] * NCT
        e_xg = [None] * NCT
        e_ye_k = [None] * NCT

        def issue_wg(k):
            ws = k % 2
            qgp.wait(wrot.free[ws])
            e_w[k] = inc(wrot.sems[ws], gp.indirect_dma_start(
                out=wrot.t[ws][:, :], out_offset=None,
                in_=WALL_flat, in_offset=IndirectOffsetOnAxis(IDXD_u[:, k:k + 1], 0),
                bounds_check=rb_we, oob_is_err=False), 16)

        def issue_xg(k):
            xs = k % 2
            qgp.wait(xrotC.free[xs], e_tokall)
            e_xg[k] = inc(xrotC.sems[xs], gp.indirect_dma_start(
                out=xrotC.t[xs][:, :], out_offset=None,
                in_=XN2_sc, in_offset=IndirectOffsetOnAxis(tokall[:, k:k + 1], 0),
                bounds_check=rb_tok, oob_is_err=False), 16)

        e_ys = None

        def issue_ys(k):
            nonlocal e_ys
            qgp.wait(e_ye_k[k])
            e_ys = inc(yrot.sems[k % 2], gp.indirect_dma_start(
                out=Y_sc, out_offset=IndirectOffsetOnAxis(IDXT_u[:, k:k + 1], 0),
                in_=yrot.t[k % 2][:, :], in_offset=None,
                bounds_check=rb_tab, oob_is_err=False), 16)
            yrot.free[k % 2] = e_ys

        e_sg_prev = None
        e_in_prev = None
        e_itv_prev = None
        e_tT_prev = None
        e_xgT_read = None
        issue_wg(0)
        issue_xg(0)
        for k in range(NCT):
            if k + 1 < NCT:
                issue_xg(k + 1)
                issue_wg(k + 1)
            if k >= 1:
                issue_ys(k - 1)
            ws = k % 2
            xs = k % 2
            qpe.wait(e_itv_prev)
            e_xgT = transpose16(lambda cc: xrotC.t[xs][:, cc * 128:(cc + 1) * 128], KC,
                                ps_trc, xgT, e_xg[k], extra_pe_wait=e_xgT_read,
                                ident=identb)
            xrotC.free[xs] = e_xgT
            qpe.wait(e_xgT, e_w[k], e_sg_prev, e_in_prev)
            for kc in range(KC):
                e_gu = mm512(ps_gu, xgT[:, kc * 128:(kc + 1) * 128],
                             wrot.t[ws][:, kc * 1024:(kc + 1) * 1024], 1024,
                             kc == 0, kc == KC - 1)
            e_gu = pestep(e_gu)
            e_xgT_read = e_gu
            qsc.wait(e_gu, e_in_prev)
            e_sg = scstep(sca.activation(out=sg_c[:, :], in_=ps_gu[:, 0:I], func=AF.Silu))
            e_sg_prev = e_sg
            qve.wait(e_sg, e_tT_prev)
            e_in = vstep(vec.scalar_tensor_tensor(out=intb[:, :], in0=ps_gu[:, I:I2],
                                                  scalar=trall[:, k, 1:2],
                                                  in1=sg_c[:, :],
                                                  op0=ALU.mult, op1=ALU.mult))
            e_in_prev = e_in
            qpe.wait(e_in, e_itv_prev)
            for u in range(4):
                e_t = pestep(ten.transpose(ps_trc[:, u * 128:(u + 1) * 128],
                                           intb[:, u * 128:(u + 1) * 128], identb[:, :]))
            e_tT_prev = e_t
            qve.wait(e_t)
            e_itv = vstep(vec.tensor_copy(out=intT[:, :], in_=ps_trc[:, :]))
            e_itv_prev = e_itv
            qpe.wait(e_itv, e_ye_k[k - 2] if k >= 2 else None)
            for ic in range(4):
                e_dn = mm512(ps_y, intT[:, ic * 128:(ic + 1) * 128],
                             wrot.t[ws][:, 16384 + ic * 2048:16384 + (ic + 1) * 2048],
                             2048, ic == 0, ic == 3)
            e_dn = pestep(e_dn)
            wrot.free[ws] = e_dn
            qsc.wait(e_dn, yrot.free[k % 2])
            e_ye_k[k] = scstep(sca.activation(out=yrot.t[k % 2][:, :], in_=ps_y[:, :],
                                              func=AF.Copy))
        issue_ys(NCT - 1)
        e_C_done = None
        e_C_dones = (yrot.free[0], yrot.free[1])
        esC.close()

    if "D" in phases:
        esD = ExitStack()
        sb = lambda n, s, d=F32: esD.enter_context(nc.sbuf_tensor(n, s, d))
        psm = lambda n, s, d=F32: esD.enter_context(nc.psum_tensor(n, s, d))
        shgu16 = sb("shgu16", [128, KC, 1024], F16)
        shd16 = sb("shd16", [128, 4, 2048], F16)
        g116 = sb("g116", [128, KC, 1], F16)
        phase_barrier(*(list(e_B_done) + list(e_A_done) +
                        (list(e_C_dones) if "C" in phases else [])))
        lrotD = Rot(mksem, [sb(f"landD{i}", [128, 4096], F32) for i in range(2)], "d_lD")
        d_xtd = mksem("d_xtd")
        d_x1d = mksem("d_x1d")
        d_gth = mksem("d_gth")
        stream_cast(shgu_in, D, 1024, shgu16, 0, lrotD)
        for cch in range(2):
            ls = lrotD.nxt()
            qsy.wait(lrotD.free[ls])
            e_l = inc(lrotD.sems[ls], sync.dma_start(
                out=lrotD.t[ls][:, :].rearrange("p (c n) -> p c n", c=2),
                in_=shd_in[cch * 256:(cch + 1) * 256, :].rearrange("(c p) n -> p c n", p=128)), 16)
            qve.wait(e_l)
            e_c = vstep(vec.tensor_copy(
                out=shd16[:, cch * 2:(cch + 1) * 2, :],
                in_=lrotD.t[ls][:, :].rearrange("p (c n) -> p c n", c=2)))
            lrotD.free[ls] = e_c
        ls = lrotD.nxt()
        qsy.wait(lrotD.free[ls])
        e_l = inc(lrotD.sems[ls], sync.dma_start(out=lrotD.t[ls][:, 0:16], in_=g1_in), 16)
        qve.wait(e_l)
        e_shw = vstep(vec.tensor_copy(out=g116[:, :, :],
                                      in_=lrotD.t[ls][:, 0:16].rearrange("p (c n) -> p c n", n=1)))

        xtd = sb("xtd", [128, D], F16)
        x1d = sb("x1d", [128, D])
        acc = sb("acc", [128, D])
        gbuf = sb("gbuf", [128, D])
        sgd = sb("sgd", [128, I])
        intd = sb("intd", [128, I], F16)
        intTd = sb("intTd", [128, I], F16)
        shg_d = sb("shg_d", [128, 1])
        outt = sb("outt", [128, D])
        ps_shgu = psm("ps_shgu", [128, 1024])
        ps_trd = psm("ps_trd", [128, 512], F16)
        ps_shd = psm("ps_shd", [128, 2048])
        ps_g1 = psm("ps_g1", [128, 1])

        if "C" in phases:
            qgp.wait(*e_C_dones)
            qsy.wait(*e_C_dones)
        if e_B_done:
            qgp.wait(e_B_done[2], e_B_done[3])

        prev_out_dma = None
        e_out_prev = None
        e_g_d = None
        e_sg_d = None
        e_int_d = None
        e_itv_d = None
        e_sig_d = None
        e_acc1 = None
        for j in range(n_nt):
            qsy.wait(e_g_d)
            e_xt = inc(d_xtd, sync.dma_start(
                out=xtd[:, :], in_=XN2T_sc[j, :, :, :].rearrange("p c x -> p (c x)")), 16)
            qsy.wait(e_acc1)
            e_x1l = inc(d_x1d, sync.dma_start(out=x1d[:, :],
                                              in_=X1_sc[j * 128:(j + 1) * 128, :]), 16)
            qpe.wait(e_xt, e_shw, e_sg_d, e_int_d)
            for kc in range(KC):
                e_g = mm512(ps_shgu, xtd[:, kc * 128:(kc + 1) * 128],
                            shgu16[:, kc, :], 1024, kc == 0, kc == KC - 1)
            e_g = pestep(e_g)
            qpe.wait(e_sig_d)
            for kc in range(KC):
                e_g1m = ten.matmul(ps_g1[:, :], xtd[:, kc * 128:(kc + 1) * 128],
                                   g116[:, kc, :], start=(kc == 0), stop=(kc == KC - 1))
            e_g1m = pestep(e_g1m)
            e_g_d = e_g1m
            qsc.wait(e_g, e_int_d)
            e_sg = scstep(sca.activation(out=sgd[:, :], in_=ps_shgu[:, 0:I], func=AF.Silu))
            e_sg_d = e_sg
            qve.wait(e_sg, e_itv_d)
            e_int = vstep(vec.tensor_tensor(out=intd[:, :], in0=ps_shgu[:, I:I2],
                                            in1=sgd[:, :], op=ALU.mult))
            e_int_d = e_int
            qpe.wait(e_int, e_itv_d)
            for u in range(4):
                e_t = pestep(ten.transpose(ps_trd[:, u * 128:(u + 1) * 128],
                                           intd[:, u * 128:(u + 1) * 128], identb[:, :]))
            qve.wait(e_t)
            e_itv = vstep(vec.tensor_copy(out=intTd[:, :], in_=ps_trd[:, :]))
            e_itv_d = e_itv
            qpe.wait(e_itv, e_out_prev)
            for ic in range(4):
                e_dn = mm512(ps_shd, intTd[:, ic * 128:(ic + 1) * 128],
                             shd16[:, ic, :], 2048, ic == 0, ic == 3)
            e_dn = pestep(e_dn)
            qsc.wait(e_g1m)
            e_sig = scstep(sca.activation(out=shg_d[:, :], in_=ps_g1[:, :], func=AF.Sigmoid))
            e_sig_d = e_sig

            qve.wait(e_x1l)
            e_acc = vstep(vec.tensor_copy(out=acc[:, :], in_=x1d[:, :]))
            e_acc1 = e_acc
            for jj in range(TOPK):
                qgp.wait(e_acc)
                e_gth = inc(d_gth, gp.indirect_dma_start(
                    out=gbuf[:, :], out_offset=None,
                    in_=Y_sc, in_offset=IndirectOffsetOnAxis(fidx_sb[:, j, jj:jj + 1], 0),
                    bounds_check=rb_tab, oob_is_err=False), 16)
                qve.wait(e_gth, e_acc)
                e_acc = vstep(vec.tensor_tensor(out=acc[:, :], in0=acc[:, :],
                                                in1=gbuf[:, :], op=ALU.add))
            qve.wait(e_acc, e_dn, e_sig, prev_out_dma)
            e_out = vstep(vec.scalar_tensor_tensor(out=outt[:, :], in0=ps_shd[:, :],
                                                   scalar=shg_d[:, 0:1], in1=acc[:, :],
                                                   op0=ALU.mult, op1=ALU.add))
            e_out_prev = e_out
            qsy.wait(e_out)
            prev_out_dma = inc(s_out, sync.dma_start(out=y_out[j * 128:(j + 1) * 128, :],
                                                     in_=outt[:, :]), 16)
        qsy.wait(prev_out_dma)
        esD.close()

    for s in all_sems:
        if s.v:
            qsy.wait((s, s.v))
            qgp.wait((s, s.v))

    return nc, dbg_outs



def _prep_in_maps(inputs):
    x = np.asarray(inputs["L0_attn_x"], dtype=np.float32)
    wq = np.ascontiguousarray(inputs["L0_attn_wq"], dtype=np.float32)
    wkv = np.ascontiguousarray(np.concatenate(
        [np.asarray(inputs["L0_attn_wk"], np.float32),
         np.asarray(inputs["L0_attn_wv"], np.float32)], axis=1))
    wo = np.ascontiguousarray(inputs["L0_attn_wo"], dtype=np.float32)
    bq = np.asarray(inputs["L0_attn_bq"], dtype=np.float32).reshape(1, D)
    bkv = np.concatenate([np.asarray(inputs["L0_attn_bk"], np.float32),
                          np.asarray(inputs["L0_attn_bv"], np.float32)]).reshape(1, 1024)
    bo = np.asarray(inputs["L0_attn_bo"], dtype=np.float32).reshape(1, D)
    qns = np.asarray(inputs["L0_attn_q_norm_scale"], np.float32).reshape(1, 128)
    kns = np.asarray(inputs["L0_attn_k_norm_scale"], np.float32).reshape(1, 128)
    gate_T = np.ascontiguousarray(np.asarray(inputs["L0_ffn_gate_w"], np.float32).T)
    guw_T = np.ascontiguousarray(np.asarray(inputs["L0_ffn_gu_w"], np.float32).transpose(0, 2, 1))
    dpw_T = np.ascontiguousarray(np.asarray(inputs["L0_ffn_dp_w"], np.float32).transpose(0, 2, 1))
    shgu_T = np.ascontiguousarray(
        np.concatenate([np.asarray(inputs["L0_ffn_sh_gate_w"], np.float32),
                        np.asarray(inputs["L0_ffn_sh_up_w"], np.float32)], axis=0).T)
    shd_T = np.ascontiguousarray(np.asarray(inputs["L0_ffn_sh_down_w"], np.float32).T)
    g1_T = np.ascontiguousarray(
        np.asarray(inputs["L0_ffn_sh_gate1_w"], np.float32).reshape(KC, 128).T)

    inv_freq = 1.0 / (10000.0 ** (np.arange(0, HD, 2, dtype=np.float32) / HD))
    pos = np.arange(S, dtype=np.float32)
    freqs = np.outer(pos, inv_freq)
    cos_t = np.ascontiguousarray(np.cos(freqs), dtype=np.float32)
    sin_t = np.ascontiguousarray(np.sin(freqs), dtype=np.float32)

    in_maps = []
    for c in range(8):
        b, r = c // 4, c % 4
        rows = (np.arange(NT)[:, None] * 512 + r * 128 + np.arange(128)[None, :]).ravel()
        maskT = np.zeros((128, 4, 128), dtype=np.float16)
        for u in range(4):
            maskT[:, u, :] = ((u * 128 + np.arange(128)[:, None])
                              <= (r * 128 + np.arange(128)[None, :]))
        mask = np.ascontiguousarray(maskT.reshape(128, 512))
        in_maps.append({
            "x_pre": np.ascontiguousarray(x[b]),
            "x_own": np.ascontiguousarray(x[b][rows]),
            "cos_pre": cos_t, "sin_pre": sin_t,
            "cos_own": np.ascontiguousarray(cos_t[rows]),
            "sin_own": np.ascontiguousarray(sin_t[rows]),
            "mask_in": mask,
            "wq_in": wq, "wkv_in": wkv, "wo_in": wo,
            "bq_in": bq, "bkv_in": bkv, "bo_in": bo,
            "qns_in": qns, "kns_in": kns,
            "gate_in": gate_T, "guw_in": guw_T, "dpw_in": dpw_T,
            "shgu_in": shgu_T, "shd_in": shd_T, "g1_in": g1_T,
        })
    return in_maps


def _assemble(results):
    out = np.empty((B, S, D), dtype=np.float32)
    for c in range(8):
        b, r = c // 4, c % 4
        rows = (np.arange(NT)[:, None] * 512 + r * 128 + np.arange(128)[None, :]).ravel()
        out[b][rows] = results[c]["y"]
    return out


def kernel(**inputs):
    nc, _ = build()
    in_maps = _prep_in_maps(inputs)
    res = run_bass_kernel_spmd(nc, in_maps, core_ids=list(range(8)))
    return _assemble(res.results)


if __name__ == "__main__":
    import time
    t0 = time.time()
    nc, _ = build()
    print("build ok", time.time() - t0, "s; instructions:", nc.n_instructions())
